# version 01
# baseline (speedup 1.0000x reference)
import numpy as np
import ml_dtypes
import concourse.bass as bass
import concourse.mybir as mybir
from concourse.bass_utils import run_bass_kernel_spmd

F32 = mybir.dt.float32
BF16 = mybir.dt.bfloat16
I32 = mybir.dt.int32
AF = mybir.ActivationFunctionType
ALU = mybir.AluOpType
AX = mybir.AxisListType
bfd = ml_dtypes.bfloat16

SHIFTS = [0, 1, -1, 3, -3, 7, -7, 20, -20, 53, -53, 141, -141, 380, -380, 1024]
CH = [0, 1, 2, 4, 8]
DB = [-8, -3, -2, -1, 0, 1, 2, 3]
H = 16
DM = 2048
HID = 5632
NHC = 44
TOK = 2048
EXT = 3456
ECH = 27
LCH = 16
OFF = 1024
EPS_RMS = 1e-6
EPS_LN = 1e-5
SCALE = 128.0 ** -0.5
NCORE = 8
DEBUG = False

EGRP = [(0, 512), (512, 512), (1024, 512), (1536, 512), (2048, 512), (2560, 512), (3072, 384)]
QGRP = [(0, 512), (512, 512), (1024, 512), (1536, 512)]


def _mask01():
    m = np.full((128, 8 * 128), -1e30, dtype=np.float32)
    ss = set(SHIFTS)
    for p in range(128):
        for bi, db in enumerate(DB):
            for q in range(128):
                if (p - q - db * 128) in ss:
                    m[p, bi * 128 + q] = 0.0
    assert ((m == 0).sum(1) == 16).all()
    return m.astype(bfd)


def _pmats():
    P = np.zeros((128, 5 * 128), dtype=np.float32)
    for ci, c in enumerate(CH):
        P[(np.arange(128) - c) % 128, ci * 128 + np.arange(128)] = 1.0
    return P.astype(bfd)


class Ct:
    def __init__(self, h):
        self.h = h
        self.n = 0


def build_nc():
    nc = bass.Bass("TRN2", target_bir_lowering=False)

    xe = nc.dram_tensor("xe", [EXT, DM], F32, kind="ExternalInput").ap()
    wqT = nc.dram_tensor("wqT", [DM, DM], F32, kind="ExternalInput").ap()
    wkT = nc.dram_tensor("wkT", [DM, DM], F32, kind="ExternalInput").ap()
    wvT = nc.dram_tensor("wvT", [DM, DM], F32, kind="ExternalInput").ap()
    woT = nc.dram_tensor("woT", [DM, DM], F32, kind="ExternalInput").ap()
    gTb = nc.dram_tensor("gTb", [NHC, DM, 128], I32, kind="ExternalInput").ap()
    uTb = nc.dram_tensor("uTb", [NHC, DM, 128], I32, kind="ExternalInput").ap()
    dT = nc.dram_tensor("dT", [HID, DM], I32, kind="ExternalInput").ap()
    r_bq = nc.dram_tensor("r_bq", [1, DM], F32, kind="ExternalInput").ap()
    r_bk = nc.dram_tensor("r_bk", [1, DM], F32, kind="ExternalInput").ap()
    r_bv = nc.dram_tensor("r_bv", [1, DM], F32, kind="ExternalInput").ap()
    r_bo = nc.dram_tensor("r_bo", [1, DM], F32, kind="ExternalInput").ap()
    r_gq = nc.dram_tensor("r_gq", [1, DM], F32, kind="ExternalInput").ap()
    r_gk = nc.dram_tensor("r_gk", [1, DM], F32, kind="ExternalInput").ap()
    r_sg = nc.dram_tensor("r_sg", [1, HID], F32, kind="ExternalInput").ap()
    r_su = nc.dram_tensor("r_su", [1, HID], F32, kind="ExternalInput").ap()
    r_sd = nc.dram_tensor("r_sd", [1, DM], F32, kind="ExternalInput").ap()
    wmix = nc.dram_tensor("wmix", [1, 5], F32, kind="ExternalInput").ap()
    cI = nc.dram_tensor("cI", [128, 128], BF16, kind="ExternalInput").ap()
    cMask = nc.dram_tensor("cMask", [128, 1024], BF16, kind="ExternalInput").ap()
    cPm = nc.dram_tensor("cPm", [128, 640], BF16, kind="ExternalInput").ap()
    cOnes = nc.dram_tensor("cOnes", [128, 512], BF16, kind="ExternalInput").ap()
    cOnesF = nc.dram_tensor("cOnesF", [1, 128], F32, kind="ExternalInput").ap()

    sk = "ExternalOutput" if DEBUG else "Internal"
    xfmD = nc.dram_tensor("xfmD", [DM, EXT], BF16, kind=sk).ap()
    kqD = nc.dram_tensor("kqD", [H, 128, TOK], BF16, kind=sk).ap()
    kkD = nc.dram_tensor("kkD", [H, 128, EXT], BF16, kind=sk).ap()
    vtD = nc.dram_tensor("vtD", [EXT, DM], BF16, kind=sk).ap()
    aoD = nc.dram_tensor("aoD", [H, 128, TOK], BF16, kind=sk).ap()
    x1D = nc.dram_tensor("x1D", [TOK, DM], F32, kind=sk).ap()
    hD = nc.dram_tensor("hD", [NHC, 128, TOK], BF16, kind=sk).ap()
    fD = nc.dram_tensor("fD", [TOK, DM], F32, kind=sk).ap()
    rqDbg = nc.dram_tensor("rqDbg", [1, TOK], F32, kind="ExternalOutput").ap() if DEBUG else None
    rqSDbg = nc.dram_tensor("rqSDbg", [128, 16], F32, kind="ExternalOutput").ap() if DEBUG else None
    y = nc.dram_tensor("y", [TOK, DM], F32, kind="ExternalOutput").ap()

    import contextlib
    st = contextlib.ExitStack()

    def sb(name, shp, dt):
        return st.enter_context(nc.sbuf_tensor(name, shp, dt))

    def psum(name, shp, dt):
        return st.enter_context(nc.psum_tensor(name, shp, dt))

    def sem(name):
        return Ct(st.enter_context(nc.semaphore(name)))

    with st:
        A = sb("A", [128, 49152], BF16)
        B = sb("B", [128, 16384], BF16)
        Fb = sb("Fb", [128, 6144], F32)
        smI = sb("smI", [128, 128], BF16)
        smM = sb("smM", [128, 1024], BF16)
        smP = sb("smP", [128, 640], BF16)
        smMq = sb("smMq", [128, 128], BF16)
        smO = sb("smO", [128, 512], BF16)
        smOF = sb("smOF", [1, 128], F32)
        gqc = sb("gqc", [128, 16], F32)
        gkc = sb("gkc", [128, 16], F32)
        sgc = sb("sgc", [128, 44], F32)
        suc = sb("suc", [128, 44], F32)
        sdb = sb("sdb", [128, 2048], F32)
        wmb = sb("wmb", [128, 8], F32)
        rbq = sb("rbq", [1, 2048], BF16)
        rbk = sb("rbk", [1, 2048], BF16)
        rbv = sb("rbv", [1, 2048], BF16)
        rbo = sb("rbo", [1, 2048], BF16)
        rsd = sb("rsd", [1, 2048], F32)
        rqA = sb("rqA", [1, 2048], F32)
        rqT = sb("rqT", [128, 16], F32)
        rqS = sb("rqS", [128, 16], F32)
        rowF = sb("rowF", [1, 1024], F32)
        stats = sb("stats", [128, 8], F32)
        spcV = sb("spcV", [128, 1], F32)
        spcS = sb("spcS", [128, 1], F32)

        PS0 = psum("PS0", [128, 2048], F32)
        PS1 = psum("PS1", [128, 2048], F32)

        bar = sem("bar")
        dG = sem("dG")
        dS = sem("dS")
        dC = sem("dC")
        tA = sem("tA"); tB = sem("tB"); tC = sem("tC"); tD = sem("tD")
        vA = sem("vA"); vB = sem("vB"); vC = sem("vC")
        sA = sem("sA"); sB = sem("sB"); sC = sem("sC")
        gA = sem("gA")

        blk = st.enter_context(nc.Block())
        ENGS = ["tensor", "vector", "scalar", "gpsimd", "sync"]
        barn = [0]

        def barrier():
            barn[0] += 1
            tgt = 5 * barn[0]
            for e in ENGS:
                def _f(en, tgt=tgt):
                    en.sem_inc(bar.h, 1)
                    en.wait_ge(bar.h, tgt)
                getattr(blk, e)(_f)

        def sect(e):
            return getattr(blk, e)

        def p0():
            tB0_p0 = tB.n
            dG0_rows = [0]
            rowP0 = Fb[0:1, 0:5632]

            @sect("gpsimd")
            def _(g):
                g.dma_start(out=smOF[:], in_=cOnesF).then_inc(dG.h, 16); dG.n += 16
                for out, src in ((smI, cI), (smM, cMask), (smP, cPm), (smO, cOnes)):
                    g.dma_start(out=out[:], in_=src).then_inc(dG.h, 16); dG.n += 16
                g.dma_start(out=wmb[0:1, 0:5], in_=wmix).then_inc(dG.h, 16); dG.n += 16
                dG0_rows[0] = dG.n
                for ri, (src, n) in enumerate(((r_gq, 16), (r_gk, 16), (r_sg, 44), (r_su, 44))):
                    g.dma_start(out=rowP0[0:1, 0:n * 128], in_=src).then_inc(dG.h, 16); dG.n += 16
                    g.dma_start(out=rowP0[0:1, 0:n * 128], in_=src).then_inc(dG.h, 16); dG.n += 16
                    g.wait_ge(gA.h, gA.n + 1); gA.n += 1
                for out, src in ((rbq, r_bq), (rbk, r_bk), (rbv, r_bv), (rbo, r_bo)):
                    g.dma_start(out=out[:], in_=src).then_inc(dG.h, 16); dG.n += 16
                g.dma_start(out=rsd[:], in_=r_sd).then_inc(dG.h, 16); dG.n += 16

            @sect("tensor")
            def _(t):
                nd = dG0_rows[0]
                vA0_p0 = vA.n
                for ri, (dst, n) in enumerate(((gqc, 16), (gkc, 16), (sgc, 44), (suc, 44))):
                    t.wait_ge(dG.h, nd + 32 * (ri + 1))
                    if ri >= 1:
                        t.wait_ge(vA.h, vA0_p0 + ri)
                    for c in range(n):
                        t.matmul(PS0[:, c:c + 1], rowP0[0:1, c * 128:(c + 1) * 128],
                                 smOF[0:1, 0:1], start=True, stop=True)
                    t.drain().then_inc(tB.h, 1); tB.n += 1
                t.wait_ge(dG.h, dG.n)
                t.wait_ge(vA.h, vA0_p0 + 4)
                t.matmul(PS0[:, 0:5], smOF[0:1, 0:128], wmb[0:1, 0:5], start=True, stop=True)
                for j in range(4):
                    t.matmul(PS1[:, j * 512:(j + 1) * 512], smOF[0:1, 0:128],
                             rsd[0:1, j * 512:(j + 1) * 512], start=True, stop=True)
                t.drain().then_inc(tA.h, 1); tA.n += 1

            @sect("scalar")
            def _(sc):
                sc.wait_ge(tA.h, tA.n)
                sc.copy(wmb[:, 0:5], PS0[:, 0:5])
                sc.copy(spcS[:], spcS[:]).then_inc(sA.h, 1); sA.n += 1

            @sect("vector")
            def _(v):
                for ri, (dst, n) in enumerate(((gqc, 16), (gkc, 16), (sgc, 44), (suc, 44))):
                    v.wait_ge(tB.h, tB0_p0 + ri + 1)
                    v.tensor_copy(dst[:, 0:n], PS0[:, 0:n])
                    v.tensor_copy(spcV[:], spcV[:]).then_inc(vA.h, 1); vA.n += 1
                    v.tensor_copy(spcV[:], spcV[:]).then_inc(gA.h, 1)
                v.wait_ge(tA.h, tA.n)
                v.tensor_copy(sdb[:, :], PS1[:, 0:2048])
                v.wait_ge(sA.h, sA.n)
                v.tensor_scalar(smMq[:], smP[:, 0:128], wmb[:, 0:1], None, op0=ALU.mult)
                for ci in range(1, 5):
                    v.tensor_copy(spcV[:], spcV[:])
                    v.scalar_tensor_tensor(smMq[:], smP[:, ci * 128:(ci + 1) * 128],
                                           wmb[:, ci:ci + 1], smMq[:],
                                           op0=ALU.mult, op1=ALU.add)
                v.tensor_copy(spcV[:], spcV[:])
            barrier()

        def p1x():
            d0, t0, v0, s0 = dG.n, tA.n, vA.n, dS.n
            xtm = (A[:, 32768:34816], A[:, 34816:36864], A[:, 36864:38912])
            ev = (B[:, 0:2048], B[:, 2048:4096])

            @sect("gpsimd")
            def _(g):
                for ec in range(ECH):
                    if ec >= 3:
                        g.wait_ge(tA.h, t0 + ec - 2)
                    g.dma_start(out=xtm[ec % 3][:], in_=xe[ec * 128:(ec + 1) * 128, :]).then_inc(dG.h, 16)
                    dG.n += 16
                g.wait_ge(tA.h, t0 + ECH - 2)
                g.dma_start(out=A[:, 38912:39424], in_=xe[0:128, 0:512]).then_inc(dG.h, 16)
                dG.n += 16

            @sect("tensor")
            def _(t):
                for ec in range(ECH):
                    t.wait_ge(dG.h, d0 + 16 * min(ec + 2, ECH + 1))
                    if ec >= 1:
                        t.wait_ge(vA.h, v0 + ec)
                    for i in range(16):
                        t.matmul(PS0[:, i * 128:(i + 1) * 128], xtm[ec % 3][:, i * 128:(i + 1) * 128],
                                 smI[:], start=True, stop=True)
                    t.drain().then_inc(tA.h, 1); tA.n += 1

            @sect("vector")
            def _(v):
                for ec in range(ECH):
                    v.wait_ge(tA.h, t0 + ec + 1)
                    if ec >= 2:
                        v.wait_ge(dS.h, s0 + 256 * (ec - 1))
                    for i in range(16):
                        v.tensor_copy(ev[ec % 2][:, i * 128:(i + 1) * 128], PS0[:, i * 128:(i + 1) * 128])
                    v.drain().then_inc(vA.h, 1); vA.n += 1

            @sect("sync")
            def _(s):
                for ec in range(ECH):
                    s.wait_ge(vA.h, v0 + ec + 1)
                    for i in range(16):
                        s.dma_start(out=xfmD[i * 128:(i + 1) * 128, ec * 128:(ec + 1) * 128],
                                    in_=ev[ec % 2][:, i * 128:(i + 1) * 128]).then_inc(dS.h, 16)
                        dS.n += 16
                s.wait_ge(dS.h, dS.n)
            barrier()

        def proj(kind):
            grps = QGRP if kind == "q" else EGRP
            NG = len(grps)
            wsrc = {"q": wqT, "k": wkT, "v": wvT}[kind]
            brow = {"q": rbq, "k": rbk, "v": rbv}[kind]
            outD = {"q": kqD, "k": kkD, "v": vtD}[kind]
            d0, tA0, tB0, tC0, tD0 = dG.n, tA.n, tB.n, tC.n, tD.n
            vB0, vC0 = vB.n, vC.n
            sA0, sB0, sC0 = sA.n, sB.n, sC.n
            gA0 = gA.n
            wA = A[:, 0:32768]
            xf = (A[:, 32768:40960], A[:, 40960:49152])
            qp = B[:, 0:8192]
            sq = (B[:, 8192:8704], B[:, 8704:9216])
            ev = (B[:, 9216:10240], B[:, 10240:11264])
            vev = (B[:, 8704:10752], B[:, 10752:12800])
            rkb = Fb[:, 0:512]

            nst = [(16 if kind != "v" else W // 128) for (c0, W) in grps]
            LOADS_DONE = {}
            STORE_DONE = {}
            cnt = dS.n
            stu = 0
            for gi in range(NG):
                cnt += 16 * 16
                LOADS_DONE[gi] = cnt
                if gi >= 1:
                    for _ in range(nst[gi - 1]):
                        cnt += 16
                        stu += 1
                        STORE_DONE[stu - 1] = cnt
            for _ in range(nst[NG - 1]):
                cnt += 16
                stu += 1
                STORE_DONE[stu - 1] = cnt

            @sect("gpsimd")
            def _(g):
                for i in range(16):
                    g.dma_start(out=wA[:, i * 2048:(i + 1) * 2048],
                                in_=wsrc[i * 128:(i + 1) * 128, :]).then_inc(dG.h, 16)
                    dG.n += 16

            @sect("sync")
            def _(s):
                stu2 = 0
                for gi, (c0, W) in enumerate(grps):
                    if gi >= 2:
                        s.wait_ge(tA.h, tA0 + gi - 1)
                    base = (OFF + c0) if kind == "q" else c0
                    for i in range(16):
                        s.dma_start(out=xf[gi % 2][:, i * 512:i * 512 + W],
                                    in_=xfmD[i * 128:(i + 1) * 128, base:base + W]).then_inc(dS.h, 16)
                        dS.n += 16
                    if gi >= 1:
                        c0p, Wp = grps[gi - 1]
                        for u in range(nst[gi - 1]):
                            s.wait_ge(vB.h, vB0 + stu2 + 1)
                            if kind == "q":
                                s.dma_start(out=kqD[u, :, c0p:c0p + Wp], in_=ev[u % 2][:, 0:Wp]).then_inc(dS.h, 16)
                            elif kind == "k":
                                s.dma_start(out=kkD[u, :, c0p:c0p + Wp], in_=ev[u % 2][:, 0:Wp]).then_inc(dS.h, 16)
                            else:
                                mc = c0p + u * 128
                                s.dma_start(out=vtD[mc:mc + 128, :], in_=vev[u % 2][:, 0:2048]).then_inc(dS.h, 16)
                            dS.n += 16
                            stu2 += 1
                c0p, Wp = grps[NG - 1]
                for u in range(nst[NG - 1]):
                    s.wait_ge(vB.h, vB0 + stu2 + 1)
                    if kind == "q":
                        s.dma_start(out=kqD[u, :, c0p:c0p + Wp], in_=ev[u % 2][:, 0:Wp]).then_inc(dS.h, 16)
                    elif kind == "k":
                        s.dma_start(out=kkD[u, :, c0p:c0p + Wp], in_=ev[u % 2][:, 0:Wp]).then_inc(dS.h, 16)
                    else:
                        mc = c0p + u * 128
                        s.dma_start(out=vtD[mc:mc + 128, :], in_=vev[u % 2][:, 0:2048]).then_inc(dS.h, 16)
                    dS.n += 16
                    stu2 += 1
                s.wait_ge(dS.h, dS.n)

            @sect("tensor")
            def _(t):
                midx = 0
                for gi, (c0, W) in enumerate(grps):
                    t.wait_ge(dS.h, LOADS_DONE[gi])
                    if gi == 0:
                        t.wait_ge(dG.h, d0 + 256)
                    if kind in ("q", "k"):
                        if kind == "k" and gi >= 1:
                            t.wait_ge(vB.h, vB0 + 16 * gi)
                        def sumsq_mm(o):
                            t.wait_ge(sB.h, sB0 + gi * 16 + o + 1)
                            t.matmul(PS1[0:1, 1024:1024 + W], smO[:, 0:1], sq[o % 2][:, 0:W],
                                     start=(o == 0), stop=(o == 15)).then_inc(tC.h, 1)
                            tC.n += 1
                        for o in range(16):
                            bank = PS1[:, (o % 2) * 512:(o % 2) * 512 + W]
                            j = gi * 16 + o
                            if j >= 2:
                                t.wait_ge(sA.h, sA0 + j - 1)
                            for i in range(16):
                                t.matmul(bank, wA[:, i * 2048 + o * 128:i * 2048 + (o + 1) * 128],
                                         xf[gi % 2][:, i * 512:i * 512 + W],
                                         start=(i == 0), stop=False)
                            t.matmul(bank, brow[0:1, o * 128:(o + 1) * 128], smO[0:1, 0:W],
                                     start=False, stop=True).then_inc(tB.h, 1)
                            tB.n += 1
                            if o >= 1:
                                sumsq_mm(o - 1)
                        sumsq_mm(15)
                        t.drain().then_inc(tA.h, 1); tA.n += 1
                        if kind == "k":
                            t.wait_ge(sC.h, sC0 + gi + 1)
                            t.matmul(PS0[:, 0:W], smOF[0:1, 0:128], rowF[0:1, 0:W],
                                     start=True, stop=True).then_inc(tD.h, 1)
                            tD.n += 1
                        if kind == "q":
                            for h in range(H):
                                j = gi * 16 + h
                                t.wait_ge(sA.h, sA0 + j + 1)
                                if j >= 1:
                                    t.wait_ge(vB.h, vB0 + j)
                                t.matmul(PS1[:, 1536:1536 + W], smMq[:],
                                         qp[:, h * 512:h * 512 + W],
                                         start=True, stop=True).then_inc(tD.h, 1)
                                tD.n += 1
                    else:
                        nm = W // 128
                        for m in range(nm):
                            if midx >= 1:
                                t.wait_ge(vB.h, vB0 + midx)
                            for og in range(4):
                                seg = PS1[:, og * 512:(og + 1) * 512]
                                for i in range(16):
                                    t.matmul(seg, xf[gi % 2][:, i * 512 + m * 128:i * 512 + (m + 1) * 128],
                                             wA[:, i * 2048 + og * 512:i * 2048 + (og + 1) * 512],
                                             start=(i == 0), stop=False)
                                t.matmul(seg, smO[0:1, 0:128], brow[0:1, og * 512:(og + 1) * 512],
                                         start=False, stop=True)
                            t.drain().then_inc(tB.h, 1); tB.n += 1
                            midx += 1
                        t.drain().then_inc(tA.h, 1); tA.n += 1

            @sect("scalar")
            def _(s):
                if kind == "v":
                    return
                gcol = gqc if kind == "q" else gkc
                for gi, (c0, W) in enumerate(grps):
                    for o in range(16):
                        s.wait_ge(tB.h, tB0 + gi * 16 + o + 1)
                        if gi * 16 + o >= 2:
                            s.wait_ge(tC.h, tC0 + gi * 16 + o - 1)
                        bank = PS1[:, (o % 2) * 512:(o % 2) * 512 + W]
                        s.activation(sq[o % 2][:, 0:W], bank, AF.Square).then_inc(sB.h, 1); sB.n += 1
                        s.activation(qp[:, o * 512:o * 512 + W], bank, AF.Copy,
                                     scale=gcol[:, o:o + 1]).then_inc(sA.h, 1); sA.n += 1
                    s.wait_ge(vC.h, vC0 + gi + 1)
                    dst = rqA[0:1, c0:c0 + W] if kind == "q" else rowF[0:1, 0:W]
                    s.activation(dst, rowF[0:1, 512:512 + W], AF.Sqrt)
                    s.copy(spcS[:], spcS[:]).then_inc(sC.h, 1); sC.n += 1

            @sect("vector")
            def _(v):
                stu2 = 0
                for gi, (c0, W) in enumerate(grps):
                    if kind in ("q", "k"):
                        v.wait_ge(tC.h, tC0 + 16 * (gi + 1))
                        if gi >= 1:
                            v.wait_ge(sC.h, sC0 + gi)
                            if kind == "k":
                                v.wait_ge(tD.h, tD0 + gi)
                        v.tensor_scalar(rowF[0:1, 0:W],
                                        PS1[0:1, 1024:1024 + W],
                                        1.0 / 2048.0, EPS_RMS, op0=ALU.mult, op1=ALU.add)
                        v.tensor_copy(spcV[:], spcV[:])
                        v.reciprocal(rowF[0:1, 512:512 + W], rowF[0:1, 0:W])
                        v.tensor_copy(spcV[:], spcV[:]).then_inc(vC.h, 1)
                        vC.n += 1
                    if kind == "q":
                        for h in range(H):
                            j = gi * 16 + h
                            v.wait_ge(tD.h, tD0 + j + 1)
                            if stu2 >= 2:
                                v.wait_ge(dS.h, STORE_DONE[stu2 - 2])
                            v.tensor_copy(ev[h % 2][:, 0:W], PS1[:, 1536:1536 + W]).then_inc(vB.h, 1)
                            vB.n += 1
                            stu2 += 1
                    elif kind == "k":
                        v.wait_ge(tD.h, tD0 + gi + 1)
                        v.tensor_copy(rkb[:, 0:W], PS0[:, 0:W])
                        v.tensor_copy(spcV[:], spcV[:])
                        for o in range(16):
                            j = gi * 16 + o
                            if stu2 >= 2:
                                v.wait_ge(dS.h, STORE_DONE[stu2 - 2])
                            v.scalar_tensor_tensor(ev[o % 2][:, 0:W], qp[:, o * 512:o * 512 + W],
                                                   1.0, rkb[:, 0:W],
                                                   op0=ALU.mult, op1=ALU.mult).then_inc(vB.h, 1)
                            vB.n += 1
                            stu2 += 1
                    else:
                        nm = W // 128
                        for m in range(nm):
                            v.wait_ge(tB.h, tB0 + stu2 + 1)
                            if stu2 >= 2:
                                v.wait_ge(dS.h, STORE_DONE[stu2 - 2])
                            v.tensor_copy(vev[m % 2][:, 0:2048], PS1[:, 0:2048]).then_inc(vB.h, 1)
                            vB.n += 1
                            stu2 += 1
            barrier()

        def p1b1():
            dG0, dS0 = dG.n, dS.n
            tB0, tC0, tD0 = tB.n, tC.n, tD.n
            vA0, vB0, vC0 = vA.n, vB.n, vC.n
            sA0, sB0 = sA.n, sB.n
            kr = A[:, 0:24576]
            vr = A[:, 24576:49152]
            qsb = (B[:, 0:2048], B[:, 2048:4096])
            aosb = (B[:, 4096:6144], B[:, 6144:8192])
            exps = B[:, 8192:9216]
            ptr = B[:, 10240:11264]
            diag = B[:, 11264:11392]
            nmx = Fb[:, 0:1]
            ebv = Fb[:, 1:2]
            zr = Fb[:, 2:3]
            rz = Fb[:, 3:4]

            def new_ecs(lc):
                return list(range(12)) if lc == 0 else [lc + 11]

            LOADS_DONE = {}
            AOST_DONE = {}
            cnt = dS.n
            for lc in range(LCH):
                cnt += 16 * (17 * len(new_ecs(lc)) + 16)
                LOADS_DONE[lc] = cnt
                if lc >= 1:
                    cnt += 16 * 16
                    AOST_DONE[lc - 1] = cnt
            cnt += 16 * 16
            AOST_DONE[LCH - 1] = cnt

            @sect("gpsimd")
            def _(g):
                if DEBUG:
                    g.dma_start(out=rqDbg, in_=rqA[0:1, 0:2048]).then_inc(dG.h, 16)
                    dG.n += 16
                    g.wait_ge(vC.h, vC.n + 1)
                    g.dma_start(out=rqSDbg, in_=rqS[:]).then_inc(dG.h, 16)
                    dG.n += 16

            @sect("tensor")
            def _(t):
                for c in range(16):
                    t.matmul(PS1[:, 1024 + c:1024 + c + 1], rqA[0:1, c * 128:(c + 1) * 128],
                             smOF[0:1, 0:1], start=True, stop=True)
                t.drain().then_inc(tA.h, 1); tA.n += 1

            @sect("vector")
            def _(v):
                v.wait_ge(tA.h, tA.n)
                v.tensor_copy(rqT[:, 0:16], PS1[:, 1024:1040])
                v.tensor_copy(spcV[:], spcV[:])
                v.tensor_scalar(rqS[:], rqT[:], SCALE, None, op0=ALU.mult)
                v.tensor_copy(spcV[:], spcV[:]).then_inc(vC.h, 1)
                vC.n += 1

            @sect("sync")
            def _(s):
                for lc in range(LCH):
                    if lc >= 1:
                        s.wait_ge(tD.h, tD0 + 16 * lc)
                    for ec in new_ecs(lc):
                        slot = ec % 12
                        for h in range(H):
                            s.dma_start(out=kr[:, slot * 2048 + h * 128:slot * 2048 + (h + 1) * 128],
                                        in_=kkD[h, :, ec * 128:(ec + 1) * 128]).then_inc(dS.h, 16)
                            dS.n += 16
                        s.dma_start(out=vr[:, slot * 2048:(slot + 1) * 2048],
                                    in_=vtD[ec * 128:(ec + 1) * 128, :]).then_inc(dS.h, 16)
                        dS.n += 16
                    if lc >= 2:
                        s.wait_ge(tB.h, tB0 + 16 * (lc - 1))
                    for h in range(H):
                        s.dma_start(out=qsb[lc % 2][:, h * 128:(h + 1) * 128],
                                    in_=kqD[h, :, lc * 128:(lc + 1) * 128]).then_inc(dS.h, 16)
                        dS.n += 16
                    if lc >= 1:
                        for h in range(H):
                            s.wait_ge(vB.h, vB0 + 16 * (lc - 1) + h + 1)
                            s.dma_start(out=aoD[h, :, (lc - 1) * 128:lc * 128],
                                        in_=aosb[(lc - 1) % 2][:, h * 128:(h + 1) * 128]).then_inc(dS.h, 16)
                            dS.n += 16
                for h in range(H):
                    s.wait_ge(vB.h, vB0 + 16 * (LCH - 1) + h + 1)
                    s.dma_start(out=aoD[h, :, (LCH - 1) * 128:LCH * 128],
                                in_=aosb[(LCH - 1) % 2][:, h * 128:(h + 1) * 128]).then_inc(dS.h, 16)
                    dS.n += 16
                s.wait_ge(dS.h, dS.n)

            sC0 = sC.n
            mb = Fb[:, 16:1040]

            @sect("tensor")
            def _(t):
                for lc in range(LCH):
                    ecs = [lc + 8 + db for db in DB]
                    t.wait_ge(dS.h, LOADS_DONE[lc])
                    for h in range(H):
                        j = lc * 16 + h
                        half = PS0[:, (h % 2) * 1024:(h % 2) * 1024 + 1024]
                        if j >= 2:
                            t.wait_ge(vA.h, vA0 + j - 1)
                        for bi, ec in enumerate(ecs):
                            slot = ec % 12
                            t.matmul(half[:, bi * 128:(bi + 1) * 128],
                                     qsb[lc % 2][:, h * 128:(h + 1) * 128],
                                     kr[:, slot * 2048 + h * 128:slot * 2048 + (h + 1) * 128],
                                     start=True, stop=True)
                        t.drain().then_inc(tB.h, 1); tB.n += 1
                        t.wait_ge(sB.h, sB0 + j + 1)
                        if j >= 1:
                            t.wait_ge(sC.h, sC0 + j)
                        for bi in range(8):
                            t.matmul(PS1[:, bi * 128:(bi + 1) * 128],
                                     exps[:, bi * 128:(bi + 1) * 128], diag[:],
                                     start=True, stop=True)
                        t.drain().then_inc(tC.h, 1); tC.n += 1
                        t.wait_ge(sC.h, sC0 + j + 1)
                        if j >= 1:
                            t.wait_ge(vB.h, vB0 + j)
                        for bi, ec in enumerate(ecs):
                            slot = ec % 12
                            t.matmul(PS1[:, 1024:1152],
                                     vr[:, slot * 2048 + h * 128:slot * 2048 + (h + 1) * 128],
                                     ptr[:, bi * 128:(bi + 1) * 128],
                                     start=(bi == 0), stop=(bi == 7))
                        t.drain().then_inc(tD.h, 1); tD.n += 1

            @sect("vector")
            def _(v):
                for lc in range(LCH):
                    for h in range(H):
                        j = lc * 16 + h
                        v.wait_ge(tB.h, tB0 + j + 1)
                        half = PS0[:, (h % 2) * 1024:(h % 2) * 1024 + 1024]
                        v.scalar_tensor_tensor(mb[:], half, 1.0, smM[:],
                                               op0=ALU.mult, op1=ALU.add)
                        v.tensor_reduce(nmx[:, 0:1], mb[:], axis=AX.X, op=ALU.max, negate=True)
                        v.tensor_copy(spcV[:], spcV[:])
                        v.tensor_tensor(ebv[:, 0:1], nmx[:, 0:1],
                                        rqS[:, lc:lc + 1], op=ALU.mult)
                        v.tensor_copy(spcV[:], spcV[:]).then_inc(vA.h, 1)
                        vA.n += 1
                        v.wait_ge(sA.h, sA0 + j + 1)
                        v.tensor_reduce(zr[:, 0:1], exps[:], axis=AX.X, op=ALU.add)
                        v.tensor_copy(spcV[:], spcV[:])
                        v.reciprocal(rz[:, 0:1], zr[:, 0:1])
                        v.tensor_copy(spcV[:], spcV[:]).then_inc(vC.h, 1)
                        vC.n += 1
                        v.wait_ge(tD.h, tD0 + j + 1)
                        if lc >= 2 and h == 0:
                            v.wait_ge(dS.h, AOST_DONE[lc - 2])
                        v.tensor_copy(aosb[lc % 2][:, h * 128:(h + 1) * 128],
                                      PS1[:, 1024:1152]).then_inc(vB.h, 1)
                        vB.n += 1

            @sect("scalar")
            def _(s):
                for lc in range(LCH):
                    for h in range(H):
                        j = lc * 16 + h
                        s.wait_ge(vA.h, vA0 + j + 1)
                        if j >= 1:
                            s.wait_ge(vC.h, vC0 + 1 + j)
                        s.activation(exps[:], mb[:], AF.Exp, bias=ebv[:, 0:1],
                                     scale=rqS[:, lc:lc + 1])
                        s.copy(spcS[:], spcS[:]).then_inc(sA.h, 1); sA.n += 1
                        s.wait_ge(vC.h, vC0 + 1 + j + 1)
                        s.activation(diag[:], smI[:], AF.Copy, scale=rz[:, 0:1])
                        s.copy(spcS[:], spcS[:]).then_inc(sB.h, 1); sB.n += 1
                        s.wait_ge(tC.h, tC0 + j + 1)
                        if j >= 1:
                            s.wait_ge(tD.h, tD0 + j)
                        s.activation(ptr[:], PS1[:, 0:1024], AF.Copy)
                        s.copy(spcS[:], spcS[:]).then_inc(sC.h, 1); sC.n += 1
            barrier()

        def p1b2():
            dG0, dS0, dC0 = dG.n, dS.n, dC.n
            tB0, vA0, vB0, vC0, sA0 = tB.n, vA.n, vB.n, vC.n, sA.n
            wA = A[:, 0:32768]
            aold = (B[:, 0:2048], B[:, 2048:4096])
            dump = B[:, 4096:6144]
            xr = (Fb[:, 0:2048], Fb[:, 2048:4096])
            xsum = Fb[:, 4096:6144]
            s1c = stats[:, 0:1]
            s2c = stats[:, 1:2]
            mu = stats[:, 2:3]
            iv = stats[:, 3:4]
            rstd = stats[:, 4:5]
            nmr = stats[:, 5:6]

            @sect("gpsimd")
            def _(g):
                for i in range(16):
                    g.dma_start(out=wA[:, i * 2048:(i + 1) * 2048],
                                in_=woT[i * 128:(i + 1) * 128, :]).then_inc(dG.h, 16)
                    dG.n += 16

            @sect("sync")
            def _(s):
                for lc in range(LCH):
                    if lc >= 2:
                        s.wait_ge(tB.h, tB0 + lc - 1)
                    for h in range(H):
                        s.dma_start(out=aold[lc % 2][:, h * 128:(h + 1) * 128],
                                    in_=aoD[h, :, lc * 128:(lc + 1) * 128]).then_inc(dS.h, 16)
                        dS.n += 16
                    if lc >= 2:
                        s.wait_ge(vA.h, vA0 + lc - 1)
                    s.dma_start(out=xr[lc % 2][:],
                                in_=xe[OFF + lc * 128:OFF + (lc + 1) * 128, :]).then_inc(dS.h, 16)
                    dS.n += 16
                s.wait_ge(dS.h, dS.n)

            @sect("tensor")
            def _(t):
                t.wait_ge(dG.h, dG0 + 256)
                for lc in range(LCH):
                    t.wait_ge(dS.h, dS0 + 272 * lc + 256)
                    PSx = PS0 if lc % 2 == 0 else PS1
                    if lc >= 2:
                        t.wait_ge(vA.h, vA0 + lc - 1)
                    for hh in range(16):
                        for og in range(4):
                            t.matmul(PSx[:, og * 512:(og + 1) * 512],
                                     aold[lc % 2][:, hh * 128:(hh + 1) * 128],
                                     wA[:, hh * 2048 + og * 512:hh * 2048 + (og + 1) * 512],
                                     start=(hh == 0), stop=False)
                    for og in range(4):
                        t.matmul(PSx[:, og * 512:(og + 1) * 512], smO[0:1, 0:128],
                                 rbo[0:1, og * 512:(og + 1) * 512], start=False, stop=True)
                    t.drain().then_inc(tB.h, 1); tB.n += 1

            @sect("vector")
            def _(v):
                for lc in range(LCH):
                    v.wait_ge(tB.h, tB0 + lc + 1)
                    v.wait_ge(dS.h, dS0 + 272 * (lc + 1))
                    if lc >= 1:
                        v.wait_ge(dC.h, dC0 + 16 * lc)
                    PSx = PS0 if lc % 2 == 0 else PS1
                    v.tensor_tensor(xsum[:], PSx[:, 0:2048], xr[lc % 2][:], op=ALU.add)
                    v.drain().then_inc(vA.h, 1); vA.n += 1
                    v.tensor_reduce(s1c, xsum[:], axis=AX.X, op=ALU.add)
                    v.wait_ge(sA.h, sA0 + 2 * lc + 1)
                    v.tensor_reduce(s2c, dump[:], axis=AX.X, op=ALU.add)
                    v.tensor_copy(spcV[:], spcV[:])
                    v.tensor_scalar(mu, s1c, -1.0 / 2048.0, None, op0=ALU.mult)
                    v.tensor_scalar(iv, s2c, 1.0 / 2048.0, EPS_LN, op0=ALU.mult, op1=ALU.add)
                    v.tensor_tensor(nmr, mu, mu, op=ALU.mult)
                    v.tensor_copy(spcV[:], spcV[:])
                    v.tensor_tensor(iv, iv, nmr, op=ALU.subtract)
                    v.tensor_copy(spcV[:], spcV[:])
                    v.reciprocal(iv, iv)
                    v.tensor_copy(spcV[:], spcV[:]).then_inc(vC.h, 1); vC.n += 1
                    v.wait_ge(sA.h, sA0 + 2 * lc + 2)
                    v.tensor_scalar(xsum[:], xsum[:], rstd[:, 0:1], nmr[:, 0:1],
                                    op0=ALU.mult, op1=ALU.add).then_inc(vB.h, 1)
                    vB.n += 1

            @sect("scalar")
            def _(s):
                for lc in range(LCH):
                    s.wait_ge(vA.h, vA0 + lc + 1)
                    s.activation(dump[:], xsum[:], AF.Square)
                    s.copy(spcS[:], spcS[:]).then_inc(sA.h, 1)
                    sA.n += 1
                    s.wait_ge(vC.h, vC0 + lc + 1)
                    s.activation(rstd, iv, AF.Sqrt)
                    s.copy(spcS[:], spcS[:])
                    s.mul(nmr, rstd, mu[:, 0:1])
                    s.copy(spcS[:], spcS[:]).then_inc(sA.h, 1); sA.n += 1
                    s.wait_ge(vB.h, vB0 + lc + 1)
                    s.dma_start(out=x1D[lc * 128:(lc + 1) * 128, :], in_=xsum[:]).then_inc(dC.h, 16)
                    dC.n += 16
                s.wait_ge(dC.h, dC.n)
            barrier()

        def p2a():
            dG0, dS0 = dG.n, dS.n
            tA0, tB0, vA0, vB0, vC0, sA0 = tA.n, tB.n, vA.n, vB.n, vC.n, sA.n
            x1fm = A[:, 0:32768]
            wgr = (A[:, 32768:34816], A[:, 34816:36864])
            wur = (A[:, 36864:38912], A[:, 38912:40960])
            wgc = (A[:, 40960:43008], A[:, 43008:45056])
            wuc = (A[:, 45056:47104], A[:, 47104:49152])
            xtm = (B[:, 0:2048], B[:, 2048:4096], B[:, 4096:6144])
            gsb = B[:, 6144:6656]
            usb = B[:, 6656:7168]
            hsb = (B[:, 7168:7680], B[:, 7680:8192])

            @sect("gpsimd")
            def _(g):
                for lc in range(LCH):
                    if lc >= 3:
                        g.wait_ge(tA.h, tA0 + lc - 2)
                    g.dma_start(out=xtm[lc % 3][:], in_=x1D[lc * 128:(lc + 1) * 128, :]).then_inc(dG.h, 16)
                    dG.n += 16
                for o in range(NHC):
                    if o >= 2:
                        g.wait_ge(vA.h, vA0 + 2 * (o - 1))
                    g.dma_start(out=wgr[o % 2][:],
                                in_=gTb[o].rearrange("(i p) c -> p i c", p=128)).then_inc(dG.h, 16)
                    dG.n += 16
                    g.dma_start(out=wur[o % 2][:],
                                in_=uTb[o].rearrange("(i p) c -> p i c", p=128)).then_inc(dG.h, 16)
                    dG.n += 16
                g.wait_ge(tA.h, tA0 + LCH)
                for _ in range(2):
                    g.dma_start(out=xtm[0][:, 0:512], in_=x1D[0:128, 0:512]).then_inc(dG.h, 16)
                    dG.n += 16

            @sect("tensor")
            def _(t):
                for lc in range(LCH):
                    t.wait_ge(dG.h, dG0 + 16 * min(lc + 2, LCH + 2))
                    if lc >= 1:
                        t.wait_ge(vB.h, vB0 + lc)
                    for i in range(16):
                        t.matmul(PS1[:, i * 128:(i + 1) * 128], xtm[lc % 3][:, i * 128:(i + 1) * 128],
                                 smI[:], start=True, stop=True)
                    t.drain().then_inc(tA.h, 1); tA.n += 1
                for o in range(NHC):
                    t.wait_ge(vA.h, vA0 + 2 * (o + 1))
                    if o == 0:
                        t.wait_ge(vB.h, vB0 + LCH)
                    for gi in range(4):
                        j = o * 4 + gi
                        bb = (gi % 2) * 1024
                        if j >= 2:
                            t.wait_ge(sA.h, sA0 + j - 1)
                            t.wait_ge(vC.h, vC0 + j - 1)
                        for i in range(16):
                            t.matmul(PS0[:, bb:bb + 512],
                                     wgc[o % 2][:, i * 128:(i + 1) * 128],
                                     x1fm[:, i * 2048 + gi * 512:i * 2048 + (gi + 1) * 512],
                                     start=(i == 0), stop=(i == 15))
                        for i in range(16):
                            t.matmul(PS0[:, bb + 512:bb + 1024],
                                     wuc[o % 2][:, i * 128:(i + 1) * 128],
                                     x1fm[:, i * 2048 + gi * 512:i * 2048 + (gi + 1) * 512],
                                     start=(i == 0), stop=(i == 15))
                        t.drain().then_inc(tB.h, 1); tB.n += 1

            @sect("vector")
            def _(v):
                for lc in range(LCH):
                    v.wait_ge(tA.h, tA0 + lc + 1)
                    for i in range(16):
                        v.tensor_copy(x1fm[:, i * 2048 + lc * 128:i * 2048 + (lc + 1) * 128],
                                      PS1[:, i * 128:(i + 1) * 128])
                    v.drain().then_inc(vB.h, 1); vB.n += 1
                for o in range(NHC):
                    v.wait_ge(dG.h, dG0 + 16 * LCH + 32 * min(o + 2, NHC + 1))
                    if o >= 2:
                        v.wait_ge(tB.h, tB0 + 4 * (o - 1))
                    v.tensor_scalar(wgc[o % 2][:], wgr[o % 2][:], -128.0, None, op0=ALU.add)
                    v.tensor_scalar(wuc[o % 2][:], wur[o % 2][:], -128.0, None, op0=ALU.add)
                    v.tensor_copy(spcV[:], spcV[:]).then_inc(vA.h, 2)
                    vA.n += 2
                    for gi in range(4):
                        j = o * 4 + gi
                        v.wait_ge(tB.h, tB0 + j + 1)
                        v.tensor_scalar(usb[:, 0:512], PS0[:, (gi % 2) * 1024 + 512:(gi % 2) * 1024 + 1024],
                                        suc[:, o:o + 1], None, op0=ALU.mult)
                        v.wait_ge(sA.h, sA0 + j + 1)
                        if j >= 2:
                            v.wait_ge(dS.h, dS0 + 16 * (j - 1))
                        v.tensor_tensor(hsb[j % 2][:, 0:512], gsb[:, 0:512],
                                        usb[:, 0:512], op=ALU.mult).then_inc(vC.h, 1)
                        vC.n += 1

            @sect("scalar")
            def _(s):
                for o in range(NHC):
                    for gi in range(4):
                        j = o * 4 + gi
                        s.wait_ge(tB.h, tB0 + j + 1)
                        if j >= 1:
                            s.wait_ge(vC.h, vC0 + j)
                        s.activation(gsb[:, 0:512], PS0[:, (gi % 2) * 1024:(gi % 2) * 1024 + 512],
                                     AF.Silu, scale=sgc[:, o:o + 1]).then_inc(sA.h, 1)
                        sA.n += 1

            @sect("sync")
            def _(s):
                for o in range(NHC):
                    for gi in range(4):
                        j = o * 4 + gi
                        s.wait_ge(vC.h, vC0 + j + 1)
                        s.dma_start(out=hD[o, :, gi * 512:(gi + 1) * 512],
                                    in_=hsb[j % 2][:, 0:512]).then_inc(dS.h, 16)
                        dS.n += 16
                s.wait_ge(dS.h, dS.n)
            barrier()

        def p2b():
            dG0, dS0, dC0 = dG.n, dS.n, dC.n
            tA0, tB0, vA0, vB0 = tA.n, tB.n, vA.n, vB.n
            dTq = A[:, 0:22528]
            hring = A[:, 22528:26624]
            wdr = tuple(B[:, i * 512:(i + 1) * 512] for i in range(4))
            wdum = B[:, 2048:2560]
            fev = Fb[:, 0:2048]

            @sect("gpsimd")
            def _(g):
                for qd in range(4):
                    for o in range(NHC):
                        j = qd * NHC + o
                        if j >= 4:
                            g.wait_ge(vA.h, vA0 + j - 3)
                        g.dma_start(out=wdr[j % 4][:],
                                    in_=dT[o * 128:(o + 1) * 128, qd * 512:(qd + 1) * 512]).then_inc(dG.h, 16)
                        dG.n += 16
                for _ in range(2):
                    g.dma_start(out=wdum[:], in_=dT[0:128, 0:512]).then_inc(dG.h, 16)
                    dG.n += 16

            @sect("vector")
            def _(v):
                for qd in range(4):
                    for o in range(NHC):
                        j = qd * NHC + o
                        v.wait_ge(dG.h, dG0 + 16 * min(j + 3, 4 * NHC + 2))
                        if qd >= 1 and o == 0:
                            v.wait_ge(tA.h, tA0 + 4 * qd)
                        v.scalar_tensor_tensor(dTq[:, o * 512:(o + 1) * 512], wdr[j % 4][:],
                                               -128.0, sdb[:, qd * 512:(qd + 1) * 512],
                                               op0=ALU.add, op1=ALU.mult)
                        if o == NHC - 1:
                            v.tensor_copy(spcV[:], spcV[:])
                            v.tensor_copy(spcV[:], spcV[:])
                            v.tensor_copy(spcV[:], spcV[:])
                        v.tensor_copy(spcV[:], spcV[:]).then_inc(vA.h, 1)
                        vA.n += 1
                    for gi in range(4):
                        k = qd * 4 + gi
                        v.wait_ge(tA.h, tA0 + k + 1)
                        if k >= 1:
                            v.wait_ge(dC.h, dC0 + 64 * k)
                        PSx = PS0 if gi % 2 == 0 else PS1
                        for lci in range(4):
                            v.tensor_copy(fev[:, lci * 512:(lci + 1) * 512],
                                          PSx[:, lci * 512:(lci + 1) * 512])
                        v.drain().then_inc(vB.h, 1); vB.n += 1

            @sect("sync")
            def _(s):
                nh = 0
                for qd in range(4):
                    for gi in range(4):
                        for o in range(NHC):
                            nh += 1
                            if nh > 8:
                                s.wait_ge(tB.h, tB0 + nh - 8)
                            s.dma_start(out=hring[:, ((nh - 1) % 8) * 512:((nh - 1) % 8 + 1) * 512],
                                        in_=hD[o, :, gi * 512:(gi + 1) * 512]).then_inc(dS.h, 16)
                            dS.n += 16

            @sect("tensor")
            def _(t):
                nh = 0
                for qd in range(4):
                    for gi in range(4):
                        k = qd * 4 + gi
                        PSx = PS0 if gi % 2 == 0 else PS1
                        if k >= 2:
                            t.wait_ge(vB.h, vB0 + k - 1)
                        for o in range(NHC):
                            nh += 1
                            t.wait_ge(dS.h, dS0 + 16 * nh)
                            t.wait_ge(vA.h, vA0 + min(qd * NHC + o + 2, (qd + 1) * NHC))
                            slot = (nh - 1) % 8
                            for lci in range(4):
                                t.matmul(PSx[:, lci * 512:(lci + 1) * 512],
                                         hring[:, slot * 512 + lci * 128:slot * 512 + (lci + 1) * 128],
                                         dTq[:, o * 512:(o + 1) * 512],
                                         start=(o == 0), stop=(o == NHC - 1))
                            t.drain().then_inc(tB.h, 1); tB.n += 1
                        t.drain().then_inc(tA.h, 1); tA.n += 1

            @sect("scalar")
            def _(s):
                for qd in range(4):
                    for gi in range(4):
                        k = qd * 4 + gi
                        s.wait_ge(vB.h, vB0 + k + 1)
                        for lci in range(4):
                            r0 = (gi * 4 + lci) * 128
                            s.dma_start(out=fD[r0:r0 + 128, qd * 512:(qd + 1) * 512],
                                        in_=fev[:, lci * 512:(lci + 1) * 512]).then_inc(dC.h, 16)
                            dC.n += 16
                s.wait_ge(dC.h, dC.n)
            barrier()

        def p2c():
            dS0, dC0 = dS.n, dC.n
            vA0, vB0, vC0, sA0 = vA.n, vB.n, vC.n, sA.n
            fsb = (Fb[:, 0:2048], Fb[:, 2048:4096])
            xsum = Fb[:, 4096:6144]
            dump = B[:, 0:2048]
            s1c = stats[:, 0:1]
            s2c = stats[:, 1:2]
            mu = stats[:, 2:3]
            iv = stats[:, 3:4]
            rstd = stats[:, 4:5]
            nmr = stats[:, 5:6]

            @sect("sync")
            def _(s):
                for lc in range(LCH):
                    if lc >= 1:
                        s.wait_ge(dC.h, dC0 + 16 * lc)
                    s.dma_start(out=xsum[:], in_=x1D[lc * 128:(lc + 1) * 128, :]).then_inc(dS.h, 16)
                    dS.n += 16
                    if lc >= 2:
                        s.wait_ge(vC.h, vC0 + lc - 1)
                    s.dma_start(out=fsb[lc % 2][:], in_=fD[lc * 128:(lc + 1) * 128, :]).then_inc(dS.h, 16)
                    dS.n += 16

            @sect("vector")
            def _(v):
                for lc in range(LCH):
                    v.wait_ge(dS.h, dS0 + 32 * (lc + 1))
                    v.tensor_tensor(xsum[:], xsum[:], fsb[lc % 2][:], op=ALU.add)
                    v.drain().then_inc(vC.h, 1); vC.n += 1
                    v.tensor_reduce(s1c, xsum[:], axis=AX.X, op=ALU.add)
                    v.wait_ge(sA.h, sA0 + 2 * lc + 1)
                    v.tensor_reduce(s2c, dump[:], axis=AX.X, op=ALU.add)
                    v.tensor_copy(spcV[:], spcV[:])
                    v.tensor_scalar(mu, s1c, -1.0 / 2048.0, None, op0=ALU.mult)
                    v.tensor_scalar(iv, s2c, 1.0 / 2048.0, EPS_LN, op0=ALU.mult, op1=ALU.add)
                    v.tensor_tensor(nmr, mu, mu, op=ALU.mult)
                    v.tensor_copy(spcV[:], spcV[:])
                    v.tensor_tensor(iv, iv, nmr, op=ALU.subtract)
                    v.tensor_copy(spcV[:], spcV[:])
                    v.reciprocal(iv, iv)
                    v.tensor_copy(spcV[:], spcV[:]).then_inc(vB.h, 1); vB.n += 1
                    v.wait_ge(sA.h, sA0 + 2 * lc + 2)
                    v.tensor_scalar(xsum[:], xsum[:], rstd[:, 0:1], nmr[:, 0:1],
                                    op0=ALU.mult, op1=ALU.add).then_inc(vA.h, 1)
                    vA.n += 1

            @sect("scalar")
            def _(s):
                for lc in range(LCH):
                    s.wait_ge(vC.h, vC0 + lc + 1)
                    s.activation(dump[:], xsum[:], AF.Square)
                    s.copy(spcS[:], spcS[:]).then_inc(sA.h, 1)
                    sA.n += 1
                    s.wait_ge(vB.h, vB0 + lc + 1)
                    s.activation(rstd, iv, AF.Sqrt)
                    s.copy(spcS[:], spcS[:])
                    s.mul(nmr, rstd, mu[:, 0:1])
                    s.copy(spcS[:], spcS[:]).then_inc(sA.h, 1); sA.n += 1
                    s.wait_ge(vA.h, vA0 + lc + 1)
                    s.dma_start(out=y[lc * 128:(lc + 1) * 128, :], in_=xsum[:]).then_inc(dC.h, 16)
                    dC.n += 16
                s.wait_ge(dC.h, dC.n)
            barrier()

        p0()
        p1x()
        proj("q")
        proj("k")
        proj("v")
        p1b1()
        p1b2()
        p2a()
        p2b()
        p2c()
    return nc


def _prep(inputs):
    x = np.asarray(inputs["L0_attn_x"], dtype=np.float32)
    shared = {
        "wqT": np.ascontiguousarray(np.asarray(inputs["L0_attn_wq"], np.float32).T),
        "wkT": np.ascontiguousarray(np.asarray(inputs["L0_attn_wk"], np.float32).T),
        "wvT": np.ascontiguousarray(np.asarray(inputs["L0_attn_wv"], np.float32).T),
        "woT": np.ascontiguousarray(np.asarray(inputs["L0_attn_wo"], np.float32).T),
        "gTb": np.ascontiguousarray(
            np.asarray(inputs["L0_ffn_wq_gate"], np.int32).reshape(NHC, 128, DM).transpose(0, 2, 1)),
        "uTb": np.ascontiguousarray(
            np.asarray(inputs["L0_ffn_wq_up"], np.int32).reshape(NHC, 128, DM).transpose(0, 2, 1)),
        "dT": np.ascontiguousarray(np.asarray(inputs["L0_ffn_wq_down"], np.int32).T),
        "r_bq": np.asarray(inputs["L0_attn_bq"], np.float32).reshape(1, DM).copy(),
        "r_bk": np.asarray(inputs["L0_attn_bk"], np.float32).reshape(1, DM).copy(),
        "r_bv": np.asarray(inputs["L0_attn_bv"], np.float32).reshape(1, DM).copy(),
        "r_bo": np.asarray(inputs["L0_attn_bo"], np.float32).reshape(1, DM).copy(),
        "r_gq": np.asarray(inputs["L0_attn_gq"], np.float32).reshape(1, DM).copy(),
        "r_gk": np.asarray(inputs["L0_attn_gk"], np.float32).reshape(1, DM).copy(),
        "r_sg": np.asarray(inputs["L0_ffn_s_gate"], np.float32).reshape(1, HID).copy(),
        "r_su": np.asarray(inputs["L0_ffn_s_up"], np.float32).reshape(1, HID).copy(),
        "r_sd": np.asarray(inputs["L0_ffn_s_down"], np.float32).reshape(1, DM).copy(),
        "wmix": np.asarray(inputs["L0_attn_w_mix"], np.float32).reshape(1, 5).copy(),
        "cI": np.eye(128, dtype=np.float32).astype(bfd),
        "cMask": _mask01(),
        "cPm": _pmats(),
        "cOnes": np.ones((128, 512), dtype=np.float32).astype(bfd),
        "cOnesF": np.ones((1, 128), dtype=np.float32),
    }
    in_maps = []
    for c in range(NCORE):
        b, half = divmod(c, 2)
        p0_ = half * 2048
        idx = (p0_ - OFF + np.arange(EXT)) % 4096
        m = dict(shared)
        m["xe"] = np.ascontiguousarray(x[b, idx, :])
        in_maps.append(m)
    return in_maps


def kernel(**inputs):
    in_maps = _prep(inputs)
    nc = build_nc()
    res = run_bass_kernel_spmd(nc, in_maps, core_ids=list(range(NCORE)))
    out = np.empty((4, 4096, DM), dtype=np.float32)
    for c in range(NCORE):
        b, half = divmod(c, 2)
        out[b, half * 2048:(half + 1) * 2048, :] = res.results[c]["y"]
    return out


# version 02
# speedup vs baseline: 1.0649x; 1.0649x over previous
import numpy as np
import ml_dtypes
import concourse.bass as bass
import concourse.mybir as mybir
from concourse.bass_utils import run_bass_kernel_spmd

F32 = mybir.dt.float32
BF16 = mybir.dt.bfloat16
I32 = mybir.dt.int32
AF = mybir.ActivationFunctionType
ALU = mybir.AluOpType
AX = mybir.AxisListType
bfd = ml_dtypes.bfloat16

SHIFTS = [0, 1, -1, 3, -3, 7, -7, 20, -20, 53, -53, 141, -141, 380, -380, 1024]
CH = [0, 1, 2, 4, 8]
DB = [-8, -3, -2, -1, 0, 1, 2, 3]
H = 16
DM = 2048
HID = 5632
NHC = 44
TOK = 2048
EXT = 3456
ECH = 27
LCH = 16
OFF = 1024
EPS_RMS = 1e-6
EPS_LN = 1e-5
SCALE = 128.0 ** -0.5
NCORE = 8
DEBUG = False

EGRP = [(0, 512), (512, 512), (1024, 512), (1536, 512), (2048, 512), (2560, 512), (3072, 384)]
QGRP = [(0, 512), (512, 512), (1024, 512), (1536, 512)]


def _mask01():
    m = np.full((128, 8 * 128), -1e30, dtype=np.float32)
    ss = set(SHIFTS)
    for p in range(128):
        for bi, db in enumerate(DB):
            for q in range(128):
                if (p - q - db * 128) in ss:
                    m[p, bi * 128 + q] = 0.0
    assert ((m == 0).sum(1) == 16).all()
    return m.astype(bfd)


def _pmats():
    P = np.zeros((128, 5 * 128), dtype=np.float32)
    for ci, c in enumerate(CH):
        P[(np.arange(128) - c) % 128, ci * 128 + np.arange(128)] = 1.0
    return P.astype(bfd)


class Ct:
    def __init__(self, h):
        self.h = h
        self.n = 0


def build_nc():
    nc = bass.Bass("TRN2", target_bir_lowering=False)

    xe = nc.dram_tensor("xe", [EXT, DM], F32, kind="ExternalInput").ap()
    wqT = nc.dram_tensor("wqT", [DM, DM], F32, kind="ExternalInput").ap()
    wkT = nc.dram_tensor("wkT", [DM, DM], F32, kind="ExternalInput").ap()
    wvT = nc.dram_tensor("wvT", [DM, DM], F32, kind="ExternalInput").ap()
    woT = nc.dram_tensor("woT", [DM, DM], F32, kind="ExternalInput").ap()
    gTb = nc.dram_tensor("gTb", [NHC, DM, 128], I32, kind="ExternalInput").ap()
    uTb = nc.dram_tensor("uTb", [NHC, DM, 128], I32, kind="ExternalInput").ap()
    dT = nc.dram_tensor("dT", [HID, DM], I32, kind="ExternalInput").ap()
    r_bq = nc.dram_tensor("r_bq", [1, DM], F32, kind="ExternalInput").ap()
    r_bk = nc.dram_tensor("r_bk", [1, DM], F32, kind="ExternalInput").ap()
    r_bv = nc.dram_tensor("r_bv", [1, DM], F32, kind="ExternalInput").ap()
    r_bo = nc.dram_tensor("r_bo", [1, DM], F32, kind="ExternalInput").ap()
    r_gq = nc.dram_tensor("r_gq", [1, DM], F32, kind="ExternalInput").ap()
    r_gk = nc.dram_tensor("r_gk", [1, DM], F32, kind="ExternalInput").ap()
    r_sg = nc.dram_tensor("r_sg", [1, HID], F32, kind="ExternalInput").ap()
    r_su = nc.dram_tensor("r_su", [1, HID], F32, kind="ExternalInput").ap()
    r_sd = nc.dram_tensor("r_sd", [1, DM], F32, kind="ExternalInput").ap()
    wmix = nc.dram_tensor("wmix", [1, 5], F32, kind="ExternalInput").ap()
    cI = nc.dram_tensor("cI", [128, 128], BF16, kind="ExternalInput").ap()
    cMask = nc.dram_tensor("cMask", [128, 1024], BF16, kind="ExternalInput").ap()
    cPm = nc.dram_tensor("cPm", [128, 640], BF16, kind="ExternalInput").ap()
    cOnes = nc.dram_tensor("cOnes", [128, 512], BF16, kind="ExternalInput").ap()
    cOnesF = nc.dram_tensor("cOnesF", [1, 128], F32, kind="ExternalInput").ap()

    sk = "ExternalOutput" if DEBUG else "Internal"
    xfmD = nc.dram_tensor("xfmD", [DM, EXT], BF16, kind=sk).ap()
    kqD = nc.dram_tensor("kqD", [H, 128, TOK], BF16, kind=sk).ap()
    kkD = nc.dram_tensor("kkD", [H, 128, EXT], BF16, kind=sk).ap()
    vtD = nc.dram_tensor("vtD", [EXT, DM], BF16, kind=sk).ap()
    aoD = nc.dram_tensor("aoD", [H, 128, TOK], BF16, kind=sk).ap()
    x1D = nc.dram_tensor("x1D", [TOK, DM], F32, kind=sk).ap()
    hD = nc.dram_tensor("hD", [NHC, 128, TOK], BF16, kind=sk).ap()
    fD = nc.dram_tensor("fD", [TOK, DM], F32, kind=sk).ap()
    rqDbg = nc.dram_tensor("rqDbg", [1, TOK], F32, kind="ExternalOutput").ap() if DEBUG else None
    rqSDbg = nc.dram_tensor("rqSDbg", [128, 16], F32, kind="ExternalOutput").ap() if DEBUG else None
    y = nc.dram_tensor("y", [TOK, DM], F32, kind="ExternalOutput").ap()

    import contextlib
    st = contextlib.ExitStack()

    def sb(name, shp, dt):
        return st.enter_context(nc.sbuf_tensor(name, shp, dt))

    def psum(name, shp, dt):
        return st.enter_context(nc.psum_tensor(name, shp, dt))

    def sem(name):
        return Ct(st.enter_context(nc.semaphore(name)))

    with st:
        A = sb("A", [128, 49152], BF16)
        B = sb("B", [128, 16384], BF16)
        Fb = sb("Fb", [128, 6144], F32)
        smI = sb("smI", [128, 128], BF16)
        smM = sb("smM", [128, 1024], BF16)
        smP = sb("smP", [128, 640], BF16)
        smMq = sb("smMq", [128, 128], BF16)
        smO = sb("smO", [128, 512], BF16)
        smOF = sb("smOF", [1, 128], F32)
        gqc = sb("gqc", [128, 16], F32)
        gkc = sb("gkc", [128, 16], F32)
        sgc = sb("sgc", [128, 44], F32)
        suc = sb("suc", [128, 44], F32)
        sdb = sb("sdb", [128, 2048], F32)
        wmb = sb("wmb", [128, 8], F32)
        rbq = sb("rbq", [1, 2048], BF16)
        rbk = sb("rbk", [1, 2048], BF16)
        rbv = sb("rbv", [1, 2048], BF16)
        rbo = sb("rbo", [1, 2048], BF16)
        rsd = sb("rsd", [1, 2048], F32)
        rqA = sb("rqA", [1, 2048], F32)
        rqT = sb("rqT", [128, 16], F32)
        rqS = sb("rqS", [128, 16], F32)
        rowF = sb("rowF", [1, 1024], F32)
        stats = sb("stats", [128, 8], F32)
        spcV = sb("spcV", [128, 1], F32)
        spcS = sb("spcS", [128, 1], F32)

        PS0 = psum("PS0", [128, 2048], F32)
        PS1 = psum("PS1", [128, 2048], F32)

        bar = sem("bar")
        dG = sem("dG")
        dS = sem("dS")
        dC = sem("dC")
        tA = sem("tA"); tB = sem("tB"); tC = sem("tC"); tD = sem("tD")
        vA = sem("vA"); vB = sem("vB"); vC = sem("vC")
        sA = sem("sA"); sB = sem("sB"); sC = sem("sC")
        gA = sem("gA")

        blk = st.enter_context(nc.Block())
        ENGS = ["tensor", "vector", "scalar", "gpsimd", "sync"]
        barn = [0]

        def barrier():
            barn[0] += 1
            tgt = 5 * barn[0]
            for e in ENGS:
                def _f(en, tgt=tgt):
                    en.sem_inc(bar.h, 1)
                    en.wait_ge(bar.h, tgt)
                getattr(blk, e)(_f)

        def sect(e):
            return getattr(blk, e)

        def p0():
            tB0_p0 = tB.n
            dG0_rows = [0]
            rowP0 = Fb[0:1, 0:5632]

            @sect("gpsimd")
            def _(g):
                g.dma_start(out=smOF[:], in_=cOnesF).then_inc(dG.h, 16); dG.n += 16
                for out, src in ((smI, cI), (smM, cMask), (smP, cPm), (smO, cOnes)):
                    g.dma_start(out=out[:], in_=src).then_inc(dG.h, 16); dG.n += 16
                g.dma_start(out=wmb[0:1, 0:5], in_=wmix).then_inc(dG.h, 16); dG.n += 16
                dG0_rows[0] = dG.n
                for ri, (src, n) in enumerate(((r_gq, 16), (r_gk, 16), (r_sg, 44), (r_su, 44))):
                    g.dma_start(out=rowP0[0:1, 0:n * 128], in_=src).then_inc(dG.h, 16); dG.n += 16
                    g.dma_start(out=rowP0[0:1, 0:n * 128], in_=src).then_inc(dG.h, 16); dG.n += 16
                    g.wait_ge(gA.h, gA.n + 1); gA.n += 1
                for out, src in ((rbq, r_bq), (rbk, r_bk), (rbv, r_bv), (rbo, r_bo)):
                    g.dma_start(out=out[:], in_=src).then_inc(dG.h, 16); dG.n += 16
                g.dma_start(out=rsd[:], in_=r_sd).then_inc(dG.h, 16); dG.n += 16

            @sect("tensor")
            def _(t):
                nd = dG0_rows[0]
                vA0_p0 = vA.n
                for ri, (dst, n) in enumerate(((gqc, 16), (gkc, 16), (sgc, 44), (suc, 44))):
                    t.wait_ge(dG.h, nd + 32 * (ri + 1))
                    if ri >= 1:
                        t.wait_ge(vA.h, vA0_p0 + ri)
                    for c in range(n):
                        t.matmul(PS0[:, c:c + 1], rowP0[0:1, c * 128:(c + 1) * 128],
                                 smOF[0:1, 0:1], start=True, stop=True)
                    t.drain().then_inc(tB.h, 1); tB.n += 1
                t.wait_ge(dG.h, dG.n)
                t.wait_ge(vA.h, vA0_p0 + 4)
                t.matmul(PS0[:, 0:5], smOF[0:1, 0:128], wmb[0:1, 0:5], start=True, stop=True)
                for j in range(4):
                    t.matmul(PS1[:, j * 512:(j + 1) * 512], smOF[0:1, 0:128],
                             rsd[0:1, j * 512:(j + 1) * 512], start=True, stop=True)
                t.drain().then_inc(tA.h, 1); tA.n += 1

            @sect("scalar")
            def _(sc):
                sc.wait_ge(tA.h, tA.n)
                sc.copy(wmb[:, 0:5], PS0[:, 0:5])
                sc.copy(spcS[:], spcS[:]).then_inc(sA.h, 1); sA.n += 1

            @sect("vector")
            def _(v):
                for ri, (dst, n) in enumerate(((gqc, 16), (gkc, 16), (sgc, 44), (suc, 44))):
                    v.wait_ge(tB.h, tB0_p0 + ri + 1)
                    v.tensor_copy(dst[:, 0:n], PS0[:, 0:n])
                    v.tensor_copy(spcV[:], spcV[:]).then_inc(vA.h, 1); vA.n += 1
                    v.tensor_copy(spcV[:], spcV[:]).then_inc(gA.h, 1)
                v.wait_ge(tA.h, tA.n)
                v.tensor_copy(sdb[:, :], PS1[:, 0:2048])
                v.wait_ge(sA.h, sA.n)
                v.tensor_scalar(smMq[:], smP[:, 0:128], wmb[:, 0:1], None, op0=ALU.mult)
                for ci in range(1, 5):
                    v.tensor_copy(spcV[:], spcV[:])
                    v.scalar_tensor_tensor(smMq[:], smP[:, ci * 128:(ci + 1) * 128],
                                           wmb[:, ci:ci + 1], smMq[:],
                                           op0=ALU.mult, op1=ALU.add)
                v.tensor_copy(spcV[:], spcV[:])
            barrier()

        def p1x():
            d0, t0, v0, s0 = dG.n, tA.n, vA.n, dS.n
            xtm = (A[:, 32768:34816], A[:, 34816:36864], A[:, 36864:38912])
            ev = (B[:, 0:2048], B[:, 2048:4096])

            @sect("gpsimd")
            def _(g):
                for ec in range(ECH):
                    if ec >= 3:
                        g.wait_ge(tA.h, t0 + ec - 2)
                    g.dma_start(out=xtm[ec % 3][:], in_=xe[ec * 128:(ec + 1) * 128, :]).then_inc(dG.h, 16)
                    dG.n += 16
                g.wait_ge(tA.h, t0 + ECH - 2)
                g.dma_start(out=A[:, 38912:39424], in_=xe[0:128, 0:512]).then_inc(dG.h, 16)
                dG.n += 16

            @sect("tensor")
            def _(t):
                for ec in range(ECH):
                    t.wait_ge(dG.h, d0 + 16 * min(ec + 2, ECH + 1))
                    if ec >= 1:
                        t.wait_ge(vA.h, v0 + ec)
                    for i in range(16):
                        t.matmul(PS0[:, i * 128:(i + 1) * 128], xtm[ec % 3][:, i * 128:(i + 1) * 128],
                                 smI[:], start=True, stop=True)
                    t.drain().then_inc(tA.h, 1); tA.n += 1

            @sect("vector")
            def _(v):
                for ec in range(ECH):
                    v.wait_ge(tA.h, t0 + ec + 1)
                    if ec >= 2:
                        v.wait_ge(dS.h, s0 + 256 * (ec - 1))
                    for i in range(16):
                        v.tensor_copy(ev[ec % 2][:, i * 128:(i + 1) * 128], PS0[:, i * 128:(i + 1) * 128])
                    v.drain().then_inc(vA.h, 1); vA.n += 1

            @sect("sync")
            def _(s):
                for ec in range(ECH):
                    s.wait_ge(vA.h, v0 + ec + 1)
                    for i in range(16):
                        s.dma_start(out=xfmD[i * 128:(i + 1) * 128, ec * 128:(ec + 1) * 128],
                                    in_=ev[ec % 2][:, i * 128:(i + 1) * 128]).then_inc(dS.h, 16)
                        dS.n += 16
                s.wait_ge(dS.h, dS.n)
            barrier()

        def proj(kind):
            grps = QGRP if kind == "q" else EGRP
            NG = len(grps)
            wsrc = {"q": wqT, "k": wkT, "v": wvT}[kind]
            brow = {"q": rbq, "k": rbk, "v": rbv}[kind]
            outD = {"q": kqD, "k": kkD, "v": vtD}[kind]
            d0, tA0, tB0, tC0, tD0 = dG.n, tA.n, tB.n, tC.n, tD.n
            vB0, vC0 = vB.n, vC.n
            sA0, sB0, sC0 = sA.n, sB.n, sC.n
            gA0 = gA.n
            wA = A[:, 0:32768]
            xf = (A[:, 32768:40960], A[:, 40960:49152])
            qp = B[:, 0:8192]
            sq = (B[:, 8192:8704], B[:, 8704:9216])
            ev = (B[:, 9216:10240], B[:, 10240:11264])
            vev = (B[:, 8704:10752], B[:, 10752:12800])
            rkb = Fb[:, 0:512]

            nst = [(16 if kind != "v" else W // 128) for (c0, W) in grps]
            LOADS_DONE = {}
            STORE_DONE = {}
            cnt = dS.n
            stu = 0
            for gi in range(NG):
                cnt += 16 * 16
                LOADS_DONE[gi] = cnt
                if gi >= 1:
                    for _ in range(nst[gi - 1]):
                        cnt += 16
                        stu += 1
                        STORE_DONE[stu - 1] = cnt
            for _ in range(nst[NG - 1]):
                cnt += 16
                stu += 1
                STORE_DONE[stu - 1] = cnt

            @sect("gpsimd")
            def _(g):
                for i in range(16):
                    g.dma_start(out=wA[:, i * 2048:(i + 1) * 2048],
                                in_=wsrc[i * 128:(i + 1) * 128, :]).then_inc(dG.h, 16)
                    dG.n += 16

            @sect("sync")
            def _(s):
                stu2 = 0
                for gi, (c0, W) in enumerate(grps):
                    if gi >= 2:
                        s.wait_ge(tA.h, tA0 + gi - 1)
                    base = (OFF + c0) if kind == "q" else c0
                    for i in range(16):
                        s.dma_start(out=xf[gi % 2][:, i * 512:i * 512 + W],
                                    in_=xfmD[i * 128:(i + 1) * 128, base:base + W]).then_inc(dS.h, 16)
                        dS.n += 16
                    if gi >= 1:
                        c0p, Wp = grps[gi - 1]
                        for u in range(nst[gi - 1]):
                            s.wait_ge(vB.h, vB0 + stu2 + 1)
                            if kind == "q":
                                s.dma_start(out=kqD[u, :, c0p:c0p + Wp], in_=ev[u % 2][:, 0:Wp]).then_inc(dS.h, 16)
                            elif kind == "k":
                                s.dma_start(out=kkD[u, :, c0p:c0p + Wp], in_=ev[u % 2][:, 0:Wp]).then_inc(dS.h, 16)
                            else:
                                mc = c0p + u * 128
                                s.dma_start(out=vtD[mc:mc + 128, :], in_=vev[u % 2][:, 0:2048]).then_inc(dS.h, 16)
                            dS.n += 16
                            stu2 += 1
                c0p, Wp = grps[NG - 1]
                for u in range(nst[NG - 1]):
                    s.wait_ge(vB.h, vB0 + stu2 + 1)
                    if kind == "q":
                        s.dma_start(out=kqD[u, :, c0p:c0p + Wp], in_=ev[u % 2][:, 0:Wp]).then_inc(dS.h, 16)
                    elif kind == "k":
                        s.dma_start(out=kkD[u, :, c0p:c0p + Wp], in_=ev[u % 2][:, 0:Wp]).then_inc(dS.h, 16)
                    else:
                        mc = c0p + u * 128
                        s.dma_start(out=vtD[mc:mc + 128, :], in_=vev[u % 2][:, 0:2048]).then_inc(dS.h, 16)
                    dS.n += 16
                    stu2 += 1
                s.wait_ge(dS.h, dS.n)

            @sect("tensor")
            def _(t):
                midx = 0
                for gi, (c0, W) in enumerate(grps):
                    t.wait_ge(dS.h, LOADS_DONE[gi])
                    if gi == 0:
                        t.wait_ge(dG.h, d0 + 256)
                    if kind in ("q", "k"):
                        if kind == "k" and gi >= 1:
                            t.wait_ge(vB.h, vB0 + 16 * gi)
                        def sumsq_mm(o):
                            t.wait_ge(sB.h, sB0 + gi * 16 + o + 1)
                            t.matmul(PS1[0:1, 1024:1024 + W], smO[:, 0:1], sq[o % 2][:, 0:W],
                                     start=(o == 0), stop=(o == 15)).then_inc(tC.h, 1)
                            tC.n += 1
                        for o in range(16):
                            bank = PS1[:, (o % 2) * 512:(o % 2) * 512 + W]
                            j = gi * 16 + o
                            if j >= 2:
                                t.wait_ge(sA.h, sA0 + j - 1)
                            for i in range(16):
                                t.matmul(bank, wA[:, i * 2048 + o * 128:i * 2048 + (o + 1) * 128],
                                         xf[gi % 2][:, i * 512:i * 512 + W],
                                         start=(i == 0), stop=False)
                            t.matmul(bank, brow[0:1, o * 128:(o + 1) * 128], smO[0:1, 0:W],
                                     start=False, stop=True).then_inc(tB.h, 1)
                            tB.n += 1
                            if o >= 1:
                                sumsq_mm(o - 1)
                        sumsq_mm(15)
                        t.drain().then_inc(tA.h, 1); tA.n += 1
                        if kind == "k":
                            t.wait_ge(sC.h, sC0 + gi + 1)
                            t.matmul(PS0[:, 0:W], smOF[0:1, 0:128], rowF[0:1, 0:W],
                                     start=True, stop=True).then_inc(tD.h, 1)
                            tD.n += 1
                        if kind == "q":
                            for h in range(H):
                                j = gi * 16 + h
                                t.wait_ge(sA.h, sA0 + j + 1)
                                if j >= 1:
                                    t.wait_ge(vB.h, vB0 + j)
                                t.matmul(PS1[:, 1536:1536 + W], smMq[:],
                                         qp[:, h * 512:h * 512 + W],
                                         start=True, stop=True).then_inc(tD.h, 1)
                                tD.n += 1
                    else:
                        nm = W // 128
                        for m in range(nm):
                            if midx >= 1:
                                t.wait_ge(vB.h, vB0 + midx)
                            for og in range(4):
                                seg = PS1[:, og * 512:(og + 1) * 512]
                                for i in range(16):
                                    t.matmul(seg, xf[gi % 2][:, i * 512 + m * 128:i * 512 + (m + 1) * 128],
                                             wA[:, i * 2048 + og * 512:i * 2048 + (og + 1) * 512],
                                             start=(i == 0), stop=False)
                                t.matmul(seg, smO[0:1, 0:128], brow[0:1, og * 512:(og + 1) * 512],
                                         start=False, stop=True)
                            t.drain().then_inc(tB.h, 1); tB.n += 1
                            midx += 1
                        t.drain().then_inc(tA.h, 1); tA.n += 1

            @sect("scalar")
            def _(s):
                if kind == "v":
                    return
                gcol = gqc if kind == "q" else gkc
                for gi, (c0, W) in enumerate(grps):
                    for o in range(16):
                        s.wait_ge(tB.h, tB0 + gi * 16 + o + 1)
                        if gi * 16 + o >= 2:
                            s.wait_ge(tC.h, tC0 + gi * 16 + o - 1)
                        bank = PS1[:, (o % 2) * 512:(o % 2) * 512 + W]
                        s.activation(sq[o % 2][:, 0:W], bank, AF.Square).then_inc(sB.h, 1); sB.n += 1
                        s.activation(qp[:, o * 512:o * 512 + W], bank, AF.Copy,
                                     scale=gcol[:, o:o + 1]).then_inc(sA.h, 1); sA.n += 1
                    s.wait_ge(vC.h, vC0 + gi + 1)
                    dst = rqA[0:1, c0:c0 + W] if kind == "q" else rowF[0:1, 0:W]
                    s.activation(dst, rowF[0:1, 512:512 + W], AF.Sqrt)
                    s.copy(spcS[:], spcS[:]).then_inc(sC.h, 1); sC.n += 1

            @sect("vector")
            def _(v):
                stu2 = 0
                for gi, (c0, W) in enumerate(grps):
                    if kind in ("q", "k"):
                        v.wait_ge(tC.h, tC0 + 16 * (gi + 1))
                        if gi >= 1:
                            v.wait_ge(sC.h, sC0 + gi)
                            if kind == "k":
                                v.wait_ge(tD.h, tD0 + gi)
                        v.tensor_scalar(rowF[0:1, 0:W],
                                        PS1[0:1, 1024:1024 + W],
                                        1.0 / 2048.0, EPS_RMS, op0=ALU.mult, op1=ALU.add)
                        v.tensor_copy(spcV[:], spcV[:])
                        v.reciprocal(rowF[0:1, 512:512 + W], rowF[0:1, 0:W])
                        v.tensor_copy(spcV[:], spcV[:]).then_inc(vC.h, 1)
                        vC.n += 1
                    if kind == "q":
                        for h in range(H):
                            j = gi * 16 + h
                            v.wait_ge(tD.h, tD0 + j + 1)
                            if stu2 >= 2:
                                v.wait_ge(dS.h, STORE_DONE[stu2 - 2])
                            v.tensor_copy(ev[h % 2][:, 0:W], PS1[:, 1536:1536 + W]).then_inc(vB.h, 1)
                            vB.n += 1
                            stu2 += 1
                    elif kind == "k":
                        v.wait_ge(tD.h, tD0 + gi + 1)
                        v.tensor_copy(rkb[:, 0:W], PS0[:, 0:W])
                        v.tensor_copy(spcV[:], spcV[:])
                        for o in range(16):
                            j = gi * 16 + o
                            if stu2 >= 2:
                                v.wait_ge(dS.h, STORE_DONE[stu2 - 2])
                            v.scalar_tensor_tensor(ev[o % 2][:, 0:W], qp[:, o * 512:o * 512 + W],
                                                   1.0, rkb[:, 0:W],
                                                   op0=ALU.mult, op1=ALU.mult).then_inc(vB.h, 1)
                            vB.n += 1
                            stu2 += 1
                    else:
                        nm = W // 128
                        for m in range(nm):
                            v.wait_ge(tB.h, tB0 + stu2 + 1)
                            if stu2 >= 2:
                                v.wait_ge(dS.h, STORE_DONE[stu2 - 2])
                            v.tensor_copy(vev[m % 2][:, 0:2048], PS1[:, 0:2048]).then_inc(vB.h, 1)
                            vB.n += 1
                            stu2 += 1
            barrier()

        def p1b1():
            dG0, dS0 = dG.n, dS.n
            tB0, tC0, tD0 = tB.n, tC.n, tD.n
            vA0, vB0, vC0 = vA.n, vB.n, vC.n
            sA0, sB0 = sA.n, sB.n
            kr = A[:, 0:24576]
            kr12 = B[:, 12544:14592]
            vr = A[:, 24576:49152]

            def kslot(ec):
                sl = ec % 13
                return kr12 if sl == 12 else kr[:, sl * 2048:(sl + 1) * 2048]
            qsb = (B[:, 0:2048], B[:, 2048:4096])
            aosb = (B[:, 4096:6144], B[:, 6144:8192])
            exps = (B[:, 8192:9216], B[:, 9216:10240])
            ptr = (B[:, 10240:11264], B[:, 11264:12288])
            diag = (B[:, 12288:12416], B[:, 12416:12544])
            nmx = (Fb[:, 0:1], Fb[:, 1:2])
            ebv = (Fb[:, 2:3], Fb[:, 3:4])
            zr = (Fb[:, 4:5], Fb[:, 5:6])
            rz = (Fb[:, 6:7], Fb[:, 7:8])
            mb = (Fb[:, 16:1040], Fb[:, 1040:2064])

            def new_ecs(lc):
                return list(range(12)) if lc == 0 else [lc + 11]

            LOADS_DONE = {}
            AOST_DONE = {}
            VR_DONE = {}
            cnt = dS.n
            cntg = dG.n
            for lc in range(LCH):
                cnt += 16 * (16 * len(new_ecs(lc)) + 16)
                LOADS_DONE[lc] = cnt
                cntg += 16 * len(new_ecs(lc))
                VR_DONE[lc] = cntg
                if lc >= 1:
                    cnt += 16 * 16
                    AOST_DONE[lc - 1] = cnt
            cnt += 16 * 16
            AOST_DONE[LCH - 1] = cnt

            @sect("gpsimd")
            def _(g):
                for lc in range(LCH):
                    if lc >= 1:
                        g.wait_ge(tD.h, tD0 + 16 * lc)
                    for ec in new_ecs(lc):
                        slot = ec % 12
                        g.dma_start(out=vr[:, slot * 2048:(slot + 1) * 2048],
                                    in_=vtD[ec * 128:(ec + 1) * 128, :]).then_inc(dG.h, 16)
                        dG.n += 16
                if DEBUG:
                    g.dma_start(out=rqDbg, in_=rqA[0:1, 0:2048]).then_inc(dG.h, 16)
                    dG.n += 16
                    g.wait_ge(vC.h, vC.n + 1)
                    g.dma_start(out=rqSDbg, in_=rqS[:]).then_inc(dG.h, 16)
                    dG.n += 16

            @sect("tensor")
            def _(t):
                for c in range(16):
                    t.matmul(PS1[:, 1024 + c:1024 + c + 1], rqA[0:1, c * 128:(c + 1) * 128],
                             smOF[0:1, 0:1], start=True, stop=True)
                t.drain().then_inc(tA.h, 1); tA.n += 1

            @sect("vector")
            def _(v):
                v.wait_ge(tA.h, tA.n)
                v.tensor_copy(rqT[:, 0:16], PS1[:, 1024:1040])
                v.tensor_copy(spcV[:], spcV[:])
                v.tensor_scalar(rqS[:], rqT[:], SCALE, None, op0=ALU.mult)
                v.tensor_copy(spcV[:], spcV[:]).then_inc(vC.h, 1)
                vC.n += 1

            @sect("sync")
            def _(s):
                for lc in range(LCH):
                    if lc >= 2:
                        s.wait_ge(tB.h, tB0 + 16 * (lc - 1))
                    for ec in new_ecs(lc):
                        kb = kslot(ec)
                        for h in range(H):
                            s.dma_start(out=kb[:, h * 128:(h + 1) * 128],
                                        in_=kkD[h, :, ec * 128:(ec + 1) * 128]).then_inc(dS.h, 16)
                            dS.n += 16
                    for h in range(H):
                        s.dma_start(out=qsb[lc % 2][:, h * 128:(h + 1) * 128],
                                    in_=kqD[h, :, lc * 128:(lc + 1) * 128]).then_inc(dS.h, 16)
                        dS.n += 16
                    if lc >= 1:
                        for h in range(H):
                            s.wait_ge(vB.h, vB0 + 16 * (lc - 1) + h + 1)
                            s.dma_start(out=aoD[h, :, (lc - 1) * 128:lc * 128],
                                        in_=aosb[(lc - 1) % 2][:, h * 128:(h + 1) * 128]).then_inc(dS.h, 16)
                            dS.n += 16
                for h in range(H):
                    s.wait_ge(vB.h, vB0 + 16 * (LCH - 1) + h + 1)
                    s.dma_start(out=aoD[h, :, (LCH - 1) * 128:LCH * 128],
                                in_=aosb[(LCH - 1) % 2][:, h * 128:(h + 1) * 128]).then_inc(dS.h, 16)
                    dS.n += 16
                s.wait_ge(dS.h, dS.n)

            sC0 = sC.n

            @sect("tensor")
            def _(t):
                NJ = LCH * H

                def band(j):
                    lc, h = divmod(j, H)
                    ecs = [lc + 8 + db for db in DB]
                    if h == 0:
                        t.wait_ge(dS.h, LOADS_DONE[lc])
                    half = PS0[:, (h % 2) * 1024:(h % 2) * 1024 + 1024]
                    if j >= 2:
                        t.wait_ge(vA.h, vA0 + j - 1)
                    for bi, ec in enumerate(ecs):
                        kb = kslot(ec)
                        t.matmul(half[:, bi * 128:(bi + 1) * 128],
                                 qsb[lc % 2][:, h * 128:(h + 1) * 128],
                                 kb[:, h * 128:(h + 1) * 128],
                                 start=True, stop=True)
                    t.drain().then_inc(tB.h, 1); tB.n += 1

                band(0)
                for j in range(NJ):
                    lc, h = divmod(j, H)
                    ecs = [lc + 8 + db for db in DB]
                    if j + 1 < NJ:
                        band(j + 1)
                    t.wait_ge(sB.h, sB0 + j + 1)
                    if j >= 1:
                        t.wait_ge(sC.h, sC0 + j)
                    for bi in range(8):
                        t.matmul(PS1[:, bi * 128:(bi + 1) * 128],
                                 exps[j % 2][:, bi * 128:(bi + 1) * 128], diag[j % 2][:],
                                 start=True, stop=True)
                    t.drain().then_inc(tC.h, 1); tC.n += 1
                    t.wait_ge(sC.h, sC0 + j + 1)
                    if j >= 1:
                        t.wait_ge(vB.h, vB0 + j)
                    if h == 0:
                        t.wait_ge(dG.h, VR_DONE[lc])
                    for bi, ec in enumerate(ecs):
                        slot = ec % 12
                        t.matmul(PS1[:, 1024:1152],
                                 vr[:, slot * 2048 + h * 128:slot * 2048 + (h + 1) * 128],
                                 ptr[j % 2][:, bi * 128:(bi + 1) * 128],
                                 start=(bi == 0), stop=(bi == 7))
                    t.drain().then_inc(tD.h, 1); tD.n += 1

            @sect("vector")
            def _(v):
                def ao_evac(j):
                    lc, h = divmod(j, H)
                    v.wait_ge(tD.h, tD0 + j + 1)
                    if lc >= 2 and h == 0:
                        v.wait_ge(dS.h, AOST_DONE[lc - 2])
                    v.tensor_copy(aosb[lc % 2][:, h * 128:(h + 1) * 128],
                                  PS1[:, 1024:1152]).then_inc(vB.h, 1)
                    vB.n += 1
                for lc in range(LCH):
                    for h in range(H):
                        j = lc * 16 + h
                        v.wait_ge(tB.h, tB0 + j + 1)
                        if j >= 2:
                            v.wait_ge(sA.h, sA0 + j - 1)
                        half = PS0[:, (h % 2) * 1024:(h % 2) * 1024 + 1024]
                        v.scalar_tensor_tensor(mb[j % 2][:], half, 1.0, smM[:],
                                               op0=ALU.mult, op1=ALU.add)
                        v.tensor_reduce(nmx[j % 2][:, 0:1], mb[j % 2][:], axis=AX.X,
                                        op=ALU.max, negate=True)
                        v.tensor_copy(spcV[:], spcV[:])
                        v.tensor_tensor(ebv[j % 2][:, 0:1], nmx[j % 2][:, 0:1],
                                        rqS[:, lc:lc + 1], op=ALU.mult)
                        v.tensor_copy(spcV[:], spcV[:]).then_inc(vA.h, 1)
                        vA.n += 1
                        v.wait_ge(sA.h, sA0 + j + 1)
                        v.tensor_reduce(zr[j % 2][:, 0:1], exps[j % 2][:], axis=AX.X, op=ALU.add)
                        v.tensor_copy(spcV[:], spcV[:])
                        v.reciprocal(rz[j % 2][:, 0:1], zr[j % 2][:, 0:1])
                        v.tensor_copy(spcV[:], spcV[:]).then_inc(vC.h, 1)
                        vC.n += 1
                        if j >= 1:
                            ao_evac(j - 1)
                ao_evac(LCH * H - 1)

            @sect("scalar")
            def _(s):
                def ptr_evac(j):
                    s.wait_ge(tC.h, tC0 + j + 1)
                    if j >= 1:
                        s.wait_ge(tD.h, tD0 + j - 1)
                    s.activation(ptr[j % 2][:], PS1[:, 0:1024], AF.Copy)
                    s.copy(spcS[:], spcS[:]).then_inc(sC.h, 1); sC.n += 1
                for lc in range(LCH):
                    for h in range(H):
                        j = lc * 16 + h
                        s.wait_ge(vA.h, vA0 + j + 1)
                        if j >= 2:
                            s.wait_ge(tC.h, tC0 + j - 1)
                        s.activation(exps[j % 2][:], mb[j % 2][:], AF.Exp, bias=ebv[j % 2][:, 0:1],
                                     scale=rqS[:, lc:lc + 1])
                        s.copy(spcS[:], spcS[:]).then_inc(sA.h, 1); sA.n += 1
                        s.wait_ge(vC.h, vC0 + 1 + j + 1)
                        s.activation(diag[j % 2][:], smI[:], AF.Copy, scale=rz[j % 2][:, 0:1])
                        s.copy(spcS[:], spcS[:]).then_inc(sB.h, 1); sB.n += 1
                        if j >= 1:
                            ptr_evac(j - 1)
                ptr_evac(LCH * H - 1)
            barrier()

        def p1b2():
            dG0, dS0, dC0 = dG.n, dS.n, dC.n
            tB0, vA0, vB0, vC0, sA0 = tB.n, vA.n, vB.n, vC.n, sA.n
            wA = A[:, 0:32768]
            aold = (B[:, 0:2048], B[:, 2048:4096])
            dump = B[:, 4096:6144]
            xr = (Fb[:, 0:2048], Fb[:, 2048:4096])
            xsum = Fb[:, 4096:6144]
            s1c = stats[:, 0:1]
            s2c = stats[:, 1:2]
            mu = stats[:, 2:3]
            iv = stats[:, 3:4]
            rstd = stats[:, 4:5]
            nmr = stats[:, 5:6]

            @sect("gpsimd")
            def _(g):
                for i in range(16):
                    g.dma_start(out=wA[:, i * 2048:(i + 1) * 2048],
                                in_=woT[i * 128:(i + 1) * 128, :]).then_inc(dG.h, 16)
                    dG.n += 16

            @sect("sync")
            def _(s):
                for lc in range(LCH):
                    if lc >= 2:
                        s.wait_ge(tB.h, tB0 + lc - 1)
                    for h in range(H):
                        s.dma_start(out=aold[lc % 2][:, h * 128:(h + 1) * 128],
                                    in_=aoD[h, :, lc * 128:(lc + 1) * 128]).then_inc(dS.h, 16)
                        dS.n += 16
                    if lc >= 2:
                        s.wait_ge(vA.h, vA0 + lc - 1)
                    s.dma_start(out=xr[lc % 2][:],
                                in_=xe[OFF + lc * 128:OFF + (lc + 1) * 128, :]).then_inc(dS.h, 16)
                    dS.n += 16
                s.wait_ge(dS.h, dS.n)

            @sect("tensor")
            def _(t):
                t.wait_ge(dG.h, dG0 + 256)
                for lc in range(LCH):
                    t.wait_ge(dS.h, dS0 + 272 * lc + 256)
                    PSx = PS0 if lc % 2 == 0 else PS1
                    if lc >= 2:
                        t.wait_ge(vA.h, vA0 + lc - 1)
                    for hh in range(16):
                        for og in range(4):
                            t.matmul(PSx[:, og * 512:(og + 1) * 512],
                                     aold[lc % 2][:, hh * 128:(hh + 1) * 128],
                                     wA[:, hh * 2048 + og * 512:hh * 2048 + (og + 1) * 512],
                                     start=(hh == 0), stop=False)
                    for og in range(4):
                        t.matmul(PSx[:, og * 512:(og + 1) * 512], smO[0:1, 0:128],
                                 rbo[0:1, og * 512:(og + 1) * 512], start=False, stop=True)
                    t.drain().then_inc(tB.h, 1); tB.n += 1

            @sect("vector")
            def _(v):
                for lc in range(LCH):
                    v.wait_ge(tB.h, tB0 + lc + 1)
                    v.wait_ge(dS.h, dS0 + 272 * (lc + 1))
                    if lc >= 1:
                        v.wait_ge(dC.h, dC0 + 16 * lc)
                    PSx = PS0 if lc % 2 == 0 else PS1
                    v.tensor_tensor(xsum[:], PSx[:, 0:2048], xr[lc % 2][:], op=ALU.add)
                    v.drain().then_inc(vA.h, 1); vA.n += 1
                    v.tensor_reduce(s1c, xsum[:], axis=AX.X, op=ALU.add)
                    v.wait_ge(sA.h, sA0 + 2 * lc + 1)
                    v.tensor_reduce(s2c, dump[:], axis=AX.X, op=ALU.add)
                    v.tensor_copy(spcV[:], spcV[:])
                    v.tensor_scalar(mu, s1c, -1.0 / 2048.0, None, op0=ALU.mult)
                    v.tensor_scalar(iv, s2c, 1.0 / 2048.0, EPS_LN, op0=ALU.mult, op1=ALU.add)
                    v.tensor_tensor(nmr, mu, mu, op=ALU.mult)
                    v.tensor_copy(spcV[:], spcV[:])
                    v.tensor_tensor(iv, iv, nmr, op=ALU.subtract)
                    v.tensor_copy(spcV[:], spcV[:])
                    v.reciprocal(iv, iv)
                    v.tensor_copy(spcV[:], spcV[:]).then_inc(vC.h, 1); vC.n += 1
                    v.wait_ge(sA.h, sA0 + 2 * lc + 2)
                    v.tensor_scalar(xsum[:], xsum[:], rstd[:, 0:1], nmr[:, 0:1],
                                    op0=ALU.mult, op1=ALU.add).then_inc(vB.h, 1)
                    vB.n += 1

            @sect("scalar")
            def _(s):
                for lc in range(LCH):
                    s.wait_ge(vA.h, vA0 + lc + 1)
                    s.activation(dump[:], xsum[:], AF.Square)
                    s.copy(spcS[:], spcS[:]).then_inc(sA.h, 1)
                    sA.n += 1
                    s.wait_ge(vC.h, vC0 + lc + 1)
                    s.activation(rstd, iv, AF.Sqrt)
                    s.copy(spcS[:], spcS[:])
                    s.mul(nmr, rstd, mu[:, 0:1])
                    s.copy(spcS[:], spcS[:]).then_inc(sA.h, 1); sA.n += 1
                    s.wait_ge(vB.h, vB0 + lc + 1)
                    s.dma_start(out=x1D[lc * 128:(lc + 1) * 128, :], in_=xsum[:]).then_inc(dC.h, 16)
                    dC.n += 16
                s.wait_ge(dC.h, dC.n)
            barrier()

        def p2a():
            dG0, dS0 = dG.n, dS.n
            tA0, tB0, vA0, vB0, vC0, sA0 = tA.n, tB.n, vA.n, vB.n, vC.n, sA.n
            x1fm = A[:, 0:32768]
            wgr = (A[:, 32768:34816], A[:, 34816:36864])
            wur = (A[:, 36864:38912], A[:, 38912:40960])
            wgc = (A[:, 40960:43008], A[:, 43008:45056])
            wuc = (A[:, 45056:47104], A[:, 47104:49152])
            xtm = (B[:, 0:2048], B[:, 2048:4096], B[:, 4096:6144])
            gsb = B[:, 6144:6656]
            usb = B[:, 6656:7168]
            hsb = (B[:, 7168:7680], B[:, 7680:8192])

            @sect("gpsimd")
            def _(g):
                for lc in range(LCH):
                    if lc >= 3:
                        g.wait_ge(tA.h, tA0 + lc - 2)
                    g.dma_start(out=xtm[lc % 3][:], in_=x1D[lc * 128:(lc + 1) * 128, :]).then_inc(dG.h, 16)
                    dG.n += 16
                for o in range(NHC):
                    if o >= 2:
                        g.wait_ge(vA.h, vA0 + 2 * (o - 1))
                    g.dma_start(out=wgr[o % 2][:],
                                in_=gTb[o].rearrange("(i p) c -> p i c", p=128)).then_inc(dG.h, 16)
                    dG.n += 16
                    g.dma_start(out=wur[o % 2][:],
                                in_=uTb[o].rearrange("(i p) c -> p i c", p=128)).then_inc(dG.h, 16)
                    dG.n += 16
                g.wait_ge(tA.h, tA0 + LCH)
                for _ in range(2):
                    g.dma_start(out=xtm[0][:, 0:512], in_=x1D[0:128, 0:512]).then_inc(dG.h, 16)
                    dG.n += 16

            @sect("tensor")
            def _(t):
                for lc in range(LCH):
                    t.wait_ge(dG.h, dG0 + 16 * min(lc + 2, LCH + 2))
                    if lc >= 1:
                        t.wait_ge(vB.h, vB0 + lc)
                    for i in range(16):
                        t.matmul(PS1[:, i * 128:(i + 1) * 128], xtm[lc % 3][:, i * 128:(i + 1) * 128],
                                 smI[:], start=True, stop=True)
                    t.drain().then_inc(tA.h, 1); tA.n += 1
                for o in range(NHC):
                    t.wait_ge(vA.h, vA0 + 2 * (o + 1))
                    if o == 0:
                        t.wait_ge(vB.h, vB0 + LCH)
                    for gi in range(4):
                        j = o * 4 + gi
                        bb = (gi % 2) * 1024
                        if j >= 2:
                            t.wait_ge(sA.h, sA0 + j - 1)
                            t.wait_ge(vC.h, vC0 + j - 1)
                        for i in range(16):
                            t.matmul(PS0[:, bb:bb + 512],
                                     wgc[o % 2][:, i * 128:(i + 1) * 128],
                                     x1fm[:, i * 2048 + gi * 512:i * 2048 + (gi + 1) * 512],
                                     start=(i == 0), stop=(i == 15))
                        for i in range(16):
                            t.matmul(PS0[:, bb + 512:bb + 1024],
                                     wuc[o % 2][:, i * 128:(i + 1) * 128],
                                     x1fm[:, i * 2048 + gi * 512:i * 2048 + (gi + 1) * 512],
                                     start=(i == 0), stop=(i == 15))
                        t.drain().then_inc(tB.h, 1); tB.n += 1

            @sect("vector")
            def _(v):
                for lc in range(LCH):
                    v.wait_ge(tA.h, tA0 + lc + 1)
                    for i in range(16):
                        v.tensor_copy(x1fm[:, i * 2048 + lc * 128:i * 2048 + (lc + 1) * 128],
                                      PS1[:, i * 128:(i + 1) * 128])
                    v.drain().then_inc(vB.h, 1); vB.n += 1
                for o in range(NHC):
                    v.wait_ge(dG.h, dG0 + 16 * LCH + 32 * min(o + 2, NHC + 1))
                    if o >= 2:
                        v.wait_ge(tB.h, tB0 + 4 * (o - 1))
                    v.tensor_scalar(wgc[o % 2][:], wgr[o % 2][:], -128.0, None, op0=ALU.add)
                    v.tensor_scalar(wuc[o % 2][:], wur[o % 2][:], -128.0, None, op0=ALU.add)
                    v.tensor_copy(spcV[:], spcV[:]).then_inc(vA.h, 2)
                    vA.n += 2
                    for gi in range(4):
                        j = o * 4 + gi
                        v.wait_ge(tB.h, tB0 + j + 1)
                        v.tensor_scalar(usb[:, 0:512], PS0[:, (gi % 2) * 1024 + 512:(gi % 2) * 1024 + 1024],
                                        suc[:, o:o + 1], None, op0=ALU.mult)
                        v.wait_ge(sA.h, sA0 + j + 1)
                        if j >= 2:
                            v.wait_ge(dS.h, dS0 + 16 * (j - 1))
                        v.tensor_tensor(hsb[j % 2][:, 0:512], gsb[:, 0:512],
                                        usb[:, 0:512], op=ALU.mult).then_inc(vC.h, 1)
                        vC.n += 1

            @sect("scalar")
            def _(s):
                for o in range(NHC):
                    for gi in range(4):
                        j = o * 4 + gi
                        s.wait_ge(tB.h, tB0 + j + 1)
                        if j >= 1:
                            s.wait_ge(vC.h, vC0 + j)
                        s.activation(gsb[:, 0:512], PS0[:, (gi % 2) * 1024:(gi % 2) * 1024 + 512],
                                     AF.Silu, scale=sgc[:, o:o + 1]).then_inc(sA.h, 1)
                        sA.n += 1

            @sect("sync")
            def _(s):
                for o in range(NHC):
                    for gi in range(4):
                        j = o * 4 + gi
                        s.wait_ge(vC.h, vC0 + j + 1)
                        s.dma_start(out=hD[o, :, gi * 512:(gi + 1) * 512],
                                    in_=hsb[j % 2][:, 0:512]).then_inc(dS.h, 16)
                        dS.n += 16
                s.wait_ge(dS.h, dS.n)
            barrier()

        def p2b():
            dG0, dS0, dC0 = dG.n, dS.n, dC.n
            tA0, tB0, vA0, vB0 = tA.n, tB.n, vA.n, vB.n
            dTq = A[:, 0:22528]
            hring = A[:, 22528:26624]
            wdr = tuple(B[:, i * 512:(i + 1) * 512] for i in range(4))
            wdum = B[:, 2048:2560]
            fev = Fb[:, 0:2048]

            @sect("gpsimd")
            def _(g):
                for qd in range(4):
                    for o in range(NHC):
                        j = qd * NHC + o
                        if j >= 4:
                            g.wait_ge(vA.h, vA0 + j - 3)
                        g.dma_start(out=wdr[j % 4][:],
                                    in_=dT[o * 128:(o + 1) * 128, qd * 512:(qd + 1) * 512]).then_inc(dG.h, 16)
                        dG.n += 16
                for _ in range(2):
                    g.dma_start(out=wdum[:], in_=dT[0:128, 0:512]).then_inc(dG.h, 16)
                    dG.n += 16

            @sect("vector")
            def _(v):
                for qd in range(4):
                    for o in range(NHC):
                        j = qd * NHC + o
                        v.wait_ge(dG.h, dG0 + 16 * min(j + 3, 4 * NHC + 2))
                        if qd >= 1 and o == 0:
                            v.wait_ge(tA.h, tA0 + 4 * qd)
                        v.scalar_tensor_tensor(dTq[:, o * 512:(o + 1) * 512], wdr[j % 4][:],
                                               -128.0, sdb[:, qd * 512:(qd + 1) * 512],
                                               op0=ALU.add, op1=ALU.mult)
                        if o == NHC - 1:
                            v.tensor_copy(spcV[:], spcV[:])
                            v.tensor_copy(spcV[:], spcV[:])
                            v.tensor_copy(spcV[:], spcV[:])
                        v.tensor_copy(spcV[:], spcV[:]).then_inc(vA.h, 1)
                        vA.n += 1
                    for gi in range(4):
                        k = qd * 4 + gi
                        v.wait_ge(tA.h, tA0 + k + 1)
                        if k >= 1:
                            v.wait_ge(dC.h, dC0 + 64 * k)
                        PSx = PS0 if gi % 2 == 0 else PS1
                        for lci in range(4):
                            v.tensor_copy(fev[:, lci * 512:(lci + 1) * 512],
                                          PSx[:, lci * 512:(lci + 1) * 512])
                        v.drain().then_inc(vB.h, 1); vB.n += 1

            @sect("sync")
            def _(s):
                nh = 0
                for qd in range(4):
                    for gi in range(4):
                        for o in range(NHC):
                            nh += 1
                            if nh > 8:
                                s.wait_ge(tB.h, tB0 + nh - 8)
                            s.dma_start(out=hring[:, ((nh - 1) % 8) * 512:((nh - 1) % 8 + 1) * 512],
                                        in_=hD[o, :, gi * 512:(gi + 1) * 512]).then_inc(dS.h, 16)
                            dS.n += 16

            @sect("tensor")
            def _(t):
                nh = 0
                for qd in range(4):
                    for gi in range(4):
                        k = qd * 4 + gi
                        PSx = PS0 if gi % 2 == 0 else PS1
                        if k >= 2:
                            t.wait_ge(vB.h, vB0 + k - 1)
                        for o in range(NHC):
                            nh += 1
                            t.wait_ge(dS.h, dS0 + 16 * nh)
                            t.wait_ge(vA.h, vA0 + min(qd * NHC + o + 2, (qd + 1) * NHC))
                            slot = (nh - 1) % 8
                            for lci in range(4):
                                t.matmul(PSx[:, lci * 512:(lci + 1) * 512],
                                         hring[:, slot * 512 + lci * 128:slot * 512 + (lci + 1) * 128],
                                         dTq[:, o * 512:(o + 1) * 512],
                                         start=(o == 0), stop=(o == NHC - 1))
                            t.drain().then_inc(tB.h, 1); tB.n += 1
                        t.drain().then_inc(tA.h, 1); tA.n += 1

            @sect("scalar")
            def _(s):
                for qd in range(4):
                    for gi in range(4):
                        k = qd * 4 + gi
                        s.wait_ge(vB.h, vB0 + k + 1)
                        for lci in range(4):
                            r0 = (gi * 4 + lci) * 128
                            s.dma_start(out=fD[r0:r0 + 128, qd * 512:(qd + 1) * 512],
                                        in_=fev[:, lci * 512:(lci + 1) * 512]).then_inc(dC.h, 16)
                            dC.n += 16
                s.wait_ge(dC.h, dC.n)
            barrier()

        def p2c():
            dS0, dC0 = dS.n, dC.n
            vA0, vB0, vC0, sA0 = vA.n, vB.n, vC.n, sA.n
            fsb = (Fb[:, 0:2048], Fb[:, 2048:4096])
            xsum = Fb[:, 4096:6144]
            dump = B[:, 0:2048]
            s1c = stats[:, 0:1]
            s2c = stats[:, 1:2]
            mu = stats[:, 2:3]
            iv = stats[:, 3:4]
            rstd = stats[:, 4:5]
            nmr = stats[:, 5:6]

            @sect("sync")
            def _(s):
                for lc in range(LCH):
                    if lc >= 1:
                        s.wait_ge(dC.h, dC0 + 16 * lc)
                    s.dma_start(out=xsum[:], in_=x1D[lc * 128:(lc + 1) * 128, :]).then_inc(dS.h, 16)
                    dS.n += 16
                    if lc >= 2:
                        s.wait_ge(vC.h, vC0 + lc - 1)
                    s.dma_start(out=fsb[lc % 2][:], in_=fD[lc * 128:(lc + 1) * 128, :]).then_inc(dS.h, 16)
                    dS.n += 16

            @sect("vector")
            def _(v):
                for lc in range(LCH):
                    v.wait_ge(dS.h, dS0 + 32 * (lc + 1))
                    v.tensor_tensor(xsum[:], xsum[:], fsb[lc % 2][:], op=ALU.add)
                    v.drain().then_inc(vC.h, 1); vC.n += 1
                    v.tensor_reduce(s1c, xsum[:], axis=AX.X, op=ALU.add)
                    v.wait_ge(sA.h, sA0 + 2 * lc + 1)
                    v.tensor_reduce(s2c, dump[:], axis=AX.X, op=ALU.add)
                    v.tensor_copy(spcV[:], spcV[:])
                    v.tensor_scalar(mu, s1c, -1.0 / 2048.0, None, op0=ALU.mult)
                    v.tensor_scalar(iv, s2c, 1.0 / 2048.0, EPS_LN, op0=ALU.mult, op1=ALU.add)
                    v.tensor_tensor(nmr, mu, mu, op=ALU.mult)
                    v.tensor_copy(spcV[:], spcV[:])
                    v.tensor_tensor(iv, iv, nmr, op=ALU.subtract)
                    v.tensor_copy(spcV[:], spcV[:])
                    v.reciprocal(iv, iv)
                    v.tensor_copy(spcV[:], spcV[:]).then_inc(vB.h, 1); vB.n += 1
                    v.wait_ge(sA.h, sA0 + 2 * lc + 2)
                    v.tensor_scalar(xsum[:], xsum[:], rstd[:, 0:1], nmr[:, 0:1],
                                    op0=ALU.mult, op1=ALU.add).then_inc(vA.h, 1)
                    vA.n += 1

            @sect("scalar")
            def _(s):
                for lc in range(LCH):
                    s.wait_ge(vC.h, vC0 + lc + 1)
                    s.activation(dump[:], xsum[:], AF.Square)
                    s.copy(spcS[:], spcS[:]).then_inc(sA.h, 1)
                    sA.n += 1
                    s.wait_ge(vB.h, vB0 + lc + 1)
                    s.activation(rstd, iv, AF.Sqrt)
                    s.copy(spcS[:], spcS[:])
                    s.mul(nmr, rstd, mu[:, 0:1])
                    s.copy(spcS[:], spcS[:]).then_inc(sA.h, 1); sA.n += 1
                    s.wait_ge(vA.h, vA0 + lc + 1)
                    s.dma_start(out=y[lc * 128:(lc + 1) * 128, :], in_=xsum[:]).then_inc(dC.h, 16)
                    dC.n += 16
                s.wait_ge(dC.h, dC.n)
            barrier()

        p0()
        p1x()
        proj("q")
        proj("k")
        proj("v")
        p1b1()
        p1b2()
        p2a()
        p2b()
        p2c()
    return nc


def _prep(inputs):
    x = np.asarray(inputs["L0_attn_x"], dtype=np.float32)
    shared = {
        "wqT": np.ascontiguousarray(np.asarray(inputs["L0_attn_wq"], np.float32).T),
        "wkT": np.ascontiguousarray(np.asarray(inputs["L0_attn_wk"], np.float32).T),
        "wvT": np.ascontiguousarray(np.asarray(inputs["L0_attn_wv"], np.float32).T),
        "woT": np.ascontiguousarray(np.asarray(inputs["L0_attn_wo"], np.float32).T),
        "gTb": np.ascontiguousarray(
            np.asarray(inputs["L0_ffn_wq_gate"], np.int32).reshape(NHC, 128, DM).transpose(0, 2, 1)),
        "uTb": np.ascontiguousarray(
            np.asarray(inputs["L0_ffn_wq_up"], np.int32).reshape(NHC, 128, DM).transpose(0, 2, 1)),
        "dT": np.ascontiguousarray(np.asarray(inputs["L0_ffn_wq_down"], np.int32).T),
        "r_bq": np.asarray(inputs["L0_attn_bq"], np.float32).reshape(1, DM).copy(),
        "r_bk": np.asarray(inputs["L0_attn_bk"], np.float32).reshape(1, DM).copy(),
        "r_bv": np.asarray(inputs["L0_attn_bv"], np.float32).reshape(1, DM).copy(),
        "r_bo": np.asarray(inputs["L0_attn_bo"], np.float32).reshape(1, DM).copy(),
        "r_gq": np.asarray(inputs["L0_attn_gq"], np.float32).reshape(1, DM).copy(),
        "r_gk": np.asarray(inputs["L0_attn_gk"], np.float32).reshape(1, DM).copy(),
        "r_sg": np.asarray(inputs["L0_ffn_s_gate"], np.float32).reshape(1, HID).copy(),
        "r_su": np.asarray(inputs["L0_ffn_s_up"], np.float32).reshape(1, HID).copy(),
        "r_sd": np.asarray(inputs["L0_ffn_s_down"], np.float32).reshape(1, DM).copy(),
        "wmix": np.asarray(inputs["L0_attn_w_mix"], np.float32).reshape(1, 5).copy(),
        "cI": np.eye(128, dtype=np.float32).astype(bfd),
        "cMask": _mask01(),
        "cPm": _pmats(),
        "cOnes": np.ones((128, 512), dtype=np.float32).astype(bfd),
        "cOnesF": np.ones((1, 128), dtype=np.float32),
    }
    in_maps = []
    for c in range(NCORE):
        b, half = divmod(c, 2)
        p0_ = half * 2048
        idx = (p0_ - OFF + np.arange(EXT)) % 4096
        m = dict(shared)
        m["xe"] = np.ascontiguousarray(x[b, idx, :])
        in_maps.append(m)
    return in_maps


def kernel(**inputs):
    in_maps = _prep(inputs)
    nc = build_nc()
    res = run_bass_kernel_spmd(nc, in_maps, core_ids=list(range(NCORE)))
    out = np.empty((4, 4096, DM), dtype=np.float32)
    for c in range(NCORE):
        b, half = divmod(c, 2)
        out[b, half * 2048:(half + 1) * 2048, :] = res.results[c]["y"]
    return out


# version 03
# speedup vs baseline: 1.1157x; 1.0476x over previous
import numpy as np
import ml_dtypes
import concourse.bass as bass
import concourse.mybir as mybir
from concourse.bass_utils import run_bass_kernel_spmd

F32 = mybir.dt.float32
BF16 = mybir.dt.bfloat16
I32 = mybir.dt.int32
AF = mybir.ActivationFunctionType
ALU = mybir.AluOpType
AX = mybir.AxisListType
bfd = ml_dtypes.bfloat16

SHIFTS = [0, 1, -1, 3, -3, 7, -7, 20, -20, 53, -53, 141, -141, 380, -380, 1024]
CH = [0, 1, 2, 4, 8]
DB = [-8, -3, -2, -1, 0, 1, 2, 3]
H = 16
DM = 2048
HID = 5632
NHC = 44
TOK = 2048
EXT = 3456
ECH = 27
LCH = 16
OFF = 1024
EPS_RMS = 1e-6
EPS_LN = 1e-5
SCALE = 128.0 ** -0.5
NCORE = 8
DEBUG = False

EGRP = [(0, 512), (512, 512), (1024, 512), (1536, 512), (2048, 512), (2560, 512), (3072, 384)]
QGRP = [(0, 512), (512, 512), (1024, 512), (1536, 512)]


def _mask01():
    m = np.full((128, 8 * 128), -1e30, dtype=np.float32)
    ss = set(SHIFTS)
    for p in range(128):
        for bi, db in enumerate(DB):
            for q in range(128):
                if (p - q - db * 128) in ss:
                    m[p, bi * 128 + q] = 0.0
    assert ((m == 0).sum(1) == 16).all()
    return m.astype(bfd)


def _pmats():
    P = np.zeros((128, 5 * 128), dtype=np.float32)
    for ci, c in enumerate(CH):
        P[(np.arange(128) - c) % 128, ci * 128 + np.arange(128)] = 1.0
    return P.astype(bfd)


class Ct:
    def __init__(self, h):
        self.h = h
        self.n = 0


def build_nc():
    nc = bass.Bass("TRN2", target_bir_lowering=False)

    xe = nc.dram_tensor("xe", [EXT, DM], F32, kind="ExternalInput").ap()
    wqT = nc.dram_tensor("wqT", [DM, DM], F32, kind="ExternalInput").ap()
    wkT = nc.dram_tensor("wkT", [DM, DM], F32, kind="ExternalInput").ap()
    wvT = nc.dram_tensor("wvT", [DM, DM], F32, kind="ExternalInput").ap()
    woT = nc.dram_tensor("woT", [DM, DM], F32, kind="ExternalInput").ap()
    gTb = nc.dram_tensor("gTb", [NHC, DM, 128], I32, kind="ExternalInput").ap()
    uTb = nc.dram_tensor("uTb", [NHC, DM, 128], I32, kind="ExternalInput").ap()
    dT = nc.dram_tensor("dT", [HID, DM], I32, kind="ExternalInput").ap()
    r_bq = nc.dram_tensor("r_bq", [1, DM], F32, kind="ExternalInput").ap()
    r_bk = nc.dram_tensor("r_bk", [1, DM], F32, kind="ExternalInput").ap()
    r_bv = nc.dram_tensor("r_bv", [1, DM], F32, kind="ExternalInput").ap()
    r_bo = nc.dram_tensor("r_bo", [1, DM], F32, kind="ExternalInput").ap()
    r_gq = nc.dram_tensor("r_gq", [1, DM], F32, kind="ExternalInput").ap()
    r_gk = nc.dram_tensor("r_gk", [1, DM], F32, kind="ExternalInput").ap()
    r_sg = nc.dram_tensor("r_sg", [1, HID], F32, kind="ExternalInput").ap()
    r_su = nc.dram_tensor("r_su", [1, HID], F32, kind="ExternalInput").ap()
    r_sd = nc.dram_tensor("r_sd", [1, DM], F32, kind="ExternalInput").ap()
    wmix = nc.dram_tensor("wmix", [1, 5], F32, kind="ExternalInput").ap()
    cI = nc.dram_tensor("cI", [128, 128], BF16, kind="ExternalInput").ap()
    cMask = nc.dram_tensor("cMask", [128, 1024], BF16, kind="ExternalInput").ap()
    cPm = nc.dram_tensor("cPm", [128, 640], BF16, kind="ExternalInput").ap()
    cOnes = nc.dram_tensor("cOnes", [128, 512], BF16, kind="ExternalInput").ap()
    cOnesF = nc.dram_tensor("cOnesF", [1, 128], F32, kind="ExternalInput").ap()

    sk = "ExternalOutput" if DEBUG else "Internal"
    xfmD = nc.dram_tensor("xfmD", [128, 16, EXT], BF16, kind=sk).ap()
    kqD = nc.dram_tensor("kqD", [128, H, TOK], BF16, kind=sk).ap()
    kkD = nc.dram_tensor("kkD", [128, H, EXT], BF16, kind=sk).ap()
    vtD = nc.dram_tensor("vtD", [EXT, DM], BF16, kind=sk).ap()
    aoD = nc.dram_tensor("aoD", [128, H, TOK], BF16, kind=sk).ap()
    x1D = nc.dram_tensor("x1D", [TOK, DM], F32, kind=sk).ap()
    hD = nc.dram_tensor("hD", [NHC, 128, TOK], BF16, kind=sk).ap()
    fD = nc.dram_tensor("fD", [TOK, DM], F32, kind=sk).ap()
    rqDbg = nc.dram_tensor("rqDbg", [1, TOK], F32, kind="ExternalOutput").ap() if DEBUG else None
    rqSDbg = nc.dram_tensor("rqSDbg", [128, 16], F32, kind="ExternalOutput").ap() if DEBUG else None
    y = nc.dram_tensor("y", [TOK, DM], F32, kind="ExternalOutput").ap()

    import contextlib
    st = contextlib.ExitStack()

    def sb(name, shp, dt):
        return st.enter_context(nc.sbuf_tensor(name, shp, dt))

    def psum(name, shp, dt):
        return st.enter_context(nc.psum_tensor(name, shp, dt))

    def sem(name):
        return Ct(st.enter_context(nc.semaphore(name)))

    with st:
        A = sb("A", [128, 49152], BF16)
        B = sb("B", [128, 16384], BF16)
        Fb = sb("Fb", [128, 6144], F32)
        smI = sb("smI", [128, 128], BF16)
        smM = sb("smM", [128, 1024], BF16)
        smP = sb("smP", [128, 640], BF16)
        smMq = sb("smMq", [128, 128], BF16)
        smO = sb("smO", [128, 512], BF16)
        smOF = sb("smOF", [1, 128], F32)
        gqc = sb("gqc", [128, 16], F32)
        gkc = sb("gkc", [128, 16], F32)
        sgc = sb("sgc", [128, 44], F32)
        suc = sb("suc", [128, 44], F32)
        sdb = sb("sdb", [128, 2048], F32)
        wmb = sb("wmb", [128, 8], F32)
        rbq = sb("rbq", [1, 2048], BF16)
        rbk = sb("rbk", [1, 2048], BF16)
        rbv = sb("rbv", [1, 2048], BF16)
        rbo = sb("rbo", [1, 2048], BF16)
        rsd = sb("rsd", [1, 2048], F32)
        rqA = sb("rqA", [1, 2048], F32)
        rqT = sb("rqT", [128, 16], F32)
        rqS = sb("rqS", [128, 16], F32)
        rowF = sb("rowF", [1, 1024], F32)
        stats = sb("stats", [128, 8], F32)
        spcV = sb("spcV", [128, 1], F32)
        spcS = sb("spcS", [128, 1], F32)

        PS0 = psum("PS0", [128, 2048], F32)
        PS1 = psum("PS1", [128, 2048], F32)

        bar = sem("bar")
        dG = sem("dG")
        dS = sem("dS")
        dC = sem("dC")
        tA = sem("tA"); tB = sem("tB"); tC = sem("tC"); tD = sem("tD")
        vA = sem("vA"); vB = sem("vB"); vC = sem("vC")
        sA = sem("sA"); sB = sem("sB"); sC = sem("sC")
        gA = sem("gA")

        blk = st.enter_context(nc.Block())
        ENGS = ["tensor", "vector", "scalar", "gpsimd", "sync"]
        barn = [0]

        def barrier():
            barn[0] += 1
            tgt = 5 * barn[0]
            for e in ENGS:
                def _f(en, tgt=tgt):
                    en.sem_inc(bar.h, 1)
                    en.wait_ge(bar.h, tgt)
                getattr(blk, e)(_f)

        def sect(e):
            return getattr(blk, e)

        def p0():
            tB0_p0 = tB.n
            dG0_rows = [0]
            rowP0 = Fb[0:1, 0:5632]

            @sect("gpsimd")
            def _(g):
                g.dma_start(out=smOF[:], in_=cOnesF).then_inc(dG.h, 16); dG.n += 16
                for out, src in ((smI, cI), (smM, cMask), (smP, cPm), (smO, cOnes)):
                    g.dma_start(out=out[:], in_=src).then_inc(dG.h, 16); dG.n += 16
                g.dma_start(out=wmb[0:1, 0:5], in_=wmix).then_inc(dG.h, 16); dG.n += 16
                dG0_rows[0] = dG.n
                for ri, (src, n) in enumerate(((r_gq, 16), (r_gk, 16), (r_sg, 44), (r_su, 44))):
                    g.dma_start(out=rowP0[0:1, 0:n * 128], in_=src).then_inc(dG.h, 16); dG.n += 16
                    g.dma_start(out=rowP0[0:1, 0:n * 128], in_=src).then_inc(dG.h, 16); dG.n += 16
                    g.wait_ge(gA.h, gA.n + 1); gA.n += 1
                for out, src in ((rbq, r_bq), (rbk, r_bk), (rbv, r_bv), (rbo, r_bo)):
                    g.dma_start(out=out[:], in_=src).then_inc(dG.h, 16); dG.n += 16
                g.dma_start(out=rsd[:], in_=r_sd).then_inc(dG.h, 16); dG.n += 16

            @sect("tensor")
            def _(t):
                nd = dG0_rows[0]
                vA0_p0 = vA.n
                for ri, (dst, n) in enumerate(((gqc, 16), (gkc, 16), (sgc, 44), (suc, 44))):
                    t.wait_ge(dG.h, nd + 32 * (ri + 1))
                    if ri >= 1:
                        t.wait_ge(vA.h, vA0_p0 + ri)
                    for c in range(n):
                        t.matmul(PS0[:, c:c + 1], rowP0[0:1, c * 128:(c + 1) * 128],
                                 smOF[0:1, 0:1], start=True, stop=True)
                    t.drain().then_inc(tB.h, 1); tB.n += 1
                t.wait_ge(dG.h, dG.n)
                t.wait_ge(vA.h, vA0_p0 + 4)
                t.matmul(PS0[:, 0:5], smOF[0:1, 0:128], wmb[0:1, 0:5], start=True, stop=True)
                for j in range(4):
                    t.matmul(PS1[:, j * 512:(j + 1) * 512], smOF[0:1, 0:128],
                             rsd[0:1, j * 512:(j + 1) * 512], start=True, stop=True)
                t.drain().then_inc(tA.h, 1); tA.n += 1

            @sect("scalar")
            def _(sc):
                sc.wait_ge(tA.h, tA.n)
                sc.copy(wmb[:, 0:5], PS0[:, 0:5])
                sc.copy(spcS[:], spcS[:]).then_inc(sA.h, 1); sA.n += 1

            @sect("vector")
            def _(v):
                for ri, (dst, n) in enumerate(((gqc, 16), (gkc, 16), (sgc, 44), (suc, 44))):
                    v.wait_ge(tB.h, tB0_p0 + ri + 1)
                    v.tensor_copy(dst[:, 0:n], PS0[:, 0:n])
                    v.tensor_copy(spcV[:], spcV[:]).then_inc(vA.h, 1); vA.n += 1
                    v.tensor_copy(spcV[:], spcV[:]).then_inc(gA.h, 1)
                v.wait_ge(tA.h, tA.n)
                v.tensor_copy(sdb[:, :], PS1[:, 0:2048])
                v.wait_ge(sA.h, sA.n)
                v.tensor_scalar(smMq[:], smP[:, 0:128], wmb[:, 0:1], None, op0=ALU.mult)
                for ci in range(1, 5):
                    v.tensor_copy(spcV[:], spcV[:])
                    v.scalar_tensor_tensor(smMq[:], smP[:, ci * 128:(ci + 1) * 128],
                                           wmb[:, ci:ci + 1], smMq[:],
                                           op0=ALU.mult, op1=ALU.add)
                v.tensor_copy(spcV[:], spcV[:])
            barrier()

        def p1x():
            d0, t0, v0, s0 = dG.n, tA.n, vA.n, dS.n
            xtm = (A[:, 32768:34816], A[:, 34816:36864], A[:, 36864:38912])
            ev = (B[:, 0:2048], B[:, 2048:4096])

            @sect("gpsimd")
            def _(g):
                for ec in range(ECH):
                    if ec >= 3:
                        g.wait_ge(tA.h, t0 + ec - 2)
                    g.dma_start(out=xtm[ec % 3][:], in_=xe[ec * 128:(ec + 1) * 128, :]).then_inc(dG.h, 16)
                    dG.n += 16
                g.wait_ge(tA.h, t0 + ECH - 2)
                g.dma_start(out=A[:, 38912:39424], in_=xe[0:128, 0:512]).then_inc(dG.h, 16)
                dG.n += 16

            @sect("tensor")
            def _(t):
                for ec in range(ECH):
                    t.wait_ge(dG.h, d0 + 16 * min(ec + 2, ECH + 1))
                    if ec >= 1:
                        t.wait_ge(vA.h, v0 + ec)
                    for i in range(16):
                        t.matmul(PS0[:, i * 128:(i + 1) * 128], xtm[ec % 3][:, i * 128:(i + 1) * 128],
                                 smI[:], start=True, stop=True)
                    t.drain().then_inc(tA.h, 1); tA.n += 1

            @sect("vector")
            def _(v):
                for ec in range(ECH):
                    v.wait_ge(tA.h, t0 + ec + 1)
                    if ec >= 2:
                        v.wait_ge(dS.h, s0 + 16 * (ec - 1))
                    for i in range(16):
                        v.tensor_copy(ev[ec % 2][:, i * 128:(i + 1) * 128], PS0[:, i * 128:(i + 1) * 128])
                    v.drain().then_inc(vA.h, 1); vA.n += 1

            @sect("sync")
            def _(s):
                for ec in range(ECH):
                    s.wait_ge(vA.h, v0 + ec + 1)
                    s.dma_start(out=xfmD[:, :, ec * 128:(ec + 1) * 128],
                                in_=ev[ec % 2].rearrange("p (i t) -> p i t", i=16)).then_inc(dS.h, 16)
                    dS.n += 16
                s.wait_ge(dS.h, dS.n)
            barrier()

        def proj(kind):
            grps = QGRP if kind == "q" else EGRP
            NG = len(grps)
            wsrc = {"q": wqT, "k": wkT, "v": wvT}[kind]
            brow = {"q": rbq, "k": rbk, "v": rbv}[kind]
            outD = {"q": kqD, "k": kkD, "v": vtD}[kind]
            d0, tA0, tB0, tC0, tD0 = dG.n, tA.n, tB.n, tC.n, tD.n
            vB0, vC0 = vB.n, vC.n
            sA0, sB0, sC0 = sA.n, sB.n, sC.n
            gA0 = gA.n
            wA = A[:, 0:32768]
            xf = (A[:, 32768:40960], A[:, 40960:49152])
            qp = B[:, 0:8192]
            sq = (B[:, 8192:8704], B[:, 8704:9216])
            ev = (B[:, 9216:10240], B[:, 10240:11264])
            vev = (B[:, 8704:10752], B[:, 10752:12800])
            rkb = Fb[:, 0:512]

            nst = [(16 if kind != "v" else W // 128) for (c0, W) in grps]
            LOADS_DONE = {}
            STORE_DONE = {}
            cnt = dS.n
            stu = 0
            for gi in range(NG):
                cnt += 16 * 16
                LOADS_DONE[gi] = cnt
                if gi >= 1:
                    for _ in range(nst[gi - 1]):
                        cnt += 16
                        stu += 1
                        STORE_DONE[stu - 1] = cnt
            for _ in range(nst[NG - 1]):
                cnt += 16
                stu += 1
                STORE_DONE[stu - 1] = cnt

            @sect("gpsimd")
            def _(g):
                for i in range(16):
                    g.dma_start(out=wA[:, i * 2048:(i + 1) * 2048],
                                in_=wsrc[i * 128:(i + 1) * 128, :]).then_inc(dG.h, 16)
                    dG.n += 16

            @sect("sync")
            def _(s):
                stu2 = 0
                for gi, (c0, W) in enumerate(grps):
                    if gi >= 2:
                        s.wait_ge(tA.h, tA0 + gi - 1)
                    base = (OFF + c0) if kind == "q" else c0
                    for i in range(16):
                        s.dma_start(out=xf[gi % 2][:, i * 512:i * 512 + W],
                                    in_=xfmD[:, i, base:base + W]).then_inc(dS.h, 16)
                        dS.n += 16
                    if gi >= 1:
                        c0p, Wp = grps[gi - 1]
                        for u in range(nst[gi - 1]):
                            s.wait_ge(vB.h, vB0 + stu2 + 1)
                            if kind == "q":
                                s.dma_start(out=kqD[:, u, c0p:c0p + Wp], in_=ev[u % 2][:, 0:Wp]).then_inc(dS.h, 16)
                            elif kind == "k":
                                s.dma_start(out=kkD[:, u, c0p:c0p + Wp], in_=ev[u % 2][:, 0:Wp]).then_inc(dS.h, 16)
                            else:
                                mc = c0p + u * 128
                                s.dma_start(out=vtD[mc:mc + 128, :], in_=vev[u % 2][:, 0:2048]).then_inc(dS.h, 16)
                            dS.n += 16
                            stu2 += 1
                c0p, Wp = grps[NG - 1]
                for u in range(nst[NG - 1]):
                    s.wait_ge(vB.h, vB0 + stu2 + 1)
                    if kind == "q":
                        s.dma_start(out=kqD[:, u, c0p:c0p + Wp], in_=ev[u % 2][:, 0:Wp]).then_inc(dS.h, 16)
                    elif kind == "k":
                        s.dma_start(out=kkD[:, u, c0p:c0p + Wp], in_=ev[u % 2][:, 0:Wp]).then_inc(dS.h, 16)
                    else:
                        mc = c0p + u * 128
                        s.dma_start(out=vtD[mc:mc + 128, :], in_=vev[u % 2][:, 0:2048]).then_inc(dS.h, 16)
                    dS.n += 16
                    stu2 += 1
                s.wait_ge(dS.h, dS.n)

            @sect("tensor")
            def _(t):
                midx = 0
                for gi, (c0, W) in enumerate(grps):
                    t.wait_ge(dS.h, LOADS_DONE[gi])
                    if gi == 0:
                        t.wait_ge(dG.h, d0 + 256)
                    if kind in ("q", "k"):
                        if kind == "k" and gi >= 1:
                            t.wait_ge(vB.h, vB0 + 16 * gi)
                        def sumsq_mm(o):
                            t.wait_ge(sB.h, sB0 + gi * 16 + o + 1)
                            t.matmul(PS1[0:1, 1024:1024 + W], smO[:, 0:1], sq[o % 2][:, 0:W],
                                     start=(o == 0), stop=(o == 15)).then_inc(tC.h, 1)
                            tC.n += 1
                        for o in range(16):
                            bank = PS1[:, (o % 2) * 512:(o % 2) * 512 + W]
                            j = gi * 16 + o
                            if j >= 2:
                                t.wait_ge(sA.h, sA0 + j - 1)
                            for i in range(16):
                                t.matmul(bank, wA[:, i * 2048 + o * 128:i * 2048 + (o + 1) * 128],
                                         xf[gi % 2][:, i * 512:i * 512 + W],
                                         start=(i == 0), stop=False)
                            t.matmul(bank, brow[0:1, o * 128:(o + 1) * 128], smO[0:1, 0:W],
                                     start=False, stop=True).then_inc(tB.h, 1)
                            tB.n += 1
                            if o >= 1:
                                sumsq_mm(o - 1)
                        sumsq_mm(15)
                        t.drain().then_inc(tA.h, 1); tA.n += 1
                        if kind == "k":
                            t.wait_ge(sC.h, sC0 + gi + 1)
                            t.matmul(PS0[:, 0:W], smOF[0:1, 0:128], rowF[0:1, 0:W],
                                     start=True, stop=True).then_inc(tD.h, 1)
                            tD.n += 1
                        if kind == "q":
                            for h in range(H):
                                j = gi * 16 + h
                                t.wait_ge(sA.h, sA0 + j + 1)
                                if j >= 1:
                                    t.wait_ge(vB.h, vB0 + j)
                                t.matmul(PS1[:, 1536:1536 + W], smMq[:],
                                         qp[:, h * 512:h * 512 + W],
                                         start=True, stop=True).then_inc(tD.h, 1)
                                tD.n += 1
                    else:
                        nm = W // 128
                        for m in range(nm):
                            if midx >= 1:
                                t.wait_ge(vB.h, vB0 + midx)
                            for og in range(4):
                                seg = PS1[:, og * 512:(og + 1) * 512]
                                for i in range(16):
                                    t.matmul(seg, xf[gi % 2][:, i * 512 + m * 128:i * 512 + (m + 1) * 128],
                                             wA[:, i * 2048 + og * 512:i * 2048 + (og + 1) * 512],
                                             start=(i == 0), stop=False)
                                t.matmul(seg, smO[0:1, 0:128], brow[0:1, og * 512:(og + 1) * 512],
                                         start=False, stop=True)
                            t.drain().then_inc(tB.h, 1); tB.n += 1
                            midx += 1
                        t.drain().then_inc(tA.h, 1); tA.n += 1

            @sect("scalar")
            def _(s):
                if kind == "v":
                    return
                gcol = gqc if kind == "q" else gkc
                for gi, (c0, W) in enumerate(grps):
                    for o in range(16):
                        s.wait_ge(tB.h, tB0 + gi * 16 + o + 1)
                        if gi * 16 + o >= 2:
                            s.wait_ge(tC.h, tC0 + gi * 16 + o - 1)
                        bank = PS1[:, (o % 2) * 512:(o % 2) * 512 + W]
                        s.activation(sq[o % 2][:, 0:W], bank, AF.Square).then_inc(sB.h, 1); sB.n += 1
                        s.activation(qp[:, o * 512:o * 512 + W], bank, AF.Copy,
                                     scale=gcol[:, o:o + 1]).then_inc(sA.h, 1); sA.n += 1
                    s.wait_ge(vC.h, vC0 + gi + 1)
                    dst = rqA[0:1, c0:c0 + W] if kind == "q" else rowF[0:1, 0:W]
                    s.activation(dst, rowF[0:1, 512:512 + W], AF.Sqrt)
                    s.copy(spcS[:], spcS[:]).then_inc(sC.h, 1); sC.n += 1

            @sect("vector")
            def _(v):
                stu2 = 0
                for gi, (c0, W) in enumerate(grps):
                    if kind in ("q", "k"):
                        v.wait_ge(tC.h, tC0 + 16 * (gi + 1))
                        if gi >= 1:
                            v.wait_ge(sC.h, sC0 + gi)
                            if kind == "k":
                                v.wait_ge(tD.h, tD0 + gi)
                        v.tensor_scalar(rowF[0:1, 0:W],
                                        PS1[0:1, 1024:1024 + W],
                                        1.0 / 2048.0, EPS_RMS, op0=ALU.mult, op1=ALU.add)
                        v.tensor_copy(spcV[:], spcV[:])
                        v.reciprocal(rowF[0:1, 512:512 + W], rowF[0:1, 0:W])
                        v.tensor_copy(spcV[:], spcV[:]).then_inc(vC.h, 1)
                        vC.n += 1
                    if kind == "q":
                        for h in range(H):
                            j = gi * 16 + h
                            v.wait_ge(tD.h, tD0 + j + 1)
                            if stu2 >= 2:
                                v.wait_ge(dS.h, STORE_DONE[stu2 - 2])
                            v.tensor_copy(ev[h % 2][:, 0:W], PS1[:, 1536:1536 + W]).then_inc(vB.h, 1)
                            vB.n += 1
                            stu2 += 1
                    elif kind == "k":
                        v.wait_ge(tD.h, tD0 + gi + 1)
                        v.tensor_copy(rkb[:, 0:W], PS0[:, 0:W])
                        v.tensor_copy(spcV[:], spcV[:])
                        for o in range(16):
                            j = gi * 16 + o
                            if stu2 >= 2:
                                v.wait_ge(dS.h, STORE_DONE[stu2 - 2])
                            v.scalar_tensor_tensor(ev[o % 2][:, 0:W], qp[:, o * 512:o * 512 + W],
                                                   1.0, rkb[:, 0:W],
                                                   op0=ALU.mult, op1=ALU.mult).then_inc(vB.h, 1)
                            vB.n += 1
                            stu2 += 1
                    else:
                        nm = W // 128
                        for m in range(nm):
                            v.wait_ge(tB.h, tB0 + stu2 + 1)
                            if stu2 >= 2:
                                v.wait_ge(dS.h, STORE_DONE[stu2 - 2])
                            v.tensor_copy(vev[m % 2][:, 0:2048], PS1[:, 0:2048]).then_inc(vB.h, 1)
                            vB.n += 1
                            stu2 += 1
            barrier()

        def p1b1():
            dG0, dS0 = dG.n, dS.n
            tB0, tC0, tD0 = tB.n, tC.n, tD.n
            vA0, vB0, vC0 = vA.n, vB.n, vC.n
            sA0, sB0 = sA.n, sB.n
            kr = A[:, 0:24576]
            kr12 = B[:, 12544:14592]
            vr = A[:, 24576:49152]

            def kslot(ec):
                sl = ec % 13
                return kr12 if sl == 12 else kr[:, sl * 2048:(sl + 1) * 2048]
            qsb = (B[:, 0:2048], B[:, 2048:4096])
            aosb = (B[:, 4096:6144], B[:, 6144:8192])
            exps = (B[:, 8192:9216], B[:, 9216:10240])
            ptr = (B[:, 10240:11264], B[:, 11264:12288])
            diag = (B[:, 12288:12416], B[:, 12416:12544])
            nmx = (Fb[:, 0:1], Fb[:, 1:2])
            ebv = (Fb[:, 2:3], Fb[:, 3:4])
            zr = (Fb[:, 4:5], Fb[:, 5:6])
            rz = (Fb[:, 6:7], Fb[:, 7:8])
            mb = (Fb[:, 16:1040], Fb[:, 1040:2064])

            def new_ecs(lc):
                return list(range(12)) if lc == 0 else [lc + 11]

            LOADS_DONE = {}
            AOST_DONE = {}
            VR_DONE = {}
            cnt = dS.n
            cntg = dG.n
            for lc in range(LCH):
                cnt += 16 * (len(new_ecs(lc)) + 1)
                LOADS_DONE[lc] = cnt
                cntg += 16 * len(new_ecs(lc))
                VR_DONE[lc] = cntg
                if lc >= 1:
                    cnt += 16
                    AOST_DONE[lc - 1] = cnt
            cnt += 16
            AOST_DONE[LCH - 1] = cnt

            @sect("gpsimd")
            def _(g):
                for lc in range(LCH):
                    if lc >= 1:
                        g.wait_ge(tD.h, tD0 + 16 * lc)
                    for ec in new_ecs(lc):
                        slot = ec % 12
                        g.dma_start(out=vr[:, slot * 2048:(slot + 1) * 2048],
                                    in_=vtD[ec * 128:(ec + 1) * 128, :]).then_inc(dG.h, 16)
                        dG.n += 16
                if DEBUG:
                    g.dma_start(out=rqDbg, in_=rqA[0:1, 0:2048]).then_inc(dG.h, 16)
                    dG.n += 16
                    g.wait_ge(vC.h, vC.n + 1)
                    g.dma_start(out=rqSDbg, in_=rqS[:]).then_inc(dG.h, 16)
                    dG.n += 16

            @sect("tensor")
            def _(t):
                for c in range(16):
                    t.matmul(PS1[:, 1024 + c:1024 + c + 1], rqA[0:1, c * 128:(c + 1) * 128],
                             smOF[0:1, 0:1], start=True, stop=True)
                t.drain().then_inc(tA.h, 1); tA.n += 1

            @sect("vector")
            def _(v):
                v.wait_ge(tA.h, tA.n)
                v.tensor_copy(rqT[:, 0:16], PS1[:, 1024:1040])
                v.tensor_copy(spcV[:], spcV[:])
                v.tensor_scalar(rqS[:], rqT[:], SCALE, None, op0=ALU.mult)
                v.tensor_copy(spcV[:], spcV[:]).then_inc(vC.h, 1)
                vC.n += 1

            @sect("sync")
            def _(s):
                for lc in range(LCH):
                    if lc >= 2:
                        s.wait_ge(tB.h, tB0 + 16 * (lc - 1))
                    for ec in new_ecs(lc):
                        kb = kslot(ec)
                        s.dma_start(out=kb.rearrange("p (h t) -> p h t", h=16),
                                    in_=kkD[:, :, ec * 128:(ec + 1) * 128]).then_inc(dS.h, 16)
                        dS.n += 16
                    s.dma_start(out=qsb[lc % 2].rearrange("p (h t) -> p h t", h=16),
                                in_=kqD[:, :, lc * 128:(lc + 1) * 128]).then_inc(dS.h, 16)
                    dS.n += 16
                    if lc >= 1:
                        s.wait_ge(vB.h, vB0 + 16 * lc)
                        s.dma_start(out=aoD[:, :, (lc - 1) * 128:lc * 128],
                                    in_=aosb[(lc - 1) % 2].rearrange("p (h t) -> p h t", h=16)).then_inc(dS.h, 16)
                        dS.n += 16
                s.wait_ge(vB.h, vB0 + 16 * LCH)
                s.dma_start(out=aoD[:, :, (LCH - 1) * 128:LCH * 128],
                            in_=aosb[(LCH - 1) % 2].rearrange("p (h t) -> p h t", h=16)).then_inc(dS.h, 16)
                dS.n += 16
                s.wait_ge(dS.h, dS.n)

            sC0 = sC.n

            @sect("tensor")
            def _(t):
                NJ = LCH * H

                def band(j):
                    lc, h = divmod(j, H)
                    ecs = [lc + 8 + db for db in DB]
                    if h == 0:
                        t.wait_ge(dS.h, LOADS_DONE[lc])
                    half = PS0[:, (h % 2) * 1024:(h % 2) * 1024 + 1024]
                    if j >= 2:
                        t.wait_ge(vA.h, vA0 + j - 1)
                    for bi, ec in enumerate(ecs):
                        kb = kslot(ec)
                        t.matmul(half[:, bi * 128:(bi + 1) * 128],
                                 qsb[lc % 2][:, h * 128:(h + 1) * 128],
                                 kb[:, h * 128:(h + 1) * 128],
                                 start=True, stop=True)
                    t.drain().then_inc(tB.h, 1); tB.n += 1

                band(0)
                for j in range(NJ):
                    lc, h = divmod(j, H)
                    ecs = [lc + 8 + db for db in DB]
                    if j + 1 < NJ:
                        band(j + 1)
                    t.wait_ge(sB.h, sB0 + j + 1)
                    if j >= 1:
                        t.wait_ge(sC.h, sC0 + j)
                    for bi in range(8):
                        t.matmul(PS1[:, bi * 128:(bi + 1) * 128],
                                 exps[j % 2][:, bi * 128:(bi + 1) * 128], diag[j % 2][:],
                                 start=True, stop=True)
                    t.drain().then_inc(tC.h, 1); tC.n += 1
                    t.wait_ge(sC.h, sC0 + j + 1)
                    if j >= 1:
                        t.wait_ge(vB.h, vB0 + j)
                    if h == 0:
                        t.wait_ge(dG.h, VR_DONE[lc])
                    for bi, ec in enumerate(ecs):
                        slot = ec % 12
                        t.matmul(PS1[:, 1024:1152],
                                 vr[:, slot * 2048 + h * 128:slot * 2048 + (h + 1) * 128],
                                 ptr[j % 2][:, bi * 128:(bi + 1) * 128],
                                 start=(bi == 0), stop=(bi == 7))
                    t.drain().then_inc(tD.h, 1); tD.n += 1

            @sect("vector")
            def _(v):
                def ao_evac(j):
                    lc, h = divmod(j, H)
                    v.wait_ge(tD.h, tD0 + j + 1)
                    if lc >= 2 and h == 0:
                        v.wait_ge(dS.h, AOST_DONE[lc - 2])
                    v.tensor_copy(aosb[lc % 2][:, h * 128:(h + 1) * 128],
                                  PS1[:, 1024:1152]).then_inc(vB.h, 1)
                    vB.n += 1
                for lc in range(LCH):
                    for h in range(H):
                        j = lc * 16 + h
                        v.wait_ge(tB.h, tB0 + j + 1)
                        if j >= 2:
                            v.wait_ge(sA.h, sA0 + j - 1)
                        half = PS0[:, (h % 2) * 1024:(h % 2) * 1024 + 1024]
                        v.scalar_tensor_tensor(mb[j % 2][:], half, 1.0, smM[:],
                                               op0=ALU.mult, op1=ALU.add)
                        v.tensor_reduce(nmx[j % 2][:, 0:1], mb[j % 2][:], axis=AX.X,
                                        op=ALU.max, negate=True)
                        v.tensor_copy(spcV[:], spcV[:])
                        v.tensor_tensor(ebv[j % 2][:, 0:1], nmx[j % 2][:, 0:1],
                                        rqS[:, lc:lc + 1], op=ALU.mult)
                        v.tensor_copy(spcV[:], spcV[:]).then_inc(vA.h, 1)
                        vA.n += 1
                        v.wait_ge(sA.h, sA0 + j + 1)
                        v.tensor_reduce(zr[j % 2][:, 0:1], exps[j % 2][:], axis=AX.X, op=ALU.add)
                        v.tensor_copy(spcV[:], spcV[:])
                        v.reciprocal(rz[j % 2][:, 0:1], zr[j % 2][:, 0:1])
                        v.tensor_copy(spcV[:], spcV[:]).then_inc(vC.h, 1)
                        vC.n += 1
                        if j >= 1:
                            ao_evac(j - 1)
                ao_evac(LCH * H - 1)

            @sect("scalar")
            def _(s):
                def ptr_evac(j):
                    s.wait_ge(tC.h, tC0 + j + 1)
                    if j >= 1:
                        s.wait_ge(tD.h, tD0 + j - 1)
                    s.activation(ptr[j % 2][:], PS1[:, 0:1024], AF.Copy)
                    s.copy(spcS[:], spcS[:]).then_inc(sC.h, 1); sC.n += 1
                for lc in range(LCH):
                    for h in range(H):
                        j = lc * 16 + h
                        s.wait_ge(vA.h, vA0 + j + 1)
                        if j >= 2:
                            s.wait_ge(tC.h, tC0 + j - 1)
                        s.activation(exps[j % 2][:], mb[j % 2][:], AF.Exp, bias=ebv[j % 2][:, 0:1],
                                     scale=rqS[:, lc:lc + 1])
                        s.copy(spcS[:], spcS[:]).then_inc(sA.h, 1); sA.n += 1
                        s.wait_ge(vC.h, vC0 + 1 + j + 1)
                        s.activation(diag[j % 2][:], smI[:], AF.Copy, scale=rz[j % 2][:, 0:1])
                        s.copy(spcS[:], spcS[:]).then_inc(sB.h, 1); sB.n += 1
                        if j >= 1:
                            ptr_evac(j - 1)
                ptr_evac(LCH * H - 1)
            barrier()

        def p1b2():
            dG0, dS0, dC0 = dG.n, dS.n, dC.n
            tB0, vA0, vB0, vC0, sA0 = tB.n, vA.n, vB.n, vC.n, sA.n
            wA = A[:, 0:32768]
            aold = (B[:, 0:2048], B[:, 2048:4096])
            dump = B[:, 4096:6144]
            xr = (Fb[:, 0:2048], Fb[:, 2048:4096])
            xsum = Fb[:, 4096:6144]
            s1c = stats[:, 0:1]
            s2c = stats[:, 1:2]
            mu = stats[:, 2:3]
            iv = stats[:, 3:4]
            rstd = stats[:, 4:5]
            nmr = stats[:, 5:6]

            @sect("gpsimd")
            def _(g):
                for i in range(16):
                    g.dma_start(out=wA[:, i * 2048:(i + 1) * 2048],
                                in_=woT[i * 128:(i + 1) * 128, :]).then_inc(dG.h, 16)
                    dG.n += 16

            @sect("sync")
            def _(s):
                for lc in range(LCH):
                    if lc >= 2:
                        s.wait_ge(tB.h, tB0 + lc - 1)
                    s.dma_start(out=aold[lc % 2].rearrange("p (h t) -> p h t", h=16),
                                in_=aoD[:, :, lc * 128:(lc + 1) * 128]).then_inc(dS.h, 16)
                    dS.n += 16
                    if lc >= 2:
                        s.wait_ge(vA.h, vA0 + lc - 1)
                    s.dma_start(out=xr[lc % 2][:],
                                in_=xe[OFF + lc * 128:OFF + (lc + 1) * 128, :]).then_inc(dS.h, 16)
                    dS.n += 16
                s.wait_ge(dS.h, dS.n)

            @sect("tensor")
            def _(t):
                t.wait_ge(dG.h, dG0 + 256)
                for lc in range(LCH):
                    t.wait_ge(dS.h, dS0 + 32 * lc + 16)
                    PSx = PS0 if lc % 2 == 0 else PS1
                    if lc >= 2:
                        t.wait_ge(vA.h, vA0 + lc - 1)
                    for hh in range(16):
                        for og in range(4):
                            t.matmul(PSx[:, og * 512:(og + 1) * 512],
                                     aold[lc % 2][:, hh * 128:(hh + 1) * 128],
                                     wA[:, hh * 2048 + og * 512:hh * 2048 + (og + 1) * 512],
                                     start=(hh == 0), stop=False)
                    for og in range(4):
                        t.matmul(PSx[:, og * 512:(og + 1) * 512], smO[0:1, 0:128],
                                 rbo[0:1, og * 512:(og + 1) * 512], start=False, stop=True)
                    t.drain().then_inc(tB.h, 1); tB.n += 1

            @sect("vector")
            def _(v):
                for lc in range(LCH):
                    v.wait_ge(tB.h, tB0 + lc + 1)
                    v.wait_ge(dS.h, dS0 + 32 * (lc + 1))
                    if lc >= 1:
                        v.wait_ge(dC.h, dC0 + 16 * lc)
                    PSx = PS0 if lc % 2 == 0 else PS1
                    v.tensor_tensor(xsum[:], PSx[:, 0:2048], xr[lc % 2][:], op=ALU.add)
                    v.drain().then_inc(vA.h, 1); vA.n += 1
                    v.tensor_reduce(s1c, xsum[:], axis=AX.X, op=ALU.add)
                    v.wait_ge(sA.h, sA0 + 2 * lc + 1)
                    v.tensor_reduce(s2c, dump[:], axis=AX.X, op=ALU.add)
                    v.tensor_copy(spcV[:], spcV[:])
                    v.tensor_scalar(mu, s1c, -1.0 / 2048.0, None, op0=ALU.mult)
                    v.tensor_scalar(iv, s2c, 1.0 / 2048.0, EPS_LN, op0=ALU.mult, op1=ALU.add)
                    v.tensor_tensor(nmr, mu, mu, op=ALU.mult)
                    v.tensor_copy(spcV[:], spcV[:])
                    v.tensor_tensor(iv, iv, nmr, op=ALU.subtract)
                    v.tensor_copy(spcV[:], spcV[:])
                    v.reciprocal(iv, iv)
                    v.tensor_copy(spcV[:], spcV[:]).then_inc(vC.h, 1); vC.n += 1
                    v.wait_ge(sA.h, sA0 + 2 * lc + 2)
                    v.tensor_scalar(xsum[:], xsum[:], rstd[:, 0:1], nmr[:, 0:1],
                                    op0=ALU.mult, op1=ALU.add).then_inc(vB.h, 1)
                    vB.n += 1

            @sect("scalar")
            def _(s):
                for lc in range(LCH):
                    s.wait_ge(vA.h, vA0 + lc + 1)
                    s.activation(dump[:], xsum[:], AF.Square)
                    s.copy(spcS[:], spcS[:]).then_inc(sA.h, 1)
                    sA.n += 1
                    s.wait_ge(vC.h, vC0 + lc + 1)
                    s.activation(rstd, iv, AF.Sqrt)
                    s.copy(spcS[:], spcS[:])
                    s.mul(nmr, rstd, mu[:, 0:1])
                    s.copy(spcS[:], spcS[:]).then_inc(sA.h, 1); sA.n += 1
                    s.wait_ge(vB.h, vB0 + lc + 1)
                    s.dma_start(out=x1D[lc * 128:(lc + 1) * 128, :], in_=xsum[:]).then_inc(dC.h, 16)
                    dC.n += 16
                s.wait_ge(dC.h, dC.n)
            barrier()

        def p2a():
            dG0, dS0 = dG.n, dS.n
            tA0, tB0, vA0, vB0, vC0, sA0 = tA.n, tB.n, vA.n, vB.n, vC.n, sA.n
            x1fm = A[:, 0:32768]
            wgr = (A[:, 32768:34816], A[:, 34816:36864])
            wur = (A[:, 36864:38912], A[:, 38912:40960])
            wgc = (A[:, 40960:43008], A[:, 43008:45056])
            wuc = (A[:, 45056:47104], A[:, 47104:49152])
            xtm = (B[:, 0:2048], B[:, 2048:4096], B[:, 4096:6144])
            gsb = B[:, 6144:6656]
            usb = B[:, 6656:7168]
            hsb = (B[:, 7168:7680], B[:, 7680:8192])

            @sect("gpsimd")
            def _(g):
                for lc in range(LCH):
                    if lc >= 3:
                        g.wait_ge(tA.h, tA0 + lc - 2)
                    g.dma_start(out=xtm[lc % 3][:], in_=x1D[lc * 128:(lc + 1) * 128, :]).then_inc(dG.h, 16)
                    dG.n += 16
                for o in range(NHC):
                    if o >= 2:
                        g.wait_ge(vA.h, vA0 + 2 * (o - 1))
                    g.dma_start(out=wgr[o % 2][:],
                                in_=gTb[o].rearrange("(i p) c -> p i c", p=128)).then_inc(dG.h, 16)
                    dG.n += 16
                    g.dma_start(out=wur[o % 2][:],
                                in_=uTb[o].rearrange("(i p) c -> p i c", p=128)).then_inc(dG.h, 16)
                    dG.n += 16
                g.wait_ge(tA.h, tA0 + LCH)
                for _ in range(2):
                    g.dma_start(out=xtm[0][:, 0:512], in_=x1D[0:128, 0:512]).then_inc(dG.h, 16)
                    dG.n += 16

            @sect("tensor")
            def _(t):
                for lc in range(LCH):
                    t.wait_ge(dG.h, dG0 + 16 * min(lc + 2, LCH + 2))
                    if lc >= 1:
                        t.wait_ge(vB.h, vB0 + lc)
                    for i in range(16):
                        t.matmul(PS1[:, i * 128:(i + 1) * 128], xtm[lc % 3][:, i * 128:(i + 1) * 128],
                                 smI[:], start=True, stop=True)
                    t.drain().then_inc(tA.h, 1); tA.n += 1
                for o in range(NHC):
                    t.wait_ge(vA.h, vA0 + 2 * (o + 1))
                    if o == 0:
                        t.wait_ge(vB.h, vB0 + LCH)
                    for gi in range(4):
                        j = o * 4 + gi
                        bb = (gi % 2) * 1024
                        if j >= 2:
                            t.wait_ge(sA.h, sA0 + j - 1)
                            t.wait_ge(vC.h, vC0 + j - 1)
                        for i in range(16):
                            t.matmul(PS0[:, bb:bb + 512],
                                     wgc[o % 2][:, i * 128:(i + 1) * 128],
                                     x1fm[:, i * 2048 + gi * 512:i * 2048 + (gi + 1) * 512],
                                     start=(i == 0), stop=(i == 15))
                        for i in range(16):
                            t.matmul(PS0[:, bb + 512:bb + 1024],
                                     wuc[o % 2][:, i * 128:(i + 1) * 128],
                                     x1fm[:, i * 2048 + gi * 512:i * 2048 + (gi + 1) * 512],
                                     start=(i == 0), stop=(i == 15))
                        t.drain().then_inc(tB.h, 1); tB.n += 1

            @sect("vector")
            def _(v):
                for lc in range(LCH):
                    v.wait_ge(tA.h, tA0 + lc + 1)
                    for i in range(16):
                        v.tensor_copy(x1fm[:, i * 2048 + lc * 128:i * 2048 + (lc + 1) * 128],
                                      PS1[:, i * 128:(i + 1) * 128])
                    v.drain().then_inc(vB.h, 1); vB.n += 1
                for o in range(NHC):
                    v.wait_ge(dG.h, dG0 + 16 * LCH + 32 * min(o + 2, NHC + 1))
                    if o >= 2:
                        v.wait_ge(tB.h, tB0 + 4 * (o - 1))
                    v.tensor_scalar(wgc[o % 2][:], wgr[o % 2][:], -128.0, None, op0=ALU.add)
                    v.tensor_scalar(wuc[o % 2][:], wur[o % 2][:], -128.0, None, op0=ALU.add)
                    v.tensor_copy(spcV[:], spcV[:]).then_inc(vA.h, 2)
                    vA.n += 2
                    for gi in range(4):
                        j = o * 4 + gi
                        v.wait_ge(tB.h, tB0 + j + 1)
                        v.tensor_scalar(usb[:, 0:512], PS0[:, (gi % 2) * 1024 + 512:(gi % 2) * 1024 + 1024],
                                        suc[:, o:o + 1], None, op0=ALU.mult)
                        v.wait_ge(sA.h, sA0 + j + 1)
                        if j >= 2:
                            v.wait_ge(dS.h, dS0 + 16 * (j - 1))
                        v.tensor_tensor(hsb[j % 2][:, 0:512], gsb[:, 0:512],
                                        usb[:, 0:512], op=ALU.mult).then_inc(vC.h, 1)
                        vC.n += 1

            @sect("scalar")
            def _(s):
                for o in range(NHC):
                    for gi in range(4):
                        j = o * 4 + gi
                        s.wait_ge(tB.h, tB0 + j + 1)
                        if j >= 1:
                            s.wait_ge(vC.h, vC0 + j)
                        s.activation(gsb[:, 0:512], PS0[:, (gi % 2) * 1024:(gi % 2) * 1024 + 512],
                                     AF.Silu, scale=sgc[:, o:o + 1]).then_inc(sA.h, 1)
                        sA.n += 1

            @sect("sync")
            def _(s):
                for o in range(NHC):
                    for gi in range(4):
                        j = o * 4 + gi
                        s.wait_ge(vC.h, vC0 + j + 1)
                        s.dma_start(out=hD[o, :, gi * 512:(gi + 1) * 512],
                                    in_=hsb[j % 2][:, 0:512]).then_inc(dS.h, 16)
                        dS.n += 16
                s.wait_ge(dS.h, dS.n)
            barrier()

        def p2b():
            dG0, dS0, dC0 = dG.n, dS.n, dC.n
            tA0, tB0, vA0, vB0 = tA.n, tB.n, vA.n, vB.n
            dTq = A[:, 0:22528]
            hring = A[:, 22528:26624]
            wdr = tuple(B[:, i * 512:(i + 1) * 512] for i in range(4))
            wdum = B[:, 2048:2560]
            fev = Fb[:, 0:2048]

            @sect("gpsimd")
            def _(g):
                for qd in range(4):
                    for o in range(NHC):
                        j = qd * NHC + o
                        if j >= 4:
                            g.wait_ge(vA.h, vA0 + j - 3)
                        g.dma_start(out=wdr[j % 4][:],
                                    in_=dT[o * 128:(o + 1) * 128, qd * 512:(qd + 1) * 512]).then_inc(dG.h, 16)
                        dG.n += 16
                for _ in range(2):
                    g.dma_start(out=wdum[:], in_=dT[0:128, 0:512]).then_inc(dG.h, 16)
                    dG.n += 16

            @sect("vector")
            def _(v):
                for qd in range(4):
                    for o in range(NHC):
                        j = qd * NHC + o
                        v.wait_ge(dG.h, dG0 + 16 * min(j + 3, 4 * NHC + 2))
                        if qd >= 1 and o == 0:
                            v.wait_ge(tA.h, tA0 + 4 * qd)
                        v.scalar_tensor_tensor(dTq[:, o * 512:(o + 1) * 512], wdr[j % 4][:],
                                               -128.0, sdb[:, qd * 512:(qd + 1) * 512],
                                               op0=ALU.add, op1=ALU.mult)
                        if o == NHC - 1:
                            v.tensor_copy(spcV[:], spcV[:])
                            v.tensor_copy(spcV[:], spcV[:])
                            v.tensor_copy(spcV[:], spcV[:])
                        v.tensor_copy(spcV[:], spcV[:]).then_inc(vA.h, 1)
                        vA.n += 1
                    for gi in range(4):
                        k = qd * 4 + gi
                        v.wait_ge(tA.h, tA0 + k + 1)
                        if k >= 1:
                            v.wait_ge(dC.h, dC0 + 64 * k)
                        PSx = PS0 if gi % 2 == 0 else PS1
                        for lci in range(4):
                            v.tensor_copy(fev[:, lci * 512:(lci + 1) * 512],
                                          PSx[:, lci * 512:(lci + 1) * 512])
                        v.drain().then_inc(vB.h, 1); vB.n += 1

            @sect("sync")
            def _(s):
                nh = 0
                for qd in range(4):
                    for gi in range(4):
                        for o in range(NHC):
                            nh += 1
                            if nh > 8:
                                s.wait_ge(tB.h, tB0 + nh - 8)
                            s.dma_start(out=hring[:, ((nh - 1) % 8) * 512:((nh - 1) % 8 + 1) * 512],
                                        in_=hD[o, :, gi * 512:(gi + 1) * 512]).then_inc(dS.h, 16)
                            dS.n += 16

            @sect("tensor")
            def _(t):
                nh = 0
                for qd in range(4):
                    for gi in range(4):
                        k = qd * 4 + gi
                        PSx = PS0 if gi % 2 == 0 else PS1
                        if k >= 2:
                            t.wait_ge(vB.h, vB0 + k - 1)
                        for o in range(NHC):
                            nh += 1
                            t.wait_ge(dS.h, dS0 + 16 * nh)
                            t.wait_ge(vA.h, vA0 + min(qd * NHC + o + 2, (qd + 1) * NHC))
                            slot = (nh - 1) % 8
                            for lci in range(4):
                                t.matmul(PSx[:, lci * 512:(lci + 1) * 512],
                                         hring[:, slot * 512 + lci * 128:slot * 512 + (lci + 1) * 128],
                                         dTq[:, o * 512:(o + 1) * 512],
                                         start=(o == 0), stop=(o == NHC - 1))
                            t.drain().then_inc(tB.h, 1); tB.n += 1
                        t.drain().then_inc(tA.h, 1); tA.n += 1

            @sect("scalar")
            def _(s):
                for qd in range(4):
                    for gi in range(4):
                        k = qd * 4 + gi
                        s.wait_ge(vB.h, vB0 + k + 1)
                        for lci in range(4):
                            r0 = (gi * 4 + lci) * 128
                            s.dma_start(out=fD[r0:r0 + 128, qd * 512:(qd + 1) * 512],
                                        in_=fev[:, lci * 512:(lci + 1) * 512]).then_inc(dC.h, 16)
                            dC.n += 16
                s.wait_ge(dC.h, dC.n)
            barrier()

        def p2c():
            dS0, dC0 = dS.n, dC.n
            vA0, vB0, vC0, sA0 = vA.n, vB.n, vC.n, sA.n
            fsb = (Fb[:, 0:2048], Fb[:, 2048:4096])
            xsum = Fb[:, 4096:6144]
            dump = B[:, 0:2048]
            s1c = stats[:, 0:1]
            s2c = stats[:, 1:2]
            mu = stats[:, 2:3]
            iv = stats[:, 3:4]
            rstd = stats[:, 4:5]
            nmr = stats[:, 5:6]

            @sect("sync")
            def _(s):
                for lc in range(LCH):
                    if lc >= 1:
                        s.wait_ge(dC.h, dC0 + 16 * lc)
                    s.dma_start(out=xsum[:], in_=x1D[lc * 128:(lc + 1) * 128, :]).then_inc(dS.h, 16)
                    dS.n += 16
                    if lc >= 2:
                        s.wait_ge(vC.h, vC0 + lc - 1)
                    s.dma_start(out=fsb[lc % 2][:], in_=fD[lc * 128:(lc + 1) * 128, :]).then_inc(dS.h, 16)
                    dS.n += 16

            @sect("vector")
            def _(v):
                for lc in range(LCH):
                    v.wait_ge(dS.h, dS0 + 32 * (lc + 1))
                    v.tensor_tensor(xsum[:], xsum[:], fsb[lc % 2][:], op=ALU.add)
                    v.drain().then_inc(vC.h, 1); vC.n += 1
                    v.tensor_reduce(s1c, xsum[:], axis=AX.X, op=ALU.add)
                    v.wait_ge(sA.h, sA0 + 2 * lc + 1)
                    v.tensor_reduce(s2c, dump[:], axis=AX.X, op=ALU.add)
                    v.tensor_copy(spcV[:], spcV[:])
                    v.tensor_scalar(mu, s1c, -1.0 / 2048.0, None, op0=ALU.mult)
                    v.tensor_scalar(iv, s2c, 1.0 / 2048.0, EPS_LN, op0=ALU.mult, op1=ALU.add)
                    v.tensor_tensor(nmr, mu, mu, op=ALU.mult)
                    v.tensor_copy(spcV[:], spcV[:])
                    v.tensor_tensor(iv, iv, nmr, op=ALU.subtract)
                    v.tensor_copy(spcV[:], spcV[:])
                    v.reciprocal(iv, iv)
                    v.tensor_copy(spcV[:], spcV[:]).then_inc(vB.h, 1); vB.n += 1
                    v.wait_ge(sA.h, sA0 + 2 * lc + 2)
                    v.tensor_scalar(xsum[:], xsum[:], rstd[:, 0:1], nmr[:, 0:1],
                                    op0=ALU.mult, op1=ALU.add).then_inc(vA.h, 1)
                    vA.n += 1

            @sect("scalar")
            def _(s):
                for lc in range(LCH):
                    s.wait_ge(vC.h, vC0 + lc + 1)
                    s.activation(dump[:], xsum[:], AF.Square)
                    s.copy(spcS[:], spcS[:]).then_inc(sA.h, 1)
                    sA.n += 1
                    s.wait_ge(vB.h, vB0 + lc + 1)
                    s.activation(rstd, iv, AF.Sqrt)
                    s.copy(spcS[:], spcS[:])
                    s.mul(nmr, rstd, mu[:, 0:1])
                    s.copy(spcS[:], spcS[:]).then_inc(sA.h, 1); sA.n += 1
                    s.wait_ge(vA.h, vA0 + lc + 1)
                    s.dma_start(out=y[lc * 128:(lc + 1) * 128, :], in_=xsum[:]).then_inc(dC.h, 16)
                    dC.n += 16
                s.wait_ge(dC.h, dC.n)
            barrier()

        p0()
        p1x()
        proj("q")
        proj("k")
        proj("v")
        p1b1()
        p1b2()
        p2a()
        p2b()
        p2c()
    return nc


def _prep(inputs):
    x = np.asarray(inputs["L0_attn_x"], dtype=np.float32)
    shared = {
        "wqT": np.ascontiguousarray(np.asarray(inputs["L0_attn_wq"], np.float32).T),
        "wkT": np.ascontiguousarray(np.asarray(inputs["L0_attn_wk"], np.float32).T),
        "wvT": np.ascontiguousarray(np.asarray(inputs["L0_attn_wv"], np.float32).T),
        "woT": np.ascontiguousarray(np.asarray(inputs["L0_attn_wo"], np.float32).T),
        "gTb": np.ascontiguousarray(
            np.asarray(inputs["L0_ffn_wq_gate"], np.int32).reshape(NHC, 128, DM).transpose(0, 2, 1)),
        "uTb": np.ascontiguousarray(
            np.asarray(inputs["L0_ffn_wq_up"], np.int32).reshape(NHC, 128, DM).transpose(0, 2, 1)),
        "dT": np.ascontiguousarray(np.asarray(inputs["L0_ffn_wq_down"], np.int32).T),
        "r_bq": np.asarray(inputs["L0_attn_bq"], np.float32).reshape(1, DM).copy(),
        "r_bk": np.asarray(inputs["L0_attn_bk"], np.float32).reshape(1, DM).copy(),
        "r_bv": np.asarray(inputs["L0_attn_bv"], np.float32).reshape(1, DM).copy(),
        "r_bo": np.asarray(inputs["L0_attn_bo"], np.float32).reshape(1, DM).copy(),
        "r_gq": np.asarray(inputs["L0_attn_gq"], np.float32).reshape(1, DM).copy(),
        "r_gk": np.asarray(inputs["L0_attn_gk"], np.float32).reshape(1, DM).copy(),
        "r_sg": np.asarray(inputs["L0_ffn_s_gate"], np.float32).reshape(1, HID).copy(),
        "r_su": np.asarray(inputs["L0_ffn_s_up"], np.float32).reshape(1, HID).copy(),
        "r_sd": np.asarray(inputs["L0_ffn_s_down"], np.float32).reshape(1, DM).copy(),
        "wmix": np.asarray(inputs["L0_attn_w_mix"], np.float32).reshape(1, 5).copy(),
        "cI": np.eye(128, dtype=np.float32).astype(bfd),
        "cMask": _mask01(),
        "cPm": _pmats(),
        "cOnes": np.ones((128, 512), dtype=np.float32).astype(bfd),
        "cOnesF": np.ones((1, 128), dtype=np.float32),
    }
    in_maps = []
    for c in range(NCORE):
        b, half = divmod(c, 2)
        p0_ = half * 2048
        idx = (p0_ - OFF + np.arange(EXT)) % 4096
        m = dict(shared)
        m["xe"] = np.ascontiguousarray(x[b, idx, :])
        in_maps.append(m)
    return in_maps


def kernel(**inputs):
    in_maps = _prep(inputs)
    nc = build_nc()
    res = run_bass_kernel_spmd(nc, in_maps, core_ids=list(range(NCORE)))
    out = np.empty((4, 4096, DM), dtype=np.float32)
    for c in range(NCORE):
        b, half = divmod(c, 2)
        out[b, half * 2048:(half + 1) * 2048, :] = res.results[c]["y"]
    return out


# version 04
# speedup vs baseline: 1.2083x; 1.0830x over previous
import numpy as np
import ml_dtypes
import concourse.bass as bass
import concourse.mybir as mybir
from concourse.bass_utils import run_bass_kernel_spmd

F32 = mybir.dt.float32
BF16 = mybir.dt.bfloat16
I32 = mybir.dt.int32
AF = mybir.ActivationFunctionType
ALU = mybir.AluOpType
AX = mybir.AxisListType
bfd = ml_dtypes.bfloat16

SHIFTS = [0, 1, -1, 3, -3, 7, -7, 20, -20, 53, -53, 141, -141, 380, -380, 1024]
CH = [0, 1, 2, 4, 8]
DB = [-8, -3, -2, -1, 0, 1, 2, 3]
H = 16
DM = 2048
HID = 5632
NHC = 44
TOK = 2048
EXT = 3456
ECH = 27
LCH = 16
OFF = 1024
EPS_RMS = 1e-6
EPS_LN = 1e-5
SCALE = 128.0 ** -0.5
NCORE = 8
DEBUG = False

EGRP = [(0, 512), (512, 512), (1024, 512), (1536, 512), (2048, 512), (2560, 512), (3072, 384)]
QGRP = [(0, 512), (512, 512), (1024, 512), (1536, 512)]


def _mask01():
    m = np.full((128, 8 * 128), -1e30, dtype=np.float32)
    ss = set(SHIFTS)
    for p in range(128):
        for bi, db in enumerate(DB):
            for q in range(128):
                if (p - q - db * 128) in ss:
                    m[p, bi * 128 + q] = 0.0
    assert ((m == 0).sum(1) == 16).all()
    return m.astype(bfd)


def _pmats():
    P = np.zeros((128, 5 * 128), dtype=np.float32)
    for ci, c in enumerate(CH):
        P[(np.arange(128) - c) % 128, ci * 128 + np.arange(128)] = 1.0
    return P.astype(bfd)


class Ct:
    def __init__(self, h):
        self.h = h
        self.n = 0


def build_nc():
    nc = bass.Bass("TRN2", target_bir_lowering=False)

    xe = nc.dram_tensor("xe", [EXT, DM], F32, kind="ExternalInput").ap()
    wqT = nc.dram_tensor("wqT", [DM, DM], F32, kind="ExternalInput").ap()
    wkT = nc.dram_tensor("wkT", [DM, DM], F32, kind="ExternalInput").ap()
    wvT = nc.dram_tensor("wvT", [DM, DM], F32, kind="ExternalInput").ap()
    woT = nc.dram_tensor("woT", [DM, DM], F32, kind="ExternalInput").ap()
    gTb = nc.dram_tensor("gTb", [NHC, DM, 128], I32, kind="ExternalInput").ap()
    uTb = nc.dram_tensor("uTb", [NHC, DM, 128], I32, kind="ExternalInput").ap()
    dT = nc.dram_tensor("dT", [HID, DM], I32, kind="ExternalInput").ap()
    r_bq = nc.dram_tensor("r_bq", [1, DM], F32, kind="ExternalInput").ap()
    r_bk = nc.dram_tensor("r_bk", [1, DM], F32, kind="ExternalInput").ap()
    r_bv = nc.dram_tensor("r_bv", [1, DM], F32, kind="ExternalInput").ap()
    r_bo = nc.dram_tensor("r_bo", [1, DM], F32, kind="ExternalInput").ap()
    r_gq = nc.dram_tensor("r_gq", [1, DM], F32, kind="ExternalInput").ap()
    r_gk = nc.dram_tensor("r_gk", [1, DM], F32, kind="ExternalInput").ap()
    r_sg = nc.dram_tensor("r_sg", [1, HID], F32, kind="ExternalInput").ap()
    r_su = nc.dram_tensor("r_su", [1, HID], F32, kind="ExternalInput").ap()
    r_sd = nc.dram_tensor("r_sd", [1, DM], F32, kind="ExternalInput").ap()
    wmix = nc.dram_tensor("wmix", [1, 5], F32, kind="ExternalInput").ap()
    cI = nc.dram_tensor("cI", [128, 128], BF16, kind="ExternalInput").ap()
    cMask = nc.dram_tensor("cMask", [128, 1024], BF16, kind="ExternalInput").ap()
    cPm = nc.dram_tensor("cPm", [128, 640], BF16, kind="ExternalInput").ap()
    cOnes = nc.dram_tensor("cOnes", [128, 512], BF16, kind="ExternalInput").ap()
    cOnesF = nc.dram_tensor("cOnesF", [1, 128], F32, kind="ExternalInput").ap()

    sk = "ExternalOutput" if DEBUG else "Internal"
    xfmD = nc.dram_tensor("xfmD", [128, 16, EXT], BF16, kind=sk).ap()
    kqD = nc.dram_tensor("kqD", [128, H, TOK], BF16, kind=sk).ap()
    kkD = nc.dram_tensor("kkD", [128, H, EXT], BF16, kind=sk).ap()
    vtD = nc.dram_tensor("vtD", [EXT, DM], BF16, kind=sk).ap()
    aoD = nc.dram_tensor("aoD", [128, H, TOK], BF16, kind=sk).ap()
    x1D = nc.dram_tensor("x1D", [TOK, DM], F32, kind=sk).ap()
    hD = nc.dram_tensor("hD", [NHC, 128, TOK], BF16, kind=sk).ap()
    fD = nc.dram_tensor("fD", [TOK, DM], F32, kind=sk).ap()
    rqDbg = nc.dram_tensor("rqDbg", [1, TOK], F32, kind="ExternalOutput").ap() if DEBUG else None
    rqSDbg = nc.dram_tensor("rqSDbg", [128, 16], F32, kind="ExternalOutput").ap() if DEBUG else None
    y = nc.dram_tensor("y", [TOK, DM], F32, kind="ExternalOutput").ap()

    import contextlib
    st = contextlib.ExitStack()

    def sb(name, shp, dt):
        return st.enter_context(nc.sbuf_tensor(name, shp, dt))

    def psum(name, shp, dt):
        return st.enter_context(nc.psum_tensor(name, shp, dt))

    def sem(name):
        return Ct(st.enter_context(nc.semaphore(name)))

    with st:
        A = sb("A", [128, 49152], BF16)
        B = sb("B", [128, 16384], BF16)
        Fb = sb("Fb", [128, 6144], F32)
        smI = sb("smI", [128, 128], BF16)
        smM = sb("smM", [128, 1024], BF16)
        smP = sb("smP", [128, 640], BF16)
        smMq = sb("smMq", [128, 128], BF16)
        smO = sb("smO", [128, 512], BF16)
        smOF = sb("smOF", [1, 128], F32)
        gqc = sb("gqc", [128, 16], F32)
        gkc = sb("gkc", [128, 16], F32)
        sgc = sb("sgc", [128, 44], F32)
        suc = sb("suc", [128, 44], F32)
        sdb = sb("sdb", [128, 2048], F32)
        wmb = sb("wmb", [128, 8], F32)
        rbq = sb("rbq", [1, 2048], BF16)
        rbk = sb("rbk", [1, 2048], BF16)
        rbv = sb("rbv", [1, 2048], BF16)
        rbo = sb("rbo", [1, 2048], BF16)
        rsd = sb("rsd", [1, 2048], F32)
        rqA = sb("rqA", [1, 2048], F32)
        rqT = sb("rqT", [128, 16], F32)
        rqS = sb("rqS", [128, 16], F32)
        rowF = sb("rowF", [1, 1024], F32)
        stats = sb("stats", [128, 8], F32)
        spcV = sb("spcV", [128, 1], F32)
        spcS = sb("spcS", [128, 1], F32)

        PS0 = psum("PS0", [128, 2048], F32)
        PS1 = psum("PS1", [128, 2048], F32)

        bar = sem("bar")
        dG = sem("dG")
        dS = sem("dS")
        dC = sem("dC")
        tA = sem("tA"); tB = sem("tB"); tC = sem("tC"); tD = sem("tD")
        vA = sem("vA"); vB = sem("vB"); vC = sem("vC")
        sA = sem("sA"); sB = sem("sB"); sC = sem("sC")
        gA = sem("gA")

        blk = st.enter_context(nc.Block())
        ENGS = ["tensor", "vector", "scalar", "gpsimd", "sync"]
        barn = [0]

        def barrier():
            barn[0] += 1
            tgt = 5 * barn[0]
            for e in ENGS:
                def _f(en, tgt=tgt):
                    en.sem_inc(bar.h, 1)
                    en.wait_ge(bar.h, tgt)
                getattr(blk, e)(_f)

        def sect(e):
            return getattr(blk, e)

        def p0():
            tB0_p0 = tB.n
            dG0_rows = [0]
            rowP0 = Fb[0:1, 0:5632]

            @sect("gpsimd")
            def _(g):
                g.dma_start(out=smOF[:], in_=cOnesF).then_inc(dG.h, 16); dG.n += 16
                for out, src in ((smI, cI), (smM, cMask), (smP, cPm), (smO, cOnes)):
                    g.dma_start(out=out[:], in_=src).then_inc(dG.h, 16); dG.n += 16
                g.dma_start(out=wmb[0:1, 0:5], in_=wmix).then_inc(dG.h, 16); dG.n += 16
                dG0_rows[0] = dG.n
                for ri, (src, n) in enumerate(((r_gq, 16), (r_gk, 16), (r_sg, 44), (r_su, 44))):
                    g.dma_start(out=rowP0[0:1, 0:n * 128], in_=src).then_inc(dG.h, 16); dG.n += 16
                    g.dma_start(out=rowP0[0:1, 0:n * 128], in_=src).then_inc(dG.h, 16); dG.n += 16
                    g.wait_ge(gA.h, gA.n + 1); gA.n += 1
                for out, src in ((rbq, r_bq), (rbk, r_bk), (rbv, r_bv), (rbo, r_bo)):
                    g.dma_start(out=out[:], in_=src).then_inc(dG.h, 16); dG.n += 16
                g.dma_start(out=rsd[:], in_=r_sd).then_inc(dG.h, 16); dG.n += 16

            @sect("tensor")
            def _(t):
                nd = dG0_rows[0]
                vA0_p0 = vA.n
                for ri, (dst, n) in enumerate(((gqc, 16), (gkc, 16), (sgc, 44), (suc, 44))):
                    t.wait_ge(dG.h, nd + 32 * (ri + 1))
                    if ri >= 1:
                        t.wait_ge(vA.h, vA0_p0 + ri)
                    for c in range(n):
                        t.matmul(PS0[:, c:c + 1], rowP0[0:1, c * 128:(c + 1) * 128],
                                 smOF[0:1, 0:1], start=True, stop=True)
                    t.drain().then_inc(tB.h, 1); tB.n += 1
                t.wait_ge(dG.h, dG.n)
                t.wait_ge(vA.h, vA0_p0 + 4)
                t.matmul(PS0[:, 0:5], smOF[0:1, 0:128], wmb[0:1, 0:5], start=True, stop=True)
                for j in range(4):
                    t.matmul(PS1[:, j * 512:(j + 1) * 512], smOF[0:1, 0:128],
                             rsd[0:1, j * 512:(j + 1) * 512], start=True, stop=True)
                t.drain().then_inc(tA.h, 1); tA.n += 1

            @sect("scalar")
            def _(sc):
                sc.wait_ge(tA.h, tA.n)
                sc.copy(wmb[:, 0:5], PS0[:, 0:5])
                sc.copy(spcS[:], spcS[:]).then_inc(sA.h, 1); sA.n += 1

            @sect("vector")
            def _(v):
                for ri, (dst, n) in enumerate(((gqc, 16), (gkc, 16), (sgc, 44), (suc, 44))):
                    v.wait_ge(tB.h, tB0_p0 + ri + 1)
                    v.tensor_copy(dst[:, 0:n], PS0[:, 0:n])
                    v.tensor_copy(spcV[:], spcV[:]).then_inc(vA.h, 1); vA.n += 1
                    v.tensor_copy(spcV[:], spcV[:]).then_inc(gA.h, 1)
                v.wait_ge(tA.h, tA.n)
                v.tensor_copy(sdb[:, :], PS1[:, 0:2048])
                v.wait_ge(sA.h, sA.n)
                v.tensor_scalar(smMq[:], smP[:, 0:128], wmb[:, 0:1], None, op0=ALU.mult)
                for ci in range(1, 5):
                    v.tensor_copy(spcV[:], spcV[:])
                    v.scalar_tensor_tensor(smMq[:], smP[:, ci * 128:(ci + 1) * 128],
                                           wmb[:, ci:ci + 1], smMq[:],
                                           op0=ALU.mult, op1=ALU.add)
                v.tensor_copy(spcV[:], spcV[:])
            barrier()

        def p1x():
            d0, t0, v0, s0 = dG.n, tA.n, vA.n, dS.n
            xtm = (A[:, 32768:34816], A[:, 34816:36864], A[:, 36864:38912])
            ev = (B[:, 0:2048], B[:, 2048:4096])

            @sect("gpsimd")
            def _(g):
                for ec in range(ECH):
                    if ec >= 3:
                        g.wait_ge(tA.h, t0 + ec - 2)
                    g.dma_start(out=xtm[ec % 3][:], in_=xe[ec * 128:(ec + 1) * 128, :]).then_inc(dG.h, 16)
                    dG.n += 16
                g.wait_ge(tA.h, t0 + ECH - 2)
                g.dma_start(out=A[:, 38912:39424], in_=xe[0:128, 0:512]).then_inc(dG.h, 16)
                dG.n += 16

            @sect("tensor")
            def _(t):
                for ec in range(ECH):
                    t.wait_ge(dG.h, d0 + 16 * min(ec + 2, ECH + 1))
                    if ec >= 1:
                        t.wait_ge(vA.h, v0 + ec)
                    for i in range(16):
                        t.matmul(PS0[:, i * 128:(i + 1) * 128], xtm[ec % 3][:, i * 128:(i + 1) * 128],
                                 smI[:], start=True, stop=True)
                    t.drain().then_inc(tA.h, 1); tA.n += 1

            @sect("vector")
            def _(v):
                for ec in range(ECH):
                    v.wait_ge(tA.h, t0 + ec + 1)
                    if ec >= 2:
                        v.wait_ge(dS.h, s0 + 16 * (ec - 1))
                    for i in range(16):
                        v.tensor_copy(ev[ec % 2][:, i * 128:(i + 1) * 128], PS0[:, i * 128:(i + 1) * 128])
                    v.drain().then_inc(vA.h, 1); vA.n += 1

            @sect("sync")
            def _(s):
                for ec in range(ECH):
                    s.wait_ge(vA.h, v0 + ec + 1)
                    s.dma_start(out=xfmD[:, :, ec * 128:(ec + 1) * 128],
                                in_=ev[ec % 2].rearrange("p (i t) -> p i t", i=16)).then_inc(dS.h, 16)
                    dS.n += 16
                s.wait_ge(dS.h, dS.n)
            barrier()

        def proj(kind):
            grps = QGRP if kind == "q" else EGRP
            NG = len(grps)
            wsrc = {"q": wqT, "k": wkT, "v": wvT}[kind]
            brow = {"q": rbq, "k": rbk, "v": rbv}[kind]
            outD = {"q": kqD, "k": kkD, "v": vtD}[kind]
            d0, tA0, tB0, tC0, tD0 = dG.n, tA.n, tB.n, tC.n, tD.n
            vB0, vC0 = vB.n, vC.n
            sA0, sB0, sC0 = sA.n, sB.n, sC.n
            gA0 = gA.n
            wA = A[:, 0:32768]
            xf = (A[:, 32768:40960], A[:, 40960:49152])
            qp = B[:, 0:8192]
            sq = (B[:, 8192:8704], B[:, 8704:9216])
            ev = (B[:, 9216:10240], B[:, 10240:11264])
            vev = (B[:, 8704:10752], B[:, 10752:12800])
            rkb = Fb[:, 0:512]

            nst = [(16 if kind != "v" else W // 128) for (c0, W) in grps]
            LOADS_DONE = {}
            STORE_DONE = {}
            cnt = dS.n
            stu = 0
            for gi in range(NG):
                cnt += 16 * 16
                LOADS_DONE[gi] = cnt
                if gi >= 1:
                    for _ in range(nst[gi - 1]):
                        cnt += 16
                        stu += 1
                        STORE_DONE[stu - 1] = cnt
            for _ in range(nst[NG - 1]):
                cnt += 16
                stu += 1
                STORE_DONE[stu - 1] = cnt

            @sect("gpsimd")
            def _(g):
                for i in range(16):
                    g.dma_start(out=wA[:, i * 2048:(i + 1) * 2048],
                                in_=wsrc[i * 128:(i + 1) * 128, :]).then_inc(dG.h, 16)
                    dG.n += 16

            @sect("sync")
            def _(s):
                stu2 = 0
                for gi, (c0, W) in enumerate(grps):
                    if gi >= 2:
                        s.wait_ge(tA.h, tA0 + gi - 1)
                    base = (OFF + c0) if kind == "q" else c0
                    for i in range(16):
                        s.dma_start(out=xf[gi % 2][:, i * 512:i * 512 + W],
                                    in_=xfmD[:, i, base:base + W]).then_inc(dS.h, 16)
                        dS.n += 16
                    if gi >= 1:
                        c0p, Wp = grps[gi - 1]
                        for u in range(nst[gi - 1]):
                            s.wait_ge(vB.h, vB0 + stu2 + 1)
                            if kind == "q":
                                s.dma_start(out=kqD[:, u, c0p:c0p + Wp], in_=ev[u % 2][:, 0:Wp]).then_inc(dS.h, 16)
                            elif kind == "k":
                                s.dma_start(out=kkD[:, u, c0p:c0p + Wp], in_=ev[u % 2][:, 0:Wp]).then_inc(dS.h, 16)
                            else:
                                mc = c0p + u * 128
                                s.dma_start(out=vtD[mc:mc + 128, :], in_=vev[u % 2][:, 0:2048]).then_inc(dS.h, 16)
                            dS.n += 16
                            stu2 += 1
                c0p, Wp = grps[NG - 1]
                for u in range(nst[NG - 1]):
                    s.wait_ge(vB.h, vB0 + stu2 + 1)
                    if kind == "q":
                        s.dma_start(out=kqD[:, u, c0p:c0p + Wp], in_=ev[u % 2][:, 0:Wp]).then_inc(dS.h, 16)
                    elif kind == "k":
                        s.dma_start(out=kkD[:, u, c0p:c0p + Wp], in_=ev[u % 2][:, 0:Wp]).then_inc(dS.h, 16)
                    else:
                        mc = c0p + u * 128
                        s.dma_start(out=vtD[mc:mc + 128, :], in_=vev[u % 2][:, 0:2048]).then_inc(dS.h, 16)
                    dS.n += 16
                    stu2 += 1
                s.wait_ge(dS.h, dS.n)

            @sect("tensor")
            def _(t):
                midx = 0
                for gi, (c0, W) in enumerate(grps):
                    t.wait_ge(dS.h, LOADS_DONE[gi])
                    if gi == 0:
                        t.wait_ge(dG.h, d0 + 256)
                    if kind in ("q", "k"):
                        if kind == "k" and gi >= 1:
                            t.wait_ge(vB.h, vB0 + 16 * gi)
                        def sumsq_mm(o):
                            t.wait_ge(sB.h, sB0 + gi * 16 + o + 1)
                            t.matmul(PS1[0:1, 1024:1024 + W], smO[:, 0:1], sq[o % 2][:, 0:W],
                                     start=(o == 0), stop=(o == 15)).then_inc(tC.h, 1)
                            tC.n += 1
                        for o in range(16):
                            bank = PS1[:, (o % 2) * 512:(o % 2) * 512 + W]
                            j = gi * 16 + o
                            if j >= 2:
                                t.wait_ge(sA.h, sA0 + j - 1)
                            for i in range(16):
                                t.matmul(bank, wA[:, i * 2048 + o * 128:i * 2048 + (o + 1) * 128],
                                         xf[gi % 2][:, i * 512:i * 512 + W],
                                         start=(i == 0), stop=False)
                            t.matmul(bank, brow[0:1, o * 128:(o + 1) * 128], smO[0:1, 0:W],
                                     start=False, stop=True).then_inc(tB.h, 1)
                            tB.n += 1
                            if o >= 1:
                                sumsq_mm(o - 1)
                        sumsq_mm(15)
                        t.drain().then_inc(tA.h, 1); tA.n += 1
                        if kind == "k":
                            t.wait_ge(sC.h, sC0 + gi + 1)
                            t.matmul(PS0[:, 0:W], smOF[0:1, 0:128], rowF[0:1, 0:W],
                                     start=True, stop=True).then_inc(tD.h, 1)
                            tD.n += 1
                        if kind == "q":
                            for h in range(H):
                                j = gi * 16 + h
                                t.wait_ge(sA.h, sA0 + j + 1)
                                if j >= 1:
                                    t.wait_ge(vB.h, vB0 + j)
                                t.matmul(PS1[:, 1536:1536 + W], smMq[:],
                                         qp[:, h * 512:h * 512 + W],
                                         start=True, stop=True).then_inc(tD.h, 1)
                                tD.n += 1
                    else:
                        nm = W // 128
                        for m in range(nm):
                            if midx >= 1:
                                t.wait_ge(vB.h, vB0 + midx)
                            for og in range(4):
                                seg = PS1[:, og * 512:(og + 1) * 512]
                                for i in range(16):
                                    t.matmul(seg, xf[gi % 2][:, i * 512 + m * 128:i * 512 + (m + 1) * 128],
                                             wA[:, i * 2048 + og * 512:i * 2048 + (og + 1) * 512],
                                             start=(i == 0), stop=False)
                                t.matmul(seg, smO[0:1, 0:128], brow[0:1, og * 512:(og + 1) * 512],
                                         start=False, stop=True)
                            t.drain().then_inc(tB.h, 1); tB.n += 1
                            midx += 1
                        t.drain().then_inc(tA.h, 1); tA.n += 1

            @sect("scalar")
            def _(s):
                if kind == "v":
                    return
                gcol = gqc if kind == "q" else gkc
                for gi, (c0, W) in enumerate(grps):
                    for o in range(16):
                        s.wait_ge(tB.h, tB0 + gi * 16 + o + 1)
                        if gi * 16 + o >= 2:
                            s.wait_ge(tC.h, tC0 + gi * 16 + o - 1)
                        bank = PS1[:, (o % 2) * 512:(o % 2) * 512 + W]
                        s.activation(sq[o % 2][:, 0:W], bank, AF.Square).then_inc(sB.h, 1); sB.n += 1
                        s.activation(qp[:, o * 512:o * 512 + W], bank, AF.Copy,
                                     scale=gcol[:, o:o + 1]).then_inc(sA.h, 1); sA.n += 1
                    s.wait_ge(vC.h, vC0 + gi + 1)
                    dst = rqA[0:1, c0:c0 + W] if kind == "q" else rowF[0:1, 0:W]
                    s.activation(dst, rowF[0:1, 512:512 + W], AF.Sqrt)
                    s.copy(spcS[:], spcS[:]).then_inc(sC.h, 1); sC.n += 1

            @sect("vector")
            def _(v):
                stu2 = 0
                for gi, (c0, W) in enumerate(grps):
                    if kind in ("q", "k"):
                        v.wait_ge(tC.h, tC0 + 16 * (gi + 1))
                        if gi >= 1:
                            v.wait_ge(sC.h, sC0 + gi)
                            if kind == "k":
                                v.wait_ge(tD.h, tD0 + gi)
                        v.tensor_scalar(rowF[0:1, 0:W],
                                        PS1[0:1, 1024:1024 + W],
                                        1.0 / 2048.0, EPS_RMS, op0=ALU.mult, op1=ALU.add)
                        v.tensor_copy(spcV[:], spcV[:])
                        v.reciprocal(rowF[0:1, 512:512 + W], rowF[0:1, 0:W])
                        v.tensor_copy(spcV[:], spcV[:]).then_inc(vC.h, 1)
                        vC.n += 1
                    if kind == "q":
                        for h in range(H):
                            j = gi * 16 + h
                            v.wait_ge(tD.h, tD0 + j + 1)
                            if stu2 >= 2:
                                v.wait_ge(dS.h, STORE_DONE[stu2 - 2])
                            v.tensor_copy(ev[h % 2][:, 0:W], PS1[:, 1536:1536 + W]).then_inc(vB.h, 1)
                            vB.n += 1
                            stu2 += 1
                    elif kind == "k":
                        v.wait_ge(tD.h, tD0 + gi + 1)
                        v.tensor_copy(rkb[:, 0:W], PS0[:, 0:W])
                        v.tensor_copy(spcV[:], spcV[:])
                        for o in range(16):
                            j = gi * 16 + o
                            if stu2 >= 2:
                                v.wait_ge(dS.h, STORE_DONE[stu2 - 2])
                            v.scalar_tensor_tensor(ev[o % 2][:, 0:W], qp[:, o * 512:o * 512 + W],
                                                   1.0, rkb[:, 0:W],
                                                   op0=ALU.mult, op1=ALU.mult).then_inc(vB.h, 1)
                            vB.n += 1
                            stu2 += 1
                    else:
                        nm = W // 128
                        for m in range(nm):
                            v.wait_ge(tB.h, tB0 + stu2 + 1)
                            if stu2 >= 2:
                                v.wait_ge(dS.h, STORE_DONE[stu2 - 2])
                            v.tensor_copy(vev[m % 2][:, 0:2048], PS1[:, 0:2048]).then_inc(vB.h, 1)
                            vB.n += 1
                            stu2 += 1
            barrier()

        def p1b1():
            dG0, dS0 = dG.n, dS.n
            tB0, tC0, tD0 = tB.n, tC.n, tD.n
            vA0, vB0, vC0 = vA.n, vB.n, vC.n
            sA0, sB0 = sA.n, sB.n
            kr = A[:, 0:24576]
            kr12 = B[:, 12544:14592]
            vr = A[:, 24576:49152]

            def kslot(ec):
                sl = ec % 13
                return kr12 if sl == 12 else kr[:, sl * 2048:(sl + 1) * 2048]
            qsb = (B[:, 0:2048], B[:, 2048:4096])
            aosb = (B[:, 4096:6144], B[:, 6144:8192])
            exps = (B[:, 8192:9216], B[:, 9216:10240])
            ptr = (B[:, 10240:11264], B[:, 11264:12288])
            diag = (B[:, 12288:12416], B[:, 12416:12544])
            nmx = (Fb[:, 0:1], Fb[:, 1:2])
            ebv = (Fb[:, 2:3], Fb[:, 3:4])
            zr = (Fb[:, 4:5], Fb[:, 5:6])
            rz = (Fb[:, 6:7], Fb[:, 7:8])
            mb = (Fb[:, 16:1040], Fb[:, 1040:2064])

            def new_ecs(lc):
                return list(range(12)) if lc == 0 else [lc + 11]

            LOADS_DONE = {}
            AOST_DONE = {}
            VR_DONE = {}
            cnt = dS.n
            cntg = dG.n
            for lc in range(LCH):
                cnt += 16 * (len(new_ecs(lc)) + 1)
                LOADS_DONE[lc] = cnt
                cntg += 16 * len(new_ecs(lc))
                VR_DONE[lc] = cntg
                if lc >= 1:
                    cnt += 16
                    AOST_DONE[lc - 1] = cnt
            cnt += 16
            AOST_DONE[LCH - 1] = cnt

            @sect("gpsimd")
            def _(g):
                for lc in range(LCH):
                    if lc >= 1:
                        g.wait_ge(tD.h, tD0 + 16 * lc)
                    for ec in new_ecs(lc):
                        slot = ec % 12
                        g.dma_start(out=vr[:, slot * 2048:(slot + 1) * 2048],
                                    in_=vtD[ec * 128:(ec + 1) * 128, :]).then_inc(dG.h, 16)
                        dG.n += 16
                if DEBUG:
                    g.dma_start(out=rqDbg, in_=rqA[0:1, 0:2048]).then_inc(dG.h, 16)
                    dG.n += 16
                    g.wait_ge(vC.h, vC.n + 1)
                    g.dma_start(out=rqSDbg, in_=rqS[:]).then_inc(dG.h, 16)
                    dG.n += 16

            @sect("tensor")
            def _(t):
                for c in range(16):
                    t.matmul(PS1[:, 1024 + c:1024 + c + 1], rqA[0:1, c * 128:(c + 1) * 128],
                             smOF[0:1, 0:1], start=True, stop=True)
                t.drain().then_inc(tA.h, 1); tA.n += 1

            @sect("vector")
            def _(v):
                v.wait_ge(tA.h, tA.n)
                v.tensor_copy(rqT[:, 0:16], PS1[:, 1024:1040])
                v.tensor_copy(spcV[:], spcV[:])
                v.tensor_scalar(rqS[:], rqT[:], SCALE, None, op0=ALU.mult)
                v.tensor_copy(spcV[:], spcV[:]).then_inc(vC.h, 1)
                vC.n += 1

            @sect("sync")
            def _(s):
                for lc in range(LCH):
                    if lc >= 2:
                        s.wait_ge(tB.h, tB0 + 16 * (lc - 1))
                    for ec in new_ecs(lc):
                        kb = kslot(ec)
                        s.dma_start(out=kb.rearrange("p (h t) -> p h t", h=16),
                                    in_=kkD[:, :, ec * 128:(ec + 1) * 128]).then_inc(dS.h, 16)
                        dS.n += 16
                    s.dma_start(out=qsb[lc % 2].rearrange("p (h t) -> p h t", h=16),
                                in_=kqD[:, :, lc * 128:(lc + 1) * 128]).then_inc(dS.h, 16)
                    dS.n += 16
                    if lc >= 1:
                        s.wait_ge(vB.h, vB0 + 16 * lc)
                        s.dma_start(out=aoD[:, :, (lc - 1) * 128:lc * 128],
                                    in_=aosb[(lc - 1) % 2].rearrange("p (h t) -> p h t", h=16)).then_inc(dS.h, 16)
                        dS.n += 16
                s.wait_ge(vB.h, vB0 + 16 * LCH)
                s.dma_start(out=aoD[:, :, (LCH - 1) * 128:LCH * 128],
                            in_=aosb[(LCH - 1) % 2].rearrange("p (h t) -> p h t", h=16)).then_inc(dS.h, 16)
                dS.n += 16
                s.wait_ge(dS.h, dS.n)

            sC0 = sC.n

            @sect("tensor")
            def _(t):
                NJ = LCH * H

                def band(j):
                    lc, h = divmod(j, H)
                    ecs = [lc + 8 + db for db in DB]
                    if h == 0:
                        t.wait_ge(dS.h, LOADS_DONE[lc])
                    half = PS0[:, (h % 2) * 1024:(h % 2) * 1024 + 1024]
                    if j >= 2:
                        t.wait_ge(vA.h, vA0 + j - 1)
                    for bi, ec in enumerate(ecs):
                        kb = kslot(ec)
                        t.matmul(half[:, bi * 128:(bi + 1) * 128],
                                 qsb[lc % 2][:, h * 128:(h + 1) * 128],
                                 kb[:, h * 128:(h + 1) * 128],
                                 start=True, stop=True)
                    t.drain().then_inc(tB.h, 1); tB.n += 1

                band(0)
                for j in range(NJ):
                    lc, h = divmod(j, H)
                    ecs = [lc + 8 + db for db in DB]
                    if j + 1 < NJ:
                        band(j + 1)
                    t.wait_ge(vC.h, vC0 + 1 + j + 1)
                    if j >= 1:
                        t.wait_ge(sC.h, sC0 + j)
                    for bi in range(8):
                        t.matmul(PS1[:, bi * 128:(bi + 1) * 128],
                                 exps[j % 2][:, bi * 128:(bi + 1) * 128], smI[:],
                                 start=True, stop=True)
                    t.drain().then_inc(tC.h, 1); tC.n += 1
                    t.wait_ge(sC.h, sC0 + j + 1)
                    if j >= 1:
                        t.wait_ge(vB.h, vB0 + j)
                    if h == 0:
                        t.wait_ge(dG.h, VR_DONE[lc])
                    for bi, ec in enumerate(ecs):
                        slot = ec % 12
                        t.matmul(PS1[:, 1024:1152],
                                 vr[:, slot * 2048 + h * 128:slot * 2048 + (h + 1) * 128],
                                 ptr[j % 2][:, bi * 128:(bi + 1) * 128],
                                 start=(bi == 0), stop=(bi == 7))
                    t.drain().then_inc(tD.h, 1); tD.n += 1

            @sect("vector")
            def _(v):
                def ao_evac(j):
                    lc, h = divmod(j, H)
                    v.wait_ge(tD.h, tD0 + j + 1)
                    if lc >= 2 and h == 0:
                        v.wait_ge(dS.h, AOST_DONE[lc - 2])
                    v.tensor_copy(aosb[lc % 2][:, h * 128:(h + 1) * 128],
                                  PS1[:, 1024:1152]).then_inc(vB.h, 1)
                    vB.n += 1
                for lc in range(LCH):
                    for h in range(H):
                        j = lc * 16 + h
                        v.wait_ge(tB.h, tB0 + j + 1)
                        if j >= 2:
                            v.wait_ge(sA.h, sA0 + j - 1)
                        half = PS0[:, (h % 2) * 1024:(h % 2) * 1024 + 1024]
                        v.scalar_tensor_tensor(mb[j % 2][:], half, 1.0, smM[:],
                                               op0=ALU.mult, op1=ALU.add)
                        v.tensor_reduce(nmx[j % 2][:, 0:1], mb[j % 2][:], axis=AX.X,
                                        op=ALU.max, negate=True)
                        v.tensor_copy(spcV[:], spcV[:])
                        v.tensor_tensor(ebv[j % 2][:, 0:1], nmx[j % 2][:, 0:1],
                                        rqS[:, lc:lc + 1], op=ALU.mult)
                        v.tensor_copy(spcV[:], spcV[:]).then_inc(vA.h, 1)
                        vA.n += 1
                        v.wait_ge(sA.h, sA0 + j + 1)
                        v.tensor_reduce(zr[j % 2][:, 0:1], exps[j % 2][:], axis=AX.X, op=ALU.add)
                        v.tensor_copy(spcV[:], spcV[:])
                        v.reciprocal(rz[j % 2][:, 0:1], zr[j % 2][:, 0:1])
                        v.tensor_copy(spcV[:], spcV[:])
                        v.tensor_scalar(exps[j % 2][:], exps[j % 2][:], rz[j % 2][:, 0:1],
                                        None, op0=ALU.mult)
                        v.tensor_copy(spcV[:], spcV[:]).then_inc(vC.h, 1)
                        vC.n += 1
                        if j >= 1:
                            ao_evac(j - 1)
                ao_evac(LCH * H - 1)

            @sect("scalar")
            def _(s):
                def ptr_evac(j):
                    s.wait_ge(tC.h, tC0 + j + 1)
                    if j >= 1:
                        s.wait_ge(tD.h, tD0 + j - 1)
                    s.activation(ptr[j % 2][:], PS1[:, 0:1024], AF.Copy)
                    s.copy(spcS[:], spcS[:]).then_inc(sC.h, 1); sC.n += 1
                for lc in range(LCH):
                    for h in range(H):
                        j = lc * 16 + h
                        s.wait_ge(vA.h, vA0 + j + 1)
                        if j >= 2:
                            s.wait_ge(tC.h, tC0 + j - 1)
                        s.activation(exps[j % 2][:], mb[j % 2][:], AF.Exp, bias=ebv[j % 2][:, 0:1],
                                     scale=rqS[:, lc:lc + 1])
                        s.copy(spcS[:], spcS[:]).then_inc(sA.h, 1); sA.n += 1
                        if j >= 1:
                            ptr_evac(j - 1)
                ptr_evac(LCH * H - 1)
            barrier()

        def p1b2():
            dG0, dS0, dC0 = dG.n, dS.n, dC.n
            tB0, vA0, vB0, vC0, sA0 = tB.n, vA.n, vB.n, vC.n, sA.n
            wA = A[:, 0:32768]
            aold = (B[:, 0:2048], B[:, 2048:4096])
            dump = B[:, 4096:6144]
            xr = (Fb[:, 0:2048], Fb[:, 2048:4096])
            xsum = Fb[:, 4096:6144]
            s1c = stats[:, 0:1]
            s2c = stats[:, 1:2]
            mu = stats[:, 2:3]
            iv = stats[:, 3:4]
            rstd = stats[:, 4:5]
            nmr = stats[:, 5:6]

            @sect("gpsimd")
            def _(g):
                for i in range(16):
                    g.dma_start(out=wA[:, i * 2048:(i + 1) * 2048],
                                in_=woT[i * 128:(i + 1) * 128, :]).then_inc(dG.h, 16)
                    dG.n += 16

            @sect("sync")
            def _(s):
                for lc in range(LCH):
                    if lc >= 2:
                        s.wait_ge(tB.h, tB0 + lc - 1)
                    s.dma_start(out=aold[lc % 2].rearrange("p (h t) -> p h t", h=16),
                                in_=aoD[:, :, lc * 128:(lc + 1) * 128]).then_inc(dS.h, 16)
                    dS.n += 16
                    if lc >= 2:
                        s.wait_ge(vA.h, vA0 + lc - 1)
                    s.dma_start(out=xr[lc % 2][:],
                                in_=xe[OFF + lc * 128:OFF + (lc + 1) * 128, :]).then_inc(dS.h, 16)
                    dS.n += 16
                s.wait_ge(dS.h, dS.n)

            @sect("tensor")
            def _(t):
                t.wait_ge(dG.h, dG0 + 256)
                for lc in range(LCH):
                    t.wait_ge(dS.h, dS0 + 32 * lc + 16)
                    PSx = PS0 if lc % 2 == 0 else PS1
                    if lc >= 2:
                        t.wait_ge(vA.h, vA0 + lc - 1)
                    for hh in range(16):
                        for og in range(4):
                            t.matmul(PSx[:, og * 512:(og + 1) * 512],
                                     aold[lc % 2][:, hh * 128:(hh + 1) * 128],
                                     wA[:, hh * 2048 + og * 512:hh * 2048 + (og + 1) * 512],
                                     start=(hh == 0), stop=False)
                    for og in range(4):
                        t.matmul(PSx[:, og * 512:(og + 1) * 512], smO[0:1, 0:128],
                                 rbo[0:1, og * 512:(og + 1) * 512], start=False, stop=True)
                    t.drain().then_inc(tB.h, 1); tB.n += 1

            @sect("vector")
            def _(v):
                for lc in range(LCH):
                    v.wait_ge(tB.h, tB0 + lc + 1)
                    v.wait_ge(dS.h, dS0 + 32 * (lc + 1))
                    if lc >= 1:
                        v.wait_ge(dC.h, dC0 + 16 * lc)
                    PSx = PS0 if lc % 2 == 0 else PS1
                    v.tensor_tensor(xsum[:], PSx[:, 0:2048], xr[lc % 2][:], op=ALU.add)
                    v.drain().then_inc(vA.h, 1); vA.n += 1
                    v.tensor_reduce(s1c, xsum[:], axis=AX.X, op=ALU.add)
                    v.wait_ge(sA.h, sA0 + 2 * lc + 1)
                    v.tensor_reduce(s2c, dump[:], axis=AX.X, op=ALU.add)
                    v.tensor_copy(spcV[:], spcV[:])
                    v.tensor_scalar(mu, s1c, -1.0 / 2048.0, None, op0=ALU.mult)
                    v.tensor_scalar(iv, s2c, 1.0 / 2048.0, EPS_LN, op0=ALU.mult, op1=ALU.add)
                    v.tensor_tensor(nmr, mu, mu, op=ALU.mult)
                    v.tensor_copy(spcV[:], spcV[:])
                    v.tensor_tensor(iv, iv, nmr, op=ALU.subtract)
                    v.tensor_copy(spcV[:], spcV[:])
                    v.reciprocal(iv, iv)
                    v.tensor_copy(spcV[:], spcV[:]).then_inc(vC.h, 1); vC.n += 1
                    v.wait_ge(sA.h, sA0 + 2 * lc + 2)
                    v.tensor_scalar(xsum[:], xsum[:], rstd[:, 0:1], nmr[:, 0:1],
                                    op0=ALU.mult, op1=ALU.add).then_inc(vB.h, 1)
                    vB.n += 1

            @sect("scalar")
            def _(s):
                for lc in range(LCH):
                    s.wait_ge(vA.h, vA0 + lc + 1)
                    s.activation(dump[:], xsum[:], AF.Square)
                    s.copy(spcS[:], spcS[:]).then_inc(sA.h, 1)
                    sA.n += 1
                    s.wait_ge(vC.h, vC0 + lc + 1)
                    s.activation(rstd, iv, AF.Sqrt)
                    s.copy(spcS[:], spcS[:])
                    s.mul(nmr, rstd, mu[:, 0:1])
                    s.copy(spcS[:], spcS[:]).then_inc(sA.h, 1); sA.n += 1
                    s.wait_ge(vB.h, vB0 + lc + 1)
                    s.dma_start(out=x1D[lc * 128:(lc + 1) * 128, :], in_=xsum[:]).then_inc(dC.h, 16)
                    dC.n += 16
                s.wait_ge(dC.h, dC.n)
            barrier()

        def p2a():
            dG0, dS0 = dG.n, dS.n
            tA0, tB0, vA0, vB0, vC0, sA0 = tA.n, tB.n, vA.n, vB.n, vC.n, sA.n
            x1fm = A[:, 0:32768]
            wgr = (A[:, 32768:34816], A[:, 34816:36864])
            wur = (A[:, 36864:38912], A[:, 38912:40960])
            wgc = (A[:, 40960:43008], A[:, 43008:45056])
            wuc = (A[:, 45056:47104], A[:, 47104:49152])
            xtm = (B[:, 0:2048], B[:, 2048:4096], B[:, 4096:6144])
            gsb = B[:, 6144:6656]
            usb = B[:, 6656:7168]
            hsb = (B[:, 7168:7680], B[:, 7680:8192])

            @sect("gpsimd")
            def _(g):
                for lc in range(LCH):
                    if lc >= 3:
                        g.wait_ge(tA.h, tA0 + lc - 2)
                    g.dma_start(out=xtm[lc % 3][:], in_=x1D[lc * 128:(lc + 1) * 128, :]).then_inc(dG.h, 16)
                    dG.n += 16
                for o in range(NHC):
                    if o >= 2:
                        g.wait_ge(vA.h, vA0 + 2 * (o - 1))
                    g.dma_start(out=wgr[o % 2][:],
                                in_=gTb[o].rearrange("(i p) c -> p i c", p=128)).then_inc(dG.h, 16)
                    dG.n += 16
                    g.dma_start(out=wur[o % 2][:],
                                in_=uTb[o].rearrange("(i p) c -> p i c", p=128)).then_inc(dG.h, 16)
                    dG.n += 16
                g.wait_ge(tA.h, tA0 + LCH)
                for _ in range(2):
                    g.dma_start(out=xtm[0][:, 0:512], in_=x1D[0:128, 0:512]).then_inc(dG.h, 16)
                    dG.n += 16

            @sect("tensor")
            def _(t):
                for lc in range(LCH):
                    t.wait_ge(dG.h, dG0 + 16 * min(lc + 2, LCH + 2))
                    if lc >= 1:
                        t.wait_ge(vB.h, vB0 + lc)
                    for i in range(16):
                        t.matmul(PS1[:, i * 128:(i + 1) * 128], xtm[lc % 3][:, i * 128:(i + 1) * 128],
                                 smI[:], start=True, stop=True)
                    t.drain().then_inc(tA.h, 1); tA.n += 1
                for o in range(NHC):
                    t.wait_ge(vA.h, vA0 + 2 * (o + 1))
                    if o == 0:
                        t.wait_ge(vB.h, vB0 + LCH)
                    for gi in range(4):
                        j = o * 4 + gi
                        bb = (gi % 2) * 1024
                        if j >= 2:
                            t.wait_ge(sA.h, sA0 + j - 1)
                            t.wait_ge(vC.h, vC0 + j - 1)
                        for i in range(16):
                            t.matmul(PS0[:, bb:bb + 512],
                                     wgc[o % 2][:, i * 128:(i + 1) * 128],
                                     x1fm[:, i * 2048 + gi * 512:i * 2048 + (gi + 1) * 512],
                                     start=(i == 0), stop=(i == 15))
                        for i in range(16):
                            t.matmul(PS0[:, bb + 512:bb + 1024],
                                     wuc[o % 2][:, i * 128:(i + 1) * 128],
                                     x1fm[:, i * 2048 + gi * 512:i * 2048 + (gi + 1) * 512],
                                     start=(i == 0), stop=(i == 15))
                        t.drain().then_inc(tB.h, 1); tB.n += 1

            @sect("vector")
            def _(v):
                for lc in range(LCH):
                    v.wait_ge(tA.h, tA0 + lc + 1)
                    for i in range(16):
                        v.tensor_copy(x1fm[:, i * 2048 + lc * 128:i * 2048 + (lc + 1) * 128],
                                      PS1[:, i * 128:(i + 1) * 128])
                    v.drain().then_inc(vB.h, 1); vB.n += 1
                for o in range(NHC):
                    v.wait_ge(dG.h, dG0 + 16 * LCH + 32 * min(o + 2, NHC + 1))
                    if o >= 2:
                        v.wait_ge(tB.h, tB0 + 4 * (o - 1))
                    v.tensor_scalar(wgc[o % 2][:], wgr[o % 2][:], -128.0, None, op0=ALU.add)
                    v.tensor_scalar(wuc[o % 2][:], wur[o % 2][:], -128.0, None, op0=ALU.add)
                    v.tensor_copy(spcV[:], spcV[:]).then_inc(vA.h, 2)
                    vA.n += 2
                    for gi in range(4):
                        j = o * 4 + gi
                        v.wait_ge(tB.h, tB0 + j + 1)
                        v.tensor_scalar(usb[:, 0:512], PS0[:, (gi % 2) * 1024 + 512:(gi % 2) * 1024 + 1024],
                                        suc[:, o:o + 1], None, op0=ALU.mult)
                        v.wait_ge(sA.h, sA0 + j + 1)
                        if j >= 2:
                            v.wait_ge(dS.h, dS0 + 16 * (j - 1))
                        v.tensor_tensor(hsb[j % 2][:, 0:512], gsb[:, 0:512],
                                        usb[:, 0:512], op=ALU.mult).then_inc(vC.h, 1)
                        vC.n += 1

            @sect("scalar")
            def _(s):
                for o in range(NHC):
                    for gi in range(4):
                        j = o * 4 + gi
                        s.wait_ge(tB.h, tB0 + j + 1)
                        if j >= 1:
                            s.wait_ge(vC.h, vC0 + j)
                        s.activation(gsb[:, 0:512], PS0[:, (gi % 2) * 1024:(gi % 2) * 1024 + 512],
                                     AF.Silu, scale=sgc[:, o:o + 1]).then_inc(sA.h, 1)
                        sA.n += 1

            @sect("sync")
            def _(s):
                for o in range(NHC):
                    for gi in range(4):
                        j = o * 4 + gi
                        s.wait_ge(vC.h, vC0 + j + 1)
                        s.dma_start(out=hD[o, :, gi * 512:(gi + 1) * 512],
                                    in_=hsb[j % 2][:, 0:512]).then_inc(dS.h, 16)
                        dS.n += 16
                s.wait_ge(dS.h, dS.n)
            barrier()

        def p2b():
            dG0, dS0, dC0 = dG.n, dS.n, dC.n
            tA0, tB0, vA0, vB0 = tA.n, tB.n, vA.n, vB.n
            dTq = A[:, 0:22528]
            hring = A[:, 22528:26624]
            wdr = tuple(B[:, i * 512:(i + 1) * 512] for i in range(4))
            wdum = B[:, 2048:2560]
            fev = Fb[:, 0:2048]

            @sect("gpsimd")
            def _(g):
                for qd in range(4):
                    for o in range(NHC):
                        j = qd * NHC + o
                        if j >= 4:
                            g.wait_ge(vA.h, vA0 + j - 3)
                        g.dma_start(out=wdr[j % 4][:],
                                    in_=dT[o * 128:(o + 1) * 128, qd * 512:(qd + 1) * 512]).then_inc(dG.h, 16)
                        dG.n += 16
                for _ in range(2):
                    g.dma_start(out=wdum[:], in_=dT[0:128, 0:512]).then_inc(dG.h, 16)
                    dG.n += 16

            @sect("vector")
            def _(v):
                for qd in range(4):
                    for o in range(NHC):
                        j = qd * NHC + o
                        v.wait_ge(dG.h, dG0 + 16 * min(j + 3, 4 * NHC + 2))
                        if qd >= 1 and o == 0:
                            v.wait_ge(tA.h, tA0 + 4 * qd)
                        v.scalar_tensor_tensor(dTq[:, o * 512:(o + 1) * 512], wdr[j % 4][:],
                                               -128.0, sdb[:, qd * 512:(qd + 1) * 512],
                                               op0=ALU.add, op1=ALU.mult)
                        if o == NHC - 1:
                            v.tensor_copy(spcV[:], spcV[:])
                            v.tensor_copy(spcV[:], spcV[:])
                            v.tensor_copy(spcV[:], spcV[:])
                        v.tensor_copy(spcV[:], spcV[:]).then_inc(vA.h, 1)
                        vA.n += 1
                    for gi in range(4):
                        k = qd * 4 + gi
                        v.wait_ge(tA.h, tA0 + k + 1)
                        if k >= 1:
                            v.wait_ge(dC.h, dC0 + 64 * k)
                        PSx = PS0 if gi % 2 == 0 else PS1
                        for lci in range(4):
                            v.tensor_copy(fev[:, lci * 512:(lci + 1) * 512],
                                          PSx[:, lci * 512:(lci + 1) * 512])
                        v.drain().then_inc(vB.h, 1); vB.n += 1

            @sect("sync")
            def _(s):
                nh = 0
                for qd in range(4):
                    for gi in range(4):
                        for o in range(NHC):
                            nh += 1
                            if nh > 8:
                                s.wait_ge(tB.h, tB0 + nh - 8)
                            s.dma_start(out=hring[:, ((nh - 1) % 8) * 512:((nh - 1) % 8 + 1) * 512],
                                        in_=hD[o, :, gi * 512:(gi + 1) * 512]).then_inc(dS.h, 16)
                            dS.n += 16

            @sect("tensor")
            def _(t):
                nh = 0
                for qd in range(4):
                    for gi in range(4):
                        k = qd * 4 + gi
                        PSx = PS0 if gi % 2 == 0 else PS1
                        if k >= 2:
                            t.wait_ge(vB.h, vB0 + k - 1)
                        for o in range(NHC):
                            nh += 1
                            t.wait_ge(dS.h, dS0 + 16 * nh)
                            t.wait_ge(vA.h, vA0 + min(qd * NHC + o + 2, (qd + 1) * NHC))
                            slot = (nh - 1) % 8
                            for lci in range(4):
                                t.matmul(PSx[:, lci * 512:(lci + 1) * 512],
                                         hring[:, slot * 512 + lci * 128:slot * 512 + (lci + 1) * 128],
                                         dTq[:, o * 512:(o + 1) * 512],
                                         start=(o == 0), stop=(o == NHC - 1))
                            t.drain().then_inc(tB.h, 1); tB.n += 1
                        t.drain().then_inc(tA.h, 1); tA.n += 1

            @sect("scalar")
            def _(s):
                for qd in range(4):
                    for gi in range(4):
                        k = qd * 4 + gi
                        s.wait_ge(vB.h, vB0 + k + 1)
                        for lci in range(4):
                            r0 = (gi * 4 + lci) * 128
                            s.dma_start(out=fD[r0:r0 + 128, qd * 512:(qd + 1) * 512],
                                        in_=fev[:, lci * 512:(lci + 1) * 512]).then_inc(dC.h, 16)
                            dC.n += 16
                s.wait_ge(dC.h, dC.n)
            barrier()

        def p2c():
            dS0, dC0 = dS.n, dC.n
            vA0, vB0, vC0, sA0 = vA.n, vB.n, vC.n, sA.n
            fsb = (Fb[:, 0:2048], Fb[:, 2048:4096])
            xsum = Fb[:, 4096:6144]
            dump = B[:, 0:2048]
            s1c = stats[:, 0:1]
            s2c = stats[:, 1:2]
            mu = stats[:, 2:3]
            iv = stats[:, 3:4]
            rstd = stats[:, 4:5]
            nmr = stats[:, 5:6]

            @sect("sync")
            def _(s):
                for lc in range(LCH):
                    if lc >= 1:
                        s.wait_ge(dC.h, dC0 + 16 * lc)
                    s.dma_start(out=xsum[:], in_=x1D[lc * 128:(lc + 1) * 128, :]).then_inc(dS.h, 16)
                    dS.n += 16
                    if lc >= 2:
                        s.wait_ge(vC.h, vC0 + lc - 1)
                    s.dma_start(out=fsb[lc % 2][:], in_=fD[lc * 128:(lc + 1) * 128, :]).then_inc(dS.h, 16)
                    dS.n += 16

            @sect("vector")
            def _(v):
                for lc in range(LCH):
                    v.wait_ge(dS.h, dS0 + 32 * (lc + 1))
                    v.tensor_tensor(xsum[:], xsum[:], fsb[lc % 2][:], op=ALU.add)
                    v.drain().then_inc(vC.h, 1); vC.n += 1
                    v.tensor_reduce(s1c, xsum[:], axis=AX.X, op=ALU.add)
                    v.wait_ge(sA.h, sA0 + 2 * lc + 1)
                    v.tensor_reduce(s2c, dump[:], axis=AX.X, op=ALU.add)
                    v.tensor_copy(spcV[:], spcV[:])
                    v.tensor_scalar(mu, s1c, -1.0 / 2048.0, None, op0=ALU.mult)
                    v.tensor_scalar(iv, s2c, 1.0 / 2048.0, EPS_LN, op0=ALU.mult, op1=ALU.add)
                    v.tensor_tensor(nmr, mu, mu, op=ALU.mult)
                    v.tensor_copy(spcV[:], spcV[:])
                    v.tensor_tensor(iv, iv, nmr, op=ALU.subtract)
                    v.tensor_copy(spcV[:], spcV[:])
                    v.reciprocal(iv, iv)
                    v.tensor_copy(spcV[:], spcV[:]).then_inc(vB.h, 1); vB.n += 1
                    v.wait_ge(sA.h, sA0 + 2 * lc + 2)
                    v.tensor_scalar(xsum[:], xsum[:], rstd[:, 0:1], nmr[:, 0:1],
                                    op0=ALU.mult, op1=ALU.add).then_inc(vA.h, 1)
                    vA.n += 1

            @sect("scalar")
            def _(s):
                for lc in range(LCH):
                    s.wait_ge(vC.h, vC0 + lc + 1)
                    s.activation(dump[:], xsum[:], AF.Square)
                    s.copy(spcS[:], spcS[:]).then_inc(sA.h, 1)
                    sA.n += 1
                    s.wait_ge(vB.h, vB0 + lc + 1)
                    s.activation(rstd, iv, AF.Sqrt)
                    s.copy(spcS[:], spcS[:])
                    s.mul(nmr, rstd, mu[:, 0:1])
                    s.copy(spcS[:], spcS[:]).then_inc(sA.h, 1); sA.n += 1
                    s.wait_ge(vA.h, vA0 + lc + 1)
                    s.dma_start(out=y[lc * 128:(lc + 1) * 128, :], in_=xsum[:]).then_inc(dC.h, 16)
                    dC.n += 16
                s.wait_ge(dC.h, dC.n)
            barrier()

        p0()
        p1x()
        proj("q")
        proj("k")
        proj("v")
        p1b1()
        p1b2()
        p2a()
        p2b()
        p2c()
    return nc


def _prep(inputs):
    x = np.asarray(inputs["L0_attn_x"], dtype=np.float32)
    shared = {
        "wqT": np.ascontiguousarray(np.asarray(inputs["L0_attn_wq"], np.float32).T),
        "wkT": np.ascontiguousarray(np.asarray(inputs["L0_attn_wk"], np.float32).T),
        "wvT": np.ascontiguousarray(np.asarray(inputs["L0_attn_wv"], np.float32).T),
        "woT": np.ascontiguousarray(np.asarray(inputs["L0_attn_wo"], np.float32).T),
        "gTb": np.ascontiguousarray(
            np.asarray(inputs["L0_ffn_wq_gate"], np.int32).reshape(NHC, 128, DM).transpose(0, 2, 1)),
        "uTb": np.ascontiguousarray(
            np.asarray(inputs["L0_ffn_wq_up"], np.int32).reshape(NHC, 128, DM).transpose(0, 2, 1)),
        "dT": np.ascontiguousarray(np.asarray(inputs["L0_ffn_wq_down"], np.int32).T),
        "r_bq": np.asarray(inputs["L0_attn_bq"], np.float32).reshape(1, DM).copy(),
        "r_bk": np.asarray(inputs["L0_attn_bk"], np.float32).reshape(1, DM).copy(),
        "r_bv": np.asarray(inputs["L0_attn_bv"], np.float32).reshape(1, DM).copy(),
        "r_bo": np.asarray(inputs["L0_attn_bo"], np.float32).reshape(1, DM).copy(),
        "r_gq": np.asarray(inputs["L0_attn_gq"], np.float32).reshape(1, DM).copy(),
        "r_gk": np.asarray(inputs["L0_attn_gk"], np.float32).reshape(1, DM).copy(),
        "r_sg": np.asarray(inputs["L0_ffn_s_gate"], np.float32).reshape(1, HID).copy(),
        "r_su": np.asarray(inputs["L0_ffn_s_up"], np.float32).reshape(1, HID).copy(),
        "r_sd": np.asarray(inputs["L0_ffn_s_down"], np.float32).reshape(1, DM).copy(),
        "wmix": np.asarray(inputs["L0_attn_w_mix"], np.float32).reshape(1, 5).copy(),
        "cI": np.eye(128, dtype=np.float32).astype(bfd),
        "cMask": _mask01(),
        "cPm": _pmats(),
        "cOnes": np.ones((128, 512), dtype=np.float32).astype(bfd),
        "cOnesF": np.ones((1, 128), dtype=np.float32),
    }
    in_maps = []
    for c in range(NCORE):
        b, half = divmod(c, 2)
        p0_ = half * 2048
        idx = (p0_ - OFF + np.arange(EXT)) % 4096
        m = dict(shared)
        m["xe"] = np.ascontiguousarray(x[b, idx, :])
        in_maps.append(m)
    return in_maps


def kernel(**inputs):
    in_maps = _prep(inputs)
    nc = build_nc()
    res = run_bass_kernel_spmd(nc, in_maps, core_ids=list(range(NCORE)))
    out = np.empty((4, 4096, DM), dtype=np.float32)
    for c in range(NCORE):
        b, half = divmod(c, 2)
        out[b, half * 2048:(half + 1) * 2048, :] = res.results[c]["y"]
    return out


# version 05 (final)
# speedup vs baseline: 1.2311x; 1.0188x over previous
import numpy as np
import ml_dtypes
import concourse.bass as bass
import concourse.mybir as mybir
from concourse.bass_utils import run_bass_kernel_spmd

F32 = mybir.dt.float32
BF16 = mybir.dt.bfloat16
I32 = mybir.dt.int32
AF = mybir.ActivationFunctionType
ALU = mybir.AluOpType
AX = mybir.AxisListType
bfd = ml_dtypes.bfloat16

SHIFTS = [0, 1, -1, 3, -3, 7, -7, 20, -20, 53, -53, 141, -141, 380, -380, 1024]
CH = [0, 1, 2, 4, 8]
DB = [-8, -3, -2, -1, 0, 1, 2, 3]
H = 16
DM = 2048
HID = 5632
NHC = 44
TOK = 2048
EXT = 3456
ECH = 27
LCH = 16
OFF = 1024
EPS_RMS = 1e-6
EPS_LN = 1e-5
SCALE = 128.0 ** -0.5
NCORE = 8
DEBUG = False

EGRP = [(0, 512), (512, 512), (1024, 512), (1536, 512), (2048, 512), (2560, 512), (3072, 384)]
QGRP = [(0, 512), (512, 512), (1024, 512), (1536, 512)]


def _mask01():
    m = np.full((128, 8 * 128), -1e30, dtype=np.float32)
    ss = set(SHIFTS)
    for p in range(128):
        for bi, db in enumerate(DB):
            for q in range(128):
                if (p - q - db * 128) in ss:
                    m[p, bi * 128 + q] = 0.0
    assert ((m == 0).sum(1) == 16).all()
    return m.astype(bfd)


def _pmats():
    P = np.zeros((128, 5 * 128), dtype=np.float32)
    for ci, c in enumerate(CH):
        P[(np.arange(128) - c) % 128, ci * 128 + np.arange(128)] = 1.0
    return P.astype(bfd)


class Ct:
    def __init__(self, h):
        self.h = h
        self.n = 0


def build_nc():
    nc = bass.Bass("TRN2", target_bir_lowering=False)

    xe = nc.dram_tensor("xe", [EXT, DM], F32, kind="ExternalInput").ap()
    wqT = nc.dram_tensor("wqT", [DM, DM], F32, kind="ExternalInput").ap()
    wkT = nc.dram_tensor("wkT", [DM, DM], F32, kind="ExternalInput").ap()
    wvT = nc.dram_tensor("wvT", [DM, DM], F32, kind="ExternalInput").ap()
    woT = nc.dram_tensor("woT", [DM, DM], F32, kind="ExternalInput").ap()
    gTb = nc.dram_tensor("gTb", [NHC, DM, 128], I32, kind="ExternalInput").ap()
    uTb = nc.dram_tensor("uTb", [NHC, DM, 128], I32, kind="ExternalInput").ap()
    dT = nc.dram_tensor("dT", [HID, DM], I32, kind="ExternalInput").ap()
    r_bq = nc.dram_tensor("r_bq", [1, DM], F32, kind="ExternalInput").ap()
    r_bk = nc.dram_tensor("r_bk", [1, DM], F32, kind="ExternalInput").ap()
    r_bv = nc.dram_tensor("r_bv", [1, DM], F32, kind="ExternalInput").ap()
    r_bo = nc.dram_tensor("r_bo", [1, DM], F32, kind="ExternalInput").ap()
    r_gq = nc.dram_tensor("r_gq", [1, DM], F32, kind="ExternalInput").ap()
    r_gk = nc.dram_tensor("r_gk", [1, DM], F32, kind="ExternalInput").ap()
    r_sg = nc.dram_tensor("r_sg", [1, HID], F32, kind="ExternalInput").ap()
    r_su = nc.dram_tensor("r_su", [1, HID], F32, kind="ExternalInput").ap()
    r_sd = nc.dram_tensor("r_sd", [1, DM], F32, kind="ExternalInput").ap()
    wmix = nc.dram_tensor("wmix", [1, 5], F32, kind="ExternalInput").ap()
    cI = nc.dram_tensor("cI", [128, 128], BF16, kind="ExternalInput").ap()
    cMask = nc.dram_tensor("cMask", [128, 1024], BF16, kind="ExternalInput").ap()
    cPm = nc.dram_tensor("cPm", [128, 640], BF16, kind="ExternalInput").ap()
    cOnes = nc.dram_tensor("cOnes", [128, 512], BF16, kind="ExternalInput").ap()
    cOnesF = nc.dram_tensor("cOnesF", [1, 128], F32, kind="ExternalInput").ap()

    sk = "ExternalOutput" if DEBUG else "Internal"
    xfmD = nc.dram_tensor("xfmD", [128, 16, EXT], BF16, kind=sk).ap()
    kqD = nc.dram_tensor("kqD", [128, H, TOK], BF16, kind=sk).ap()
    kkD = nc.dram_tensor("kkD", [128, H, EXT], BF16, kind=sk).ap()
    vtD = nc.dram_tensor("vtD", [EXT, DM], BF16, kind=sk).ap()
    aoD = nc.dram_tensor("aoD", [128, H, TOK], BF16, kind=sk).ap()
    x1D = nc.dram_tensor("x1D", [TOK, DM], F32, kind=sk).ap()
    hD = nc.dram_tensor("hD", [NHC, 128, TOK], BF16, kind=sk).ap()
    fD = nc.dram_tensor("fD", [TOK, DM], F32, kind=sk).ap()
    rqDbg = nc.dram_tensor("rqDbg", [1, TOK], F32, kind="ExternalOutput").ap() if DEBUG else None
    rqSDbg = nc.dram_tensor("rqSDbg", [128, 16], F32, kind="ExternalOutput").ap() if DEBUG else None
    y = nc.dram_tensor("y", [TOK, DM], F32, kind="ExternalOutput").ap()

    import contextlib
    st = contextlib.ExitStack()

    def sb(name, shp, dt):
        return st.enter_context(nc.sbuf_tensor(name, shp, dt))

    def psum(name, shp, dt):
        return st.enter_context(nc.psum_tensor(name, shp, dt))

    def sem(name):
        return Ct(st.enter_context(nc.semaphore(name)))

    with st:
        A = sb("A", [128, 49152], BF16)
        B = sb("B", [128, 16384], BF16)
        Fb = sb("Fb", [128, 6144], F32)
        smI = sb("smI", [128, 128], BF16)
        smM = sb("smM", [128, 1024], BF16)
        smP = sb("smP", [128, 640], BF16)
        smMq = sb("smMq", [128, 128], BF16)
        smO = sb("smO", [128, 512], BF16)
        smOF = sb("smOF", [1, 128], F32)
        gqc = sb("gqc", [128, 16], F32)
        gkc = sb("gkc", [128, 16], F32)
        sgc = sb("sgc", [128, 44], F32)
        suc = sb("suc", [128, 44], F32)
        sdb = sb("sdb", [128, 2048], F32)
        wmb = sb("wmb", [128, 8], F32)
        rbq = sb("rbq", [1, 2048], BF16)
        rbk = sb("rbk", [1, 2048], BF16)
        rbv = sb("rbv", [1, 2048], BF16)
        rbo = sb("rbo", [1, 2048], BF16)
        rsd = sb("rsd", [1, 2048], F32)
        rqA = sb("rqA", [1, 2048], F32)
        rqT = sb("rqT", [128, 16], F32)
        rqS = sb("rqS", [128, 16], F32)
        rowF = sb("rowF", [1, 1024], F32)
        stats = sb("stats", [128, 8], F32)
        spcV = sb("spcV", [128, 1], F32)
        spcS = sb("spcS", [128, 1], F32)

        PS0 = psum("PS0", [128, 2048], F32)
        PS1 = psum("PS1", [128, 2048], F32)

        bar = sem("bar")
        dG = sem("dG")
        dS = sem("dS")
        dC = sem("dC")
        tA = sem("tA"); tB = sem("tB"); tC = sem("tC"); tD = sem("tD")
        vA = sem("vA"); vB = sem("vB"); vC = sem("vC")
        sA = sem("sA"); sB = sem("sB"); sC = sem("sC")
        gA = sem("gA")

        blk = st.enter_context(nc.Block())
        ENGS = ["tensor", "vector", "scalar", "gpsimd", "sync"]
        barn = [0]

        def barrier():
            barn[0] += 1
            tgt = 5 * barn[0]
            for e in ENGS:
                def _f(en, tgt=tgt):
                    en.sem_inc(bar.h, 1)
                    en.wait_ge(bar.h, tgt)
                getattr(blk, e)(_f)

        def sect(e):
            return getattr(blk, e)

        def p0():
            tB0_p0 = tB.n
            dG0_rows = [0]
            rowP0 = Fb[0:1, 0:5632]

            @sect("gpsimd")
            def _(g):
                g.dma_start(out=smOF[:], in_=cOnesF).then_inc(dG.h, 16); dG.n += 16
                for out, src in ((smI, cI), (smM, cMask), (smP, cPm), (smO, cOnes)):
                    g.dma_start(out=out[:], in_=src).then_inc(dG.h, 16); dG.n += 16
                g.dma_start(out=wmb[0:1, 0:5], in_=wmix).then_inc(dG.h, 16); dG.n += 16
                dG0_rows[0] = dG.n
                for ri, (src, n) in enumerate(((r_gq, 16), (r_gk, 16), (r_sg, 44), (r_su, 44))):
                    g.dma_start(out=rowP0[0:1, 0:n * 128], in_=src).then_inc(dG.h, 16); dG.n += 16
                    g.dma_start(out=rowP0[0:1, 0:n * 128], in_=src).then_inc(dG.h, 16); dG.n += 16
                    g.wait_ge(gA.h, gA.n + 1); gA.n += 1
                for out, src in ((rbq, r_bq), (rbk, r_bk), (rbv, r_bv), (rbo, r_bo)):
                    g.dma_start(out=out[:], in_=src).then_inc(dG.h, 16); dG.n += 16
                g.dma_start(out=rsd[:], in_=r_sd).then_inc(dG.h, 16); dG.n += 16

            @sect("tensor")
            def _(t):
                nd = dG0_rows[0]
                vA0_p0 = vA.n
                for ri, (dst, n) in enumerate(((gqc, 16), (gkc, 16), (sgc, 44), (suc, 44))):
                    t.wait_ge(dG.h, nd + 32 * (ri + 1))
                    if ri >= 1:
                        t.wait_ge(vA.h, vA0_p0 + ri)
                    for c in range(n):
                        t.matmul(PS0[:, c:c + 1], rowP0[0:1, c * 128:(c + 1) * 128],
                                 smOF[0:1, 0:1], start=True, stop=True)
                    t.drain().then_inc(tB.h, 1); tB.n += 1
                t.wait_ge(dG.h, dG.n)
                t.wait_ge(vA.h, vA0_p0 + 4)
                t.matmul(PS0[:, 0:5], smOF[0:1, 0:128], wmb[0:1, 0:5], start=True, stop=True)
                for j in range(4):
                    t.matmul(PS1[:, j * 512:(j + 1) * 512], smOF[0:1, 0:128],
                             rsd[0:1, j * 512:(j + 1) * 512], start=True, stop=True)
                t.drain().then_inc(tA.h, 1); tA.n += 1

            @sect("scalar")
            def _(sc):
                sc.wait_ge(tA.h, tA.n)
                sc.copy(wmb[:, 0:5], PS0[:, 0:5])
                sc.copy(spcS[:], spcS[:]).then_inc(sA.h, 1); sA.n += 1

            @sect("vector")
            def _(v):
                for ri, (dst, n) in enumerate(((gqc, 16), (gkc, 16), (sgc, 44), (suc, 44))):
                    v.wait_ge(tB.h, tB0_p0 + ri + 1)
                    v.tensor_copy(dst[:, 0:n], PS0[:, 0:n])
                    v.tensor_copy(spcV[:], spcV[:]).then_inc(vA.h, 1); vA.n += 1
                    v.tensor_copy(spcV[:], spcV[:]).then_inc(gA.h, 1)
                v.wait_ge(tA.h, tA.n)
                v.tensor_copy(sdb[:, :], PS1[:, 0:2048])
                v.wait_ge(sA.h, sA.n)
                v.tensor_scalar(smMq[:], smP[:, 0:128], wmb[:, 0:1], None, op0=ALU.mult)
                for ci in range(1, 5):
                    v.tensor_copy(spcV[:], spcV[:])
                    v.scalar_tensor_tensor(smMq[:], smP[:, ci * 128:(ci + 1) * 128],
                                           wmb[:, ci:ci + 1], smMq[:],
                                           op0=ALU.mult, op1=ALU.add)
                v.tensor_copy(spcV[:], spcV[:])
            barrier()

        def p1x():
            d0, t0, v0, s0 = dG.n, tA.n, vA.n, dS.n
            xtm = (A[:, 32768:34816], A[:, 34816:36864], A[:, 36864:38912])
            ev = (B[:, 0:2048], B[:, 2048:4096])

            @sect("gpsimd")
            def _(g):
                for ec in range(ECH):
                    if ec >= 3:
                        g.wait_ge(tA.h, t0 + ec - 2)
                    g.dma_start(out=xtm[ec % 3][:], in_=xe[ec * 128:(ec + 1) * 128, :]).then_inc(dG.h, 16)
                    dG.n += 16
                g.wait_ge(tA.h, t0 + ECH - 2)
                g.dma_start(out=A[:, 38912:39424], in_=xe[0:128, 0:512]).then_inc(dG.h, 16)
                dG.n += 16

            @sect("tensor")
            def _(t):
                for ec in range(ECH):
                    t.wait_ge(dG.h, d0 + 16 * min(ec + 2, ECH + 1))
                    if ec >= 1:
                        t.wait_ge(vA.h, v0 + ec)
                    for i in range(16):
                        t.matmul(PS0[:, i * 128:(i + 1) * 128], xtm[ec % 3][:, i * 128:(i + 1) * 128],
                                 smI[:], start=True, stop=True)
                    t.drain().then_inc(tA.h, 1); tA.n += 1

            @sect("vector")
            def _(v):
                for ec in range(ECH):
                    v.wait_ge(tA.h, t0 + ec + 1)
                    if ec >= 2:
                        v.wait_ge(dS.h, s0 + 16 * (ec - 1))
                    for i in range(16):
                        v.tensor_copy(ev[ec % 2][:, i * 128:(i + 1) * 128], PS0[:, i * 128:(i + 1) * 128])
                    v.drain().then_inc(vA.h, 1); vA.n += 1

            @sect("sync")
            def _(s):
                for ec in range(ECH):
                    s.wait_ge(vA.h, v0 + ec + 1)
                    s.dma_start(out=xfmD[:, :, ec * 128:(ec + 1) * 128],
                                in_=ev[ec % 2].rearrange("p (i t) -> p i t", i=16)).then_inc(dS.h, 16)
                    dS.n += 16
                s.wait_ge(dS.h, dS.n)
            barrier()

        def proj(kind):
            grps = QGRP if kind == "q" else EGRP
            NG = len(grps)
            wsrc = {"q": wqT, "k": wkT, "v": wvT}[kind]
            brow = {"q": rbq, "k": rbk, "v": rbv}[kind]
            outD = {"q": kqD, "k": kkD, "v": vtD}[kind]
            d0, tA0, tB0, tC0, tD0 = dG.n, tA.n, tB.n, tC.n, tD.n
            vB0, vC0 = vB.n, vC.n
            sA0, sB0, sC0 = sA.n, sB.n, sC.n
            gA0 = gA.n
            wA = A[:, 0:32768]
            xf = (A[:, 32768:40960], A[:, 40960:49152])
            qp = B[:, 0:8192]
            sq = (B[:, 8192:8704], B[:, 8704:9216])
            ev = (B[:, 9216:10240], B[:, 10240:11264])
            vev = (B[:, 8704:10752], B[:, 10752:12800])
            rkb = Fb[:, 0:512]

            nst = [(16 if kind != "v" else W // 128) for (c0, W) in grps]
            LOADS_DONE = {}
            STORE_DONE = {}
            cnt = dS.n
            stu = 0
            for gi in range(NG):
                cnt += 16 * 16
                LOADS_DONE[gi] = cnt
                if gi >= 1:
                    for _ in range(nst[gi - 1]):
                        cnt += 16
                        stu += 1
                        STORE_DONE[stu - 1] = cnt
            for _ in range(nst[NG - 1]):
                cnt += 16
                stu += 1
                STORE_DONE[stu - 1] = cnt

            @sect("gpsimd")
            def _(g):
                for i in range(16):
                    g.dma_start(out=wA[:, i * 2048:(i + 1) * 2048],
                                in_=wsrc[i * 128:(i + 1) * 128, :]).then_inc(dG.h, 16)
                    dG.n += 16

            @sect("sync")
            def _(s):
                stu2 = 0
                for gi, (c0, W) in enumerate(grps):
                    if gi >= 2:
                        s.wait_ge(tA.h, tA0 + gi - 1)
                    base = (OFF + c0) if kind == "q" else c0
                    for i in range(16):
                        s.dma_start(out=xf[gi % 2][:, i * 512:i * 512 + W],
                                    in_=xfmD[:, i, base:base + W]).then_inc(dS.h, 16)
                        dS.n += 16
                    if gi >= 1:
                        c0p, Wp = grps[gi - 1]
                        for u in range(nst[gi - 1]):
                            s.wait_ge(vB.h, vB0 + stu2 + 1)
                            if kind == "q":
                                s.dma_start(out=kqD[:, u, c0p:c0p + Wp], in_=ev[u % 2][:, 0:Wp]).then_inc(dS.h, 16)
                            elif kind == "k":
                                s.dma_start(out=kkD[:, u, c0p:c0p + Wp], in_=ev[u % 2][:, 0:Wp]).then_inc(dS.h, 16)
                            else:
                                mc = c0p + u * 128
                                s.dma_start(out=vtD[mc:mc + 128, :], in_=vev[u % 2][:, 0:2048]).then_inc(dS.h, 16)
                            dS.n += 16
                            stu2 += 1
                c0p, Wp = grps[NG - 1]
                for u in range(nst[NG - 1]):
                    s.wait_ge(vB.h, vB0 + stu2 + 1)
                    if kind == "q":
                        s.dma_start(out=kqD[:, u, c0p:c0p + Wp], in_=ev[u % 2][:, 0:Wp]).then_inc(dS.h, 16)
                    elif kind == "k":
                        s.dma_start(out=kkD[:, u, c0p:c0p + Wp], in_=ev[u % 2][:, 0:Wp]).then_inc(dS.h, 16)
                    else:
                        mc = c0p + u * 128
                        s.dma_start(out=vtD[mc:mc + 128, :], in_=vev[u % 2][:, 0:2048]).then_inc(dS.h, 16)
                    dS.n += 16
                    stu2 += 1
                s.wait_ge(dS.h, dS.n)

            @sect("tensor")
            def _(t):
                midx = 0
                for gi, (c0, W) in enumerate(grps):
                    t.wait_ge(dS.h, LOADS_DONE[gi])
                    if gi == 0:
                        t.wait_ge(dG.h, d0 + 256)
                    if kind in ("q", "k"):
                        if kind == "k" and gi >= 1:
                            t.wait_ge(vB.h, vB0 + 16 * gi)
                        def sumsq_mm(o):
                            t.wait_ge(sB.h, sB0 + gi * 16 + o + 1)
                            t.matmul(PS1[0:1, 1024:1024 + W], smO[:, 0:1], sq[o % 2][:, 0:W],
                                     start=(o == 0), stop=(o == 15)).then_inc(tC.h, 1)
                            tC.n += 1
                        for o in range(16):
                            bank = PS1[:, (o % 2) * 512:(o % 2) * 512 + W]
                            j = gi * 16 + o
                            if j >= 2:
                                t.wait_ge(sA.h, sA0 + j - 1)
                            for i in range(16):
                                t.matmul(bank, wA[:, i * 2048 + o * 128:i * 2048 + (o + 1) * 128],
                                         xf[gi % 2][:, i * 512:i * 512 + W],
                                         start=(i == 0), stop=False)
                            t.matmul(bank, brow[0:1, o * 128:(o + 1) * 128], smO[0:1, 0:W],
                                     start=False, stop=True).then_inc(tB.h, 1)
                            tB.n += 1
                            if o >= 1:
                                sumsq_mm(o - 1)
                        sumsq_mm(15)
                        t.drain().then_inc(tA.h, 1); tA.n += 1
                        if kind == "k":
                            t.wait_ge(sC.h, sC0 + gi + 1)
                            t.matmul(PS0[:, 0:W], smOF[0:1, 0:128], rowF[0:1, 0:W],
                                     start=True, stop=True).then_inc(tD.h, 1)
                            tD.n += 1
                        if kind == "q":
                            for h in range(H):
                                j = gi * 16 + h
                                t.wait_ge(sA.h, sA0 + j + 1)
                                if j >= 1:
                                    t.wait_ge(vB.h, vB0 + j)
                                t.matmul(PS1[:, 1536:1536 + W], smMq[:],
                                         qp[:, h * 512:h * 512 + W],
                                         start=True, stop=True).then_inc(tD.h, 1)
                                tD.n += 1
                    else:
                        nm = W // 128
                        for m in range(nm):
                            if midx >= 1:
                                t.wait_ge(vB.h, vB0 + midx)
                            for og in range(4):
                                seg = PS1[:, og * 512:(og + 1) * 512]
                                for i in range(16):
                                    t.matmul(seg, xf[gi % 2][:, i * 512 + m * 128:i * 512 + (m + 1) * 128],
                                             wA[:, i * 2048 + og * 512:i * 2048 + (og + 1) * 512],
                                             start=(i == 0), stop=False)
                                t.matmul(seg, smO[0:1, 0:128], brow[0:1, og * 512:(og + 1) * 512],
                                         start=False, stop=True)
                            t.drain().then_inc(tB.h, 1); tB.n += 1
                            midx += 1
                        t.drain().then_inc(tA.h, 1); tA.n += 1

            @sect("scalar")
            def _(s):
                if kind == "v":
                    return
                gcol = gqc if kind == "q" else gkc
                for gi, (c0, W) in enumerate(grps):
                    for o in range(16):
                        s.wait_ge(tB.h, tB0 + gi * 16 + o + 1)
                        if gi * 16 + o >= 2:
                            s.wait_ge(tC.h, tC0 + gi * 16 + o - 1)
                        bank = PS1[:, (o % 2) * 512:(o % 2) * 512 + W]
                        s.activation(sq[o % 2][:, 0:W], bank, AF.Square).then_inc(sB.h, 1); sB.n += 1
                        s.activation(qp[:, o * 512:o * 512 + W], bank, AF.Copy,
                                     scale=gcol[:, o:o + 1]).then_inc(sA.h, 1); sA.n += 1
                    s.wait_ge(vC.h, vC0 + gi + 1)
                    dst = rqA[0:1, c0:c0 + W] if kind == "q" else rowF[0:1, 0:W]
                    s.activation(dst, rowF[0:1, 512:512 + W], AF.Sqrt)
                    s.copy(spcS[:], spcS[:]).then_inc(sC.h, 1); sC.n += 1

            @sect("vector")
            def _(v):
                stu2 = 0
                for gi, (c0, W) in enumerate(grps):
                    if kind in ("q", "k"):
                        v.wait_ge(tC.h, tC0 + 16 * (gi + 1))
                        if gi >= 1:
                            v.wait_ge(sC.h, sC0 + gi)
                            if kind == "k":
                                v.wait_ge(tD.h, tD0 + gi)
                        v.tensor_scalar(rowF[0:1, 0:W],
                                        PS1[0:1, 1024:1024 + W],
                                        1.0 / 2048.0, EPS_RMS, op0=ALU.mult, op1=ALU.add)
                        v.tensor_copy(spcV[:], spcV[:])
                        v.reciprocal(rowF[0:1, 512:512 + W], rowF[0:1, 0:W])
                        v.tensor_copy(spcV[:], spcV[:]).then_inc(vC.h, 1)
                        vC.n += 1
                    if kind == "q":
                        for h in range(H):
                            j = gi * 16 + h
                            v.wait_ge(tD.h, tD0 + j + 1)
                            if stu2 >= 2:
                                v.wait_ge(dS.h, STORE_DONE[stu2 - 2])
                            v.tensor_copy(ev[h % 2][:, 0:W], PS1[:, 1536:1536 + W]).then_inc(vB.h, 1)
                            vB.n += 1
                            stu2 += 1
                    elif kind == "k":
                        v.wait_ge(tD.h, tD0 + gi + 1)
                        v.tensor_copy(rkb[:, 0:W], PS0[:, 0:W])
                        v.tensor_copy(spcV[:], spcV[:])
                        for o in range(16):
                            j = gi * 16 + o
                            if stu2 >= 2:
                                v.wait_ge(dS.h, STORE_DONE[stu2 - 2])
                            v.scalar_tensor_tensor(ev[o % 2][:, 0:W], qp[:, o * 512:o * 512 + W],
                                                   1.0, rkb[:, 0:W],
                                                   op0=ALU.mult, op1=ALU.mult).then_inc(vB.h, 1)
                            vB.n += 1
                            stu2 += 1
                    else:
                        nm = W // 128
                        for m in range(nm):
                            v.wait_ge(tB.h, tB0 + stu2 + 1)
                            if stu2 >= 2:
                                v.wait_ge(dS.h, STORE_DONE[stu2 - 2])
                            v.tensor_copy(vev[m % 2][:, 0:2048], PS1[:, 0:2048]).then_inc(vB.h, 1)
                            vB.n += 1
                            stu2 += 1
            barrier()

        def p1b1():
            dG0, dS0 = dG.n, dS.n
            tB0, tC0, tD0 = tB.n, tC.n, tD.n
            vA0, vB0, vC0 = vA.n, vB.n, vC.n
            sA0, sB0 = sA.n, sB.n
            kr = A[:, 0:24576]
            kr12 = B[:, 12544:14592]
            vr = A[:, 24576:49152]

            def kslot(ec):
                sl = ec % 13
                return kr12 if sl == 12 else kr[:, sl * 2048:(sl + 1) * 2048]
            qsb = (B[:, 0:2048], B[:, 2048:4096])
            aosb = (B[:, 4096:6144], B[:, 6144:8192])
            exps = (B[:, 8192:9216], B[:, 9216:10240])
            ptr = (B[:, 10240:11264], B[:, 11264:12288])
            diag = (B[:, 12288:12416], B[:, 12416:12544])
            nmx = (Fb[:, 0:1], Fb[:, 1:2])
            ebv = (Fb[:, 2:3], Fb[:, 3:4])
            zr = (Fb[:, 4:5], Fb[:, 5:6])
            rz = (Fb[:, 6:7], Fb[:, 7:8])
            mb = (Fb[:, 16:1040], Fb[:, 1040:2064])

            def new_ecs(lc):
                return list(range(12)) if lc == 0 else [lc + 11]

            LOADS_DONE = {}
            AOST_DONE = {}
            VR_DONE = {}
            cnt = dS.n
            cntg = dG.n
            for lc in range(LCH):
                cnt += 16 * (len(new_ecs(lc)) + 1)
                LOADS_DONE[lc] = cnt
                cntg += 16 * len(new_ecs(lc))
                VR_DONE[lc] = cntg
                if lc >= 1:
                    cnt += 16
                    AOST_DONE[lc - 1] = cnt
            cnt += 16
            AOST_DONE[LCH - 1] = cnt

            @sect("gpsimd")
            def _(g):
                for lc in range(LCH):
                    if lc >= 1:
                        g.wait_ge(tD.h, tD0 + 16 * lc)
                    for ec in new_ecs(lc):
                        slot = ec % 12
                        g.dma_start(out=vr[:, slot * 2048:(slot + 1) * 2048],
                                    in_=vtD[ec * 128:(ec + 1) * 128, :]).then_inc(dG.h, 16)
                        dG.n += 16
                if DEBUG:
                    g.dma_start(out=rqDbg, in_=rqA[0:1, 0:2048]).then_inc(dG.h, 16)
                    dG.n += 16
                    g.wait_ge(vC.h, vC.n + 1)
                    g.dma_start(out=rqSDbg, in_=rqS[:]).then_inc(dG.h, 16)
                    dG.n += 16

            @sect("tensor")
            def _(t):
                for c in range(16):
                    t.matmul(PS1[:, 1024 + c:1024 + c + 1], rqA[0:1, c * 128:(c + 1) * 128],
                             smOF[0:1, 0:1], start=True, stop=True)
                t.drain().then_inc(tA.h, 1); tA.n += 1

            @sect("vector")
            def _(v):
                v.wait_ge(tA.h, tA.n)
                v.tensor_copy(rqT[:, 0:16], PS1[:, 1024:1040])
                v.tensor_copy(spcV[:], spcV[:])
                v.tensor_scalar(rqS[:], rqT[:], SCALE, None, op0=ALU.mult)
                v.tensor_copy(spcV[:], spcV[:]).then_inc(vC.h, 1)
                vC.n += 1

            @sect("sync")
            def _(s):
                for lc in range(LCH):
                    if lc >= 2:
                        s.wait_ge(tB.h, tB0 + 16 * (lc - 1))
                    for ec in new_ecs(lc):
                        kb = kslot(ec)
                        s.dma_start(out=kb.rearrange("p (h t) -> p h t", h=16),
                                    in_=kkD[:, :, ec * 128:(ec + 1) * 128]).then_inc(dS.h, 16)
                        dS.n += 16
                    s.dma_start(out=qsb[lc % 2].rearrange("p (h t) -> p h t", h=16),
                                in_=kqD[:, :, lc * 128:(lc + 1) * 128]).then_inc(dS.h, 16)
                    dS.n += 16
                    if lc >= 1:
                        s.wait_ge(vB.h, vB0 + 16 * lc)
                        s.dma_start(out=aoD[:, :, (lc - 1) * 128:lc * 128],
                                    in_=aosb[(lc - 1) % 2].rearrange("p (h t) -> p h t", h=16)).then_inc(dS.h, 16)
                        dS.n += 16
                s.wait_ge(vB.h, vB0 + 16 * LCH)
                s.dma_start(out=aoD[:, :, (LCH - 1) * 128:LCH * 128],
                            in_=aosb[(LCH - 1) % 2].rearrange("p (h t) -> p h t", h=16)).then_inc(dS.h, 16)
                dS.n += 16
                s.wait_ge(dS.h, dS.n)

            sC0 = sC.n

            @sect("tensor")
            def _(t):
                NJ = LCH * H

                def band(j):
                    lc, h = divmod(j, H)
                    ecs = [lc + 8 + db for db in DB]
                    if h == 0:
                        t.wait_ge(dS.h, LOADS_DONE[lc])
                    half = PS0[:, (h % 2) * 1024:(h % 2) * 1024 + 1024]
                    if j >= 2:
                        t.wait_ge(vA.h, vA0 + j - 1)
                    for bi, ec in enumerate(ecs):
                        kb = kslot(ec)
                        t.matmul(half[:, bi * 128:(bi + 1) * 128],
                                 qsb[lc % 2][:, h * 128:(h + 1) * 128],
                                 kb[:, h * 128:(h + 1) * 128],
                                 start=True, stop=True)
                    t.drain().then_inc(tB.h, 1); tB.n += 1

                band(0)
                for j in range(NJ):
                    lc, h = divmod(j, H)
                    ecs = [lc + 8 + db for db in DB]
                    if j + 1 < NJ:
                        band(j + 1)
                    t.wait_ge(vC.h, vC0 + 1 + j + 1)
                    if j >= 1:
                        t.wait_ge(sC.h, sC0 + j)
                    for bi in range(8):
                        t.matmul(PS1[:, bi * 128:(bi + 1) * 128],
                                 exps[j % 2][:, bi * 128:(bi + 1) * 128], smI[:],
                                 start=True, stop=True)
                    t.drain().then_inc(tC.h, 1); tC.n += 1
                    t.wait_ge(sC.h, sC0 + j + 1)
                    if j >= 1:
                        t.wait_ge(vB.h, vB0 + j)
                    if h == 0:
                        t.wait_ge(dG.h, VR_DONE[lc])
                    for bi, ec in enumerate(ecs):
                        slot = ec % 12
                        t.matmul(PS1[:, 1024:1152],
                                 vr[:, slot * 2048 + h * 128:slot * 2048 + (h + 1) * 128],
                                 ptr[j % 2][:, bi * 128:(bi + 1) * 128],
                                 start=(bi == 0), stop=(bi == 7))
                    t.drain().then_inc(tD.h, 1); tD.n += 1

            @sect("vector")
            def _(v):
                def ao_evac(j):
                    lc, h = divmod(j, H)
                    v.wait_ge(tD.h, tD0 + j + 1)
                    if lc >= 2 and h == 0:
                        v.wait_ge(dS.h, AOST_DONE[lc - 2])
                    v.tensor_copy(aosb[lc % 2][:, h * 128:(h + 1) * 128],
                                  PS1[:, 1024:1152]).then_inc(vB.h, 1)
                    vB.n += 1
                for lc in range(LCH):
                    for h in range(H):
                        j = lc * 16 + h
                        v.wait_ge(tB.h, tB0 + j + 1)
                        if j >= 2:
                            v.wait_ge(sA.h, sA0 + j - 1)
                        half = PS0[:, (h % 2) * 1024:(h % 2) * 1024 + 1024]
                        v.scalar_tensor_tensor(mb[j % 2][:], half, 1.0, smM[:],
                                               op0=ALU.mult, op1=ALU.add)
                        v.tensor_reduce(nmx[j % 2][:, 0:1], mb[j % 2][:], axis=AX.X,
                                        op=ALU.max, negate=True)
                        v.tensor_copy(spcV[:], spcV[:])
                        v.tensor_tensor(ebv[j % 2][:, 0:1], nmx[j % 2][:, 0:1],
                                        rqS[:, lc:lc + 1], op=ALU.mult)
                        v.tensor_copy(spcV[:], spcV[:]).then_inc(vA.h, 1)
                        vA.n += 1
                        v.wait_ge(sA.h, sA0 + j + 1)
                        v.tensor_reduce(zr[j % 2][:, 0:1], exps[j % 2][:], axis=AX.X, op=ALU.add)
                        v.tensor_copy(spcV[:], spcV[:])
                        v.reciprocal(rz[j % 2][:, 0:1], zr[j % 2][:, 0:1])
                        v.tensor_copy(spcV[:], spcV[:])
                        v.tensor_scalar(exps[j % 2][:], exps[j % 2][:], rz[j % 2][:, 0:1],
                                        None, op0=ALU.mult)
                        v.tensor_copy(spcV[:], spcV[:]).then_inc(vC.h, 1)
                        vC.n += 1
                        if j >= 1:
                            ao_evac(j - 1)
                ao_evac(LCH * H - 1)

            @sect("scalar")
            def _(s):
                def ptr_evac(j):
                    s.wait_ge(tC.h, tC0 + j + 1)
                    if j >= 1:
                        s.wait_ge(tD.h, tD0 + j - 1)
                    s.activation(ptr[j % 2][:], PS1[:, 0:1024], AF.Copy)
                    s.copy(spcS[:], spcS[:]).then_inc(sC.h, 1); sC.n += 1
                for lc in range(LCH):
                    for h in range(H):
                        j = lc * 16 + h
                        s.wait_ge(vA.h, vA0 + j + 1)
                        if j >= 2:
                            s.wait_ge(tC.h, tC0 + j - 1)
                        s.activation(exps[j % 2][:], mb[j % 2][:], AF.Exp, bias=ebv[j % 2][:, 0:1],
                                     scale=rqS[:, lc:lc + 1])
                        s.copy(spcS[:], spcS[:]).then_inc(sA.h, 1); sA.n += 1
                        if j >= 1:
                            ptr_evac(j - 1)
                ptr_evac(LCH * H - 1)
            barrier()

        def p1b2():
            dG0, dS0, dC0 = dG.n, dS.n, dC.n
            tB0, vA0, vB0, vC0, sA0 = tB.n, vA.n, vB.n, vC.n, sA.n
            wA = A[:, 0:32768]
            aold = (B[:, 0:2048], B[:, 2048:4096])
            dump = B[:, 4096:6144]
            xr = (Fb[:, 0:2048], Fb[:, 2048:4096])
            xsum = Fb[:, 4096:6144]
            s1c = stats[:, 0:1]
            s2c = stats[:, 1:2]
            mu = stats[:, 2:3]
            iv = stats[:, 3:4]
            rstd = stats[:, 4:5]
            nmr = stats[:, 5:6]

            @sect("gpsimd")
            def _(g):
                for i in range(16):
                    g.dma_start(out=wA[:, i * 2048:(i + 1) * 2048],
                                in_=woT[i * 128:(i + 1) * 128, :]).then_inc(dG.h, 16)
                    dG.n += 16

            @sect("sync")
            def _(s):
                for lc in range(LCH):
                    if lc >= 2:
                        s.wait_ge(tB.h, tB0 + lc - 1)
                    s.dma_start(out=aold[lc % 2].rearrange("p (h t) -> p h t", h=16),
                                in_=aoD[:, :, lc * 128:(lc + 1) * 128]).then_inc(dS.h, 16)
                    dS.n += 16
                    if lc >= 2:
                        s.wait_ge(vA.h, vA0 + lc - 1)
                    s.dma_start(out=xr[lc % 2][:],
                                in_=xe[OFF + lc * 128:OFF + (lc + 1) * 128, :]).then_inc(dS.h, 16)
                    dS.n += 16
                s.wait_ge(dS.h, dS.n)

            @sect("tensor")
            def _(t):
                t.wait_ge(dG.h, dG0 + 256)
                for lc in range(LCH):
                    t.wait_ge(dS.h, dS0 + 32 * lc + 16)
                    PSx = PS0 if lc % 2 == 0 else PS1
                    if lc >= 2:
                        t.wait_ge(vA.h, vA0 + lc - 1)
                    for hh in range(16):
                        for og in range(4):
                            t.matmul(PSx[:, og * 512:(og + 1) * 512],
                                     aold[lc % 2][:, hh * 128:(hh + 1) * 128],
                                     wA[:, hh * 2048 + og * 512:hh * 2048 + (og + 1) * 512],
                                     start=(hh == 0), stop=False)
                    for og in range(4):
                        t.matmul(PSx[:, og * 512:(og + 1) * 512], smO[0:1, 0:128],
                                 rbo[0:1, og * 512:(og + 1) * 512], start=False, stop=True)
                    t.drain().then_inc(tB.h, 1); tB.n += 1

            @sect("vector")
            def _(v):
                for lc in range(LCH):
                    v.wait_ge(tB.h, tB0 + lc + 1)
                    v.wait_ge(dS.h, dS0 + 32 * (lc + 1))
                    if lc >= 1:
                        v.wait_ge(dC.h, dC0 + 16 * lc)
                    PSx = PS0 if lc % 2 == 0 else PS1
                    v.tensor_tensor(xsum[:], PSx[:, 0:2048], xr[lc % 2][:], op=ALU.add)
                    v.drain().then_inc(vA.h, 1); vA.n += 1
                    v.tensor_reduce(s1c, xsum[:], axis=AX.X, op=ALU.add)
                    v.wait_ge(sA.h, sA0 + 2 * lc + 1)
                    v.tensor_reduce(s2c, dump[:], axis=AX.X, op=ALU.add)
                    v.tensor_copy(spcV[:], spcV[:])
                    v.tensor_scalar(mu, s1c, -1.0 / 2048.0, None, op0=ALU.mult)
                    v.tensor_scalar(iv, s2c, 1.0 / 2048.0, EPS_LN, op0=ALU.mult, op1=ALU.add)
                    v.tensor_tensor(nmr, mu, mu, op=ALU.mult)
                    v.tensor_copy(spcV[:], spcV[:])
                    v.tensor_tensor(iv, iv, nmr, op=ALU.subtract)
                    v.tensor_copy(spcV[:], spcV[:])
                    v.reciprocal(iv, iv)
                    v.tensor_copy(spcV[:], spcV[:]).then_inc(vC.h, 1); vC.n += 1
                    v.wait_ge(sA.h, sA0 + 2 * lc + 2)
                    v.tensor_scalar(xsum[:], xsum[:], rstd[:, 0:1], nmr[:, 0:1],
                                    op0=ALU.mult, op1=ALU.add).then_inc(vB.h, 1)
                    vB.n += 1

            @sect("scalar")
            def _(s):
                for lc in range(LCH):
                    s.wait_ge(vA.h, vA0 + lc + 1)
                    s.activation(dump[:], xsum[:], AF.Square)
                    s.copy(spcS[:], spcS[:]).then_inc(sA.h, 1)
                    sA.n += 1
                    s.wait_ge(vC.h, vC0 + lc + 1)
                    s.activation(rstd, iv, AF.Sqrt)
                    s.copy(spcS[:], spcS[:])
                    s.mul(nmr, rstd, mu[:, 0:1])
                    s.copy(spcS[:], spcS[:]).then_inc(sA.h, 1); sA.n += 1
                    s.wait_ge(vB.h, vB0 + lc + 1)
                    s.dma_start(out=x1D[lc * 128:(lc + 1) * 128, :], in_=xsum[:]).then_inc(dC.h, 16)
                    dC.n += 16
                s.wait_ge(dC.h, dC.n)
            barrier()

        def p2a():
            dG0, dS0 = dG.n, dS.n
            tA0, tB0, vA0, vB0, vC0, sA0 = tA.n, tB.n, vA.n, vB.n, vC.n, sA.n
            x1fm = A[:, 0:32768]
            wgr = (A[:, 32768:34816], A[:, 34816:36864])
            wur = (A[:, 36864:38912], A[:, 38912:40960])
            wgc = (A[:, 40960:43008], A[:, 43008:45056])
            wuc = (A[:, 45056:47104], A[:, 47104:49152])
            xtm = (B[:, 0:2048], B[:, 2048:4096], B[:, 4096:6144])
            gsb = B[:, 6144:6656]
            usb = B[:, 6656:7168]
            hsb = (B[:, 7168:7680], B[:, 7680:8192])

            @sect("gpsimd")
            def _(g):
                for lc in range(LCH):
                    if lc >= 3:
                        g.wait_ge(tA.h, tA0 + lc - 2)
                    g.dma_start(out=xtm[lc % 3][:], in_=x1D[lc * 128:(lc + 1) * 128, :]).then_inc(dG.h, 16)
                    dG.n += 16
                for o in range(NHC):
                    if o >= 2:
                        g.wait_ge(vA.h, vA0 + 2 * (o - 1))
                    g.dma_start(out=wgr[o % 2][:],
                                in_=gTb[o].rearrange("(i p) c -> p i c", p=128)).then_inc(dG.h, 16)
                    dG.n += 16
                    g.dma_start(out=wur[o % 2][:],
                                in_=uTb[o].rearrange("(i p) c -> p i c", p=128)).then_inc(dG.h, 16)
                    dG.n += 16
                g.wait_ge(tA.h, tA0 + LCH)
                for _ in range(2):
                    g.dma_start(out=xtm[0][:, 0:512], in_=x1D[0:128, 0:512]).then_inc(dG.h, 16)
                    dG.n += 16

            @sect("tensor")
            def _(t):
                for lc in range(LCH):
                    t.wait_ge(dG.h, dG0 + 16 * min(lc + 2, LCH + 2))
                    if lc >= 1:
                        t.wait_ge(vB.h, vB0 + lc)
                    for i in range(16):
                        t.matmul(PS1[:, i * 128:(i + 1) * 128], xtm[lc % 3][:, i * 128:(i + 1) * 128],
                                 smI[:], start=True, stop=True)
                    t.drain().then_inc(tA.h, 1); tA.n += 1
                for o in range(NHC):
                    t.wait_ge(vA.h, vA0 + 2 * (o + 1))
                    if o == 0:
                        t.wait_ge(vB.h, vB0 + LCH)
                    for gi in range(4):
                        j = o * 4 + gi
                        bb = (gi % 2) * 1024
                        if j >= 2:
                            t.wait_ge(sA.h, sA0 + j - 1)
                            t.wait_ge(vC.h, vC0 + j - 1)
                        for i in range(16):
                            t.matmul(PS0[:, bb:bb + 512],
                                     wgc[o % 2][:, i * 128:(i + 1) * 128],
                                     x1fm[:, i * 2048 + gi * 512:i * 2048 + (gi + 1) * 512],
                                     start=(i == 0), stop=(i == 15))
                        for i in range(16):
                            t.matmul(PS0[:, bb + 512:bb + 1024],
                                     wuc[o % 2][:, i * 128:(i + 1) * 128],
                                     x1fm[:, i * 2048 + gi * 512:i * 2048 + (gi + 1) * 512],
                                     start=(i == 0), stop=(i == 15))
                        t.drain().then_inc(tB.h, 1); tB.n += 1

            @sect("vector")
            def _(v):
                for lc in range(LCH):
                    v.wait_ge(tA.h, tA0 + lc + 1)
                    for i in range(16):
                        v.tensor_copy(x1fm[:, i * 2048 + lc * 128:i * 2048 + (lc + 1) * 128],
                                      PS1[:, i * 128:(i + 1) * 128])
                    v.drain().then_inc(vB.h, 1); vB.n += 1
                for o in range(NHC):
                    v.wait_ge(dG.h, dG0 + 16 * LCH + 32 * min(o + 2, NHC + 1))
                    if o >= 2:
                        v.wait_ge(tB.h, tB0 + 4 * (o - 1))
                    v.tensor_scalar(wgc[o % 2][:], wgr[o % 2][:], -128.0, None, op0=ALU.add)
                    v.tensor_scalar(wuc[o % 2][:], wur[o % 2][:], -128.0, None, op0=ALU.add)
                    v.tensor_copy(spcV[:], spcV[:]).then_inc(vA.h, 2)
                    vA.n += 2
                    for gi in range(4):
                        j = o * 4 + gi
                        v.wait_ge(tB.h, tB0 + j + 1)
                        v.tensor_scalar(usb[:, 0:512], PS0[:, (gi % 2) * 1024 + 512:(gi % 2) * 1024 + 1024],
                                        suc[:, o:o + 1], None, op0=ALU.mult)
                        v.wait_ge(sA.h, sA0 + j + 1)
                        if j >= 2:
                            v.wait_ge(dS.h, dS0 + 16 * (j - 1))
                        v.tensor_tensor(hsb[j % 2][:, 0:512], gsb[:, 0:512],
                                        usb[:, 0:512], op=ALU.mult).then_inc(vC.h, 1)
                        vC.n += 1

            @sect("scalar")
            def _(s):
                for o in range(NHC):
                    for gi in range(4):
                        j = o * 4 + gi
                        s.wait_ge(tB.h, tB0 + j + 1)
                        if j >= 1:
                            s.wait_ge(vC.h, vC0 + j)
                        s.activation(gsb[:, 0:512], PS0[:, (gi % 2) * 1024:(gi % 2) * 1024 + 512],
                                     AF.Silu, scale=sgc[:, o:o + 1]).then_inc(sA.h, 1)
                        sA.n += 1

            @sect("sync")
            def _(s):
                for o in range(NHC):
                    for gi in range(4):
                        j = o * 4 + gi
                        s.wait_ge(vC.h, vC0 + j + 1)
                        s.dma_start(out=hD[o, :, gi * 512:(gi + 1) * 512],
                                    in_=hsb[j % 2][:, 0:512]).then_inc(dS.h, 16)
                        dS.n += 16
                s.wait_ge(dS.h, dS.n)
            barrier()

        def p2b():
            dG0, dS0, dC0 = dG.n, dS.n, dC.n
            tA0, tB0, vA0, vB0 = tA.n, tB.n, vA.n, vB.n
            dTq = A[:, 0:22528]
            hring = A[:, 22528:26624]
            wdr = tuple(B[:, i * 512:(i + 1) * 512] for i in range(4))
            wdum = B[:, 2048:2560]
            fev = Fb[:, 0:2048]

            @sect("gpsimd")
            def _(g):
                for qd in range(4):
                    for o in range(NHC):
                        j = qd * NHC + o
                        if j >= 4:
                            g.wait_ge(vA.h, vA0 + j - 3)
                        g.dma_start(out=wdr[j % 4][:],
                                    in_=dT[o * 128:(o + 1) * 128, qd * 512:(qd + 1) * 512]).then_inc(dG.h, 16)
                        dG.n += 16
                for _ in range(2):
                    g.dma_start(out=wdum[:], in_=dT[0:128, 0:512]).then_inc(dG.h, 16)
                    dG.n += 16

            @sect("vector")
            def _(v):
                for qd in range(4):
                    for o in range(NHC):
                        j = qd * NHC + o
                        v.wait_ge(dG.h, dG0 + 16 * min(j + 3, 4 * NHC + 2))
                        if qd >= 1 and o == 0:
                            v.wait_ge(tA.h, tA0 + 4 * qd)
                        v.scalar_tensor_tensor(dTq[:, o * 512:(o + 1) * 512], wdr[j % 4][:],
                                               -128.0, sdb[:, qd * 512:(qd + 1) * 512],
                                               op0=ALU.add, op1=ALU.mult)
                        if o == NHC - 1:
                            v.tensor_copy(spcV[:], spcV[:])
                            v.tensor_copy(spcV[:], spcV[:])
                            v.tensor_copy(spcV[:], spcV[:])
                        v.tensor_copy(spcV[:], spcV[:]).then_inc(vA.h, 1)
                        vA.n += 1
                    for gi in range(4):
                        k = qd * 4 + gi
                        v.wait_ge(tA.h, tA0 + k + 1)
                        if k >= 1:
                            v.wait_ge(dC.h, dC0 + 64 * k)
                        PSx = PS0 if gi % 2 == 0 else PS1
                        for lci in range(4):
                            v.tensor_copy(fev[:, lci * 512:(lci + 1) * 512],
                                          PSx[:, lci * 512:(lci + 1) * 512])
                        v.drain().then_inc(vB.h, 1); vB.n += 1

            @sect("sync")
            def _(s):
                nh = 0
                for qd in range(4):
                    for gi in range(4):
                        for o in range(NHC):
                            nh += 1
                            if nh > 8:
                                s.wait_ge(tB.h, tB0 + nh - 8)
                            s.dma_start(out=hring[:, ((nh - 1) % 8) * 512:((nh - 1) % 8 + 1) * 512],
                                        in_=hD[o, :, gi * 512:(gi + 1) * 512]).then_inc(dS.h, 16)
                            dS.n += 16

            @sect("tensor")
            def _(t):
                nh = 0
                for qd in range(4):
                    for gi in range(4):
                        k = qd * 4 + gi
                        PSx = PS0 if gi % 2 == 0 else PS1
                        if k >= 2:
                            t.wait_ge(vB.h, vB0 + k - 1)
                        for o in range(NHC):
                            nh += 1
                            t.wait_ge(dS.h, dS0 + 16 * nh)
                            t.wait_ge(vA.h, vA0 + min(qd * NHC + o + 2, (qd + 1) * NHC))
                            slot = (nh - 1) % 8
                            for lci in range(4):
                                t.matmul(PSx[:, lci * 512:(lci + 1) * 512],
                                         hring[:, slot * 512 + lci * 128:slot * 512 + (lci + 1) * 128],
                                         dTq[:, o * 512:(o + 1) * 512],
                                         start=(o == 0), stop=(o == NHC - 1))
                            t.drain().then_inc(tB.h, 1); tB.n += 1
                        t.drain().then_inc(tA.h, 1); tA.n += 1

            @sect("scalar")
            def _(s):
                for qd in range(4):
                    for gi in range(4):
                        k = qd * 4 + gi
                        s.wait_ge(vB.h, vB0 + k + 1)
                        for lci in range(4):
                            r0 = (gi * 4 + lci) * 128
                            s.dma_start(out=fD[r0:r0 + 128, qd * 512:(qd + 1) * 512],
                                        in_=fev[:, lci * 512:(lci + 1) * 512]).then_inc(dC.h, 16)
                            dC.n += 16
                s.wait_ge(dC.h, dC.n)
            barrier()

        def p2c():
            dS0, dC0 = dS.n, dC.n
            vA0, vB0, vC0, sA0 = vA.n, vB.n, vC.n, sA.n
            fsb = (Fb[:, 0:2048], Fb[:, 2048:4096])
            xsum = (Fb[:, 4096:6144], sdb[:, 0:2048])
            dump = (B[:, 0:2048], B[:, 2048:4096])
            s1c = stats[:, 0:1]
            s2c = stats[:, 1:2]
            mu = stats[:, 2:3]
            iv = stats[:, 3:4]
            rstd = stats[:, 4:5]
            nmr = stats[:, 5:6]

            @sect("sync")
            def _(s):
                for lc in range(LCH):
                    if lc >= 2:
                        s.wait_ge(dC.h, dC0 + 16 * (lc - 1))
                    s.dma_start(out=xsum[lc % 2][:], in_=x1D[lc * 128:(lc + 1) * 128, :]).then_inc(dS.h, 16)
                    dS.n += 16
                    if lc >= 2:
                        s.wait_ge(vC.h, vC0 + lc - 1)
                    s.dma_start(out=fsb[lc % 2][:], in_=fD[lc * 128:(lc + 1) * 128, :]).then_inc(dS.h, 16)
                    dS.n += 16

            @sect("vector")
            def _(v):
                for lc in range(LCH):
                    v.wait_ge(dS.h, dS0 + 32 * (lc + 1))
                    v.tensor_tensor(xsum[lc % 2][:], xsum[lc % 2][:], fsb[lc % 2][:], op=ALU.add)
                    v.drain().then_inc(vC.h, 1); vC.n += 1
                    v.tensor_reduce(s1c, xsum[lc % 2][:], axis=AX.X, op=ALU.add)
                    v.wait_ge(sA.h, sA0 + 2 * lc + 1)
                    v.tensor_reduce(s2c, dump[lc % 2][:], axis=AX.X, op=ALU.add)
                    v.tensor_copy(spcV[:], spcV[:])
                    v.tensor_scalar(mu, s1c, -1.0 / 2048.0, None, op0=ALU.mult)
                    v.tensor_scalar(iv, s2c, 1.0 / 2048.0, EPS_LN, op0=ALU.mult, op1=ALU.add)
                    v.tensor_tensor(nmr, mu, mu, op=ALU.mult)
                    v.tensor_copy(spcV[:], spcV[:])
                    v.tensor_tensor(iv, iv, nmr, op=ALU.subtract)
                    v.tensor_copy(spcV[:], spcV[:])
                    v.reciprocal(iv, iv)
                    v.tensor_copy(spcV[:], spcV[:]).then_inc(vB.h, 1); vB.n += 1
                    v.wait_ge(sA.h, sA0 + 2 * lc + 2)
                    v.tensor_scalar(xsum[lc % 2][:], xsum[lc % 2][:], rstd[:, 0:1], nmr[:, 0:1],
                                    op0=ALU.mult, op1=ALU.add).then_inc(vA.h, 1)
                    vA.n += 1

            @sect("scalar")
            def _(s):
                for lc in range(LCH):
                    s.wait_ge(vC.h, vC0 + lc + 1)
                    s.activation(dump[lc % 2][:], xsum[lc % 2][:], AF.Square)
                    s.copy(spcS[:], spcS[:]).then_inc(sA.h, 1)
                    sA.n += 1
                    s.wait_ge(vB.h, vB0 + lc + 1)
                    s.activation(rstd, iv, AF.Sqrt)
                    s.copy(spcS[:], spcS[:])
                    s.mul(nmr, rstd, mu[:, 0:1])
                    s.copy(spcS[:], spcS[:]).then_inc(sA.h, 1); sA.n += 1
                    s.wait_ge(vA.h, vA0 + lc + 1)
                    s.dma_start(out=y[lc * 128:(lc + 1) * 128, :], in_=xsum[lc % 2][:]).then_inc(dC.h, 16)
                    dC.n += 16
                s.wait_ge(dC.h, dC.n)
            barrier()

        p0()
        p1x()
        proj("q")
        proj("k")
        proj("v")
        p1b1()
        p1b2()
        p2a()
        p2b()
        p2c()
    return nc


def _prep(inputs):
    x = np.asarray(inputs["L0_attn_x"], dtype=np.float32)
    shared = {
        "wqT": np.ascontiguousarray(np.asarray(inputs["L0_attn_wq"], np.float32).T),
        "wkT": np.ascontiguousarray(np.asarray(inputs["L0_attn_wk"], np.float32).T),
        "wvT": np.ascontiguousarray(np.asarray(inputs["L0_attn_wv"], np.float32).T),
        "woT": np.ascontiguousarray(np.asarray(inputs["L0_attn_wo"], np.float32).T),
        "gTb": np.ascontiguousarray(
            np.asarray(inputs["L0_ffn_wq_gate"], np.int32).reshape(NHC, 128, DM).transpose(0, 2, 1)),
        "uTb": np.ascontiguousarray(
            np.asarray(inputs["L0_ffn_wq_up"], np.int32).reshape(NHC, 128, DM).transpose(0, 2, 1)),
        "dT": np.ascontiguousarray(np.asarray(inputs["L0_ffn_wq_down"], np.int32).T),
        "r_bq": np.asarray(inputs["L0_attn_bq"], np.float32).reshape(1, DM).copy(),
        "r_bk": np.asarray(inputs["L0_attn_bk"], np.float32).reshape(1, DM).copy(),
        "r_bv": np.asarray(inputs["L0_attn_bv"], np.float32).reshape(1, DM).copy(),
        "r_bo": np.asarray(inputs["L0_attn_bo"], np.float32).reshape(1, DM).copy(),
        "r_gq": np.asarray(inputs["L0_attn_gq"], np.float32).reshape(1, DM).copy(),
        "r_gk": np.asarray(inputs["L0_attn_gk"], np.float32).reshape(1, DM).copy(),
        "r_sg": np.asarray(inputs["L0_ffn_s_gate"], np.float32).reshape(1, HID).copy(),
        "r_su": np.asarray(inputs["L0_ffn_s_up"], np.float32).reshape(1, HID).copy(),
        "r_sd": np.asarray(inputs["L0_ffn_s_down"], np.float32).reshape(1, DM).copy(),
        "wmix": np.asarray(inputs["L0_attn_w_mix"], np.float32).reshape(1, 5).copy(),
        "cI": np.eye(128, dtype=np.float32).astype(bfd),
        "cMask": _mask01(),
        "cPm": _pmats(),
        "cOnes": np.ones((128, 512), dtype=np.float32).astype(bfd),
        "cOnesF": np.ones((1, 128), dtype=np.float32),
    }
    in_maps = []
    for c in range(NCORE):
        b, half = divmod(c, 2)
        p0_ = half * 2048
        idx = (p0_ - OFF + np.arange(EXT)) % 4096
        m = dict(shared)
        m["xe"] = np.ascontiguousarray(x[b, idx, :])
        in_maps.append(m)
    return in_maps


def kernel(**inputs):
    in_maps = _prep(inputs)
    nc = build_nc()
    res = run_bass_kernel_spmd(nc, in_maps, core_ids=list(range(NCORE)))
    out = np.empty((4, 4096, DM), dtype=np.float32)
    for c in range(NCORE):
        b, half = divmod(c, 2)
        out[b, half * 2048:(half + 1) * 2048, :] = res.results[c]["y"]
    return out
